# version 24
# baseline (speedup 1.0000x reference)
import os
import numpy as np
import ml_dtypes
import concourse.bass as bass
import concourse.mybir as mybir
from concourse.bass_utils import run_bass_kernel_spmd

F32 = mybir.dt.float32
BF16 = mybir.dt.bfloat16
FP8 = mybir.dt.float8e4
AF = mybir.ActivationFunctionType
ALU = mybir.AluOpType
AX = mybir.AxisListType

B, L, D, H, HD, INNER = 4, 4096, 1024, 16, 64, 4096
NCORE = 8
NOWN = 2048
TG = 256
NTG = NOWN // TG
NQB = 4

_EXP = [(-4.0 + h * (3.7 / 15.0)) for h in range(H)]
SLOPES64 = [float(2.0 ** e) for e in _EXP]
KEPT = [min(int(np.ceil(130.0 / s / 128.0)) * 128, L) for s in SLOPES64]
KVT = max(KEPT)
NKVT = KVT // 128
NOT = NOWN // 128
NKB = [k // 128 for k in KEPT]
BOFF = np.concatenate([[0], np.cumsum(NKB)]).astype(int)
TOTKB = int(BOFF[-1])
KEPT_E = [KEPT[2 * c] for c in range(8)]
KOFF = np.concatenate([[0], np.cumsum(KEPT_E)]).astype(int)
KCOLS = int(KOFF[-1])
NHT = [sum(1 for h in range(H) if KEPT[h] > 128 * t) for t in range(NKVT)]
VOFF = np.concatenate([[0], np.cumsum([65 * n for n in NHT])]).astype(int)
VCOLS = int(VOFF[-1])

TINY = float(2.0 ** -26)
MAGIC = float(1.5 * 2 ** 23)
MSCALE = float(1.5 * 2 ** 78)
EPS_SOFT = 1e-5
LN2_64 = float(np.log(np.float64(2.0)))


def _dual(c64):
    c32 = float(np.float32(c64))
    corr = float(np.float32((c64 - c32) / TINY))
    return c32, corr


def _build():
    nc = bass.Bass("TRN2", target_bir_lowering=False)
    x_own = nc.dram_tensor("x_own", [NOWN, D], F32, kind="ExternalInput").ap()
    x_kv = nc.dram_tensor("x_kv", [KVT, D], F32, kind="ExternalInput").ap()
    WqT = nc.dram_tensor("WqT", [D, D], F32, kind="ExternalInput").ap()
    WkT = nc.dram_tensor("WkT", [D, D], F32, kind="ExternalInput").ap()
    WvT = nc.dram_tensor("WvT", [D, D], F32, kind="ExternalInput").ap()
    WoT = nc.dram_tensor("WoT", [D, D], F32, kind="ExternalInput").ap()
    bq_r = nc.dram_tensor("bq_r", [128, 8], F32, kind="ExternalInput").ap()
    bk_r = nc.dram_tensor("bk_r", [128, 8], F32, kind="ExternalInput").ap()
    bv_r = nc.dram_tensor("bv_r", [128, 8], F32, kind="ExternalInput").ap()
    bo_r = nc.dram_tensor("bo_r", [1, D], F32, kind="ExternalInput").ap()
    wgT = nc.dram_tensor("wgT", [D, INNER], F32, kind="ExternalInput").ap()
    wuT = nc.dram_tensor("wuT", [D, INNER], F32, kind="ExternalInput").ap()
    wdT = nc.dram_tensor("wdT", [INNER, D], F32, kind="ExternalInput").ap()
    y = nc.dram_tensor("y", [NOWN, D], F32, kind="ExternalOutput").ap()
    x1scr = nc.dram_tensor("x1scr", [NOWN, D], F32, kind="Internal").ap()
    wg8d = nc.dram_tensor("wg8d", [128, 8 * 4096], FP8, kind="Internal").ap()
    wu8d = nc.dram_tensor("wu8d", [128, 8 * 4096], FP8, kind="Internal").ap()
    wd8d = nc.dram_tensor("wd8d", [128, 32 * 1024], FP8, kind="Internal").ap()

    xo_t = x_own.rearrange("(t p) d -> t p d", p=128)
    xk_t = x_kv.rearrange("(t p) d -> t p d", p=128)
    y_t = y.rearrange("(t p) d -> t p d", p=128)
    x1_t = x1scr.rearrange("(t p) d -> t p d", p=128)
    wgT_c = wgT.rearrange("(c p) n -> p c n", p=128)
    wuT_c = wuT.rearrange("(c p) n -> p c n", p=128)
    wdT_c = wdT.rearrange("(c p) n -> p c n", p=128)
    WT_c = {"q": WqT.rearrange("(c p) n -> p c n", p=128),
            "k": WkT.rearrange("(c p) n -> p c n", p=128),
            "v": WvT.rearrange("(c p) n -> p c n", p=128),
            "o": WoT.rearrange("(c p) n -> p c n", p=128)}

    class Bld:
        def __init__(self):
            self.plan = {"sync": [], "tensor": [], "vector": [], "scalar": [], "gpsimd": []}
            self.cnt = {"PE": 0, "DVE": 0, "ACT": 0, "POOL": 0}
            for i in range(8):
                self.cnt[f"QS{i}"] = 0
            for i in range(2):
                self.cnt[f"QG{i}"] = 0
            self.rr = {"QS": 0, "QG": 0}
            self.semof = {"tensor": "PE", "vector": "DVE", "scalar": "ACT", "gpsimd": "POOL"}
            self.waited = {}
            self.lastw = {}
            self.readers = {}
            self.floor = {}
            self.sems = None
            self.wsince = {}
            self.lasttok = {}

        def _wait(self, eng, sem, val):
            if val <= 0:
                return
            if self.waited.get((eng, sem), 0) >= val:
                return
            self.waited[(eng, sem)] = val
            self.wsince[eng] = True
            def f(e, sem=sem, val=val):
                e.wait_ge(self.sems[sem], val)
            self.plan[eng].append(f)

        def barrier(self):
            for s, v in self.cnt.items():
                self.floor[s] = v

        def dep(self, eng, reads=(), war=()):
            need = dict(self.floor)
            for k in reads:
                t = self.lastw.get(k)
                if t is not None:
                    assert t != "PENDING", f"read of in-flight group buffer {k}"
                    need[t[0]] = max(need.get(t[0], 0), t[1])
            for k in war:
                t = self.lastw.get(k)
                if t is not None and t != "PENDING":
                    need[t[0]] = max(need.get(t[0], 0), t[1])
                for (s, v) in self.readers.get(k, ()):
                    need[s] = max(need.get(s, 0), v)
            for s, v in need.items():
                self._wait(eng, s, v)

        def op(self, eng, fn, reads=(), writes=(), token=True, q=None):
            self.dep(eng, reads, war=writes)
            if q:
                nl = 8 if q == "QS" else 2
                lane = self.rr[q]
                self.rr[q] = (lane + 1) % nl
                sem = f"{q}{lane}"
                self._wait(eng, sem, self.cnt[sem])
            else:
                sem = self.semof[eng]
            if token:
                inc = 16 if q else 1
                if q is None and eng in ("tensor", "scalar"):
                    if not self.wsince.get(eng, True):
                        pv = self.lasttok[eng]
                        if self.waited.get((eng, sem), 0) < pv:
                            self.waited[(eng, sem)] = pv
                            def fw(e, sem=sem, pv=pv):
                                e.wait_ge(self.sems[sem], pv)
                            self.plan[eng].append(fw)
                    self.cnt[sem] += inc
                    tok = (sem, self.cnt[sem])
                    self.wsince[eng] = False
                    self.lasttok[eng] = self.cnt[sem]
                    def f(e, fn=fn, sem=sem, inc=inc):
                        fn(e).then_inc(self.sems[sem], inc)
                elif q is None:
                    self.cnt[sem] += inc
                    tok = (sem, self.cnt[sem])
                    val = self.cnt[sem]
                    def f(e, fn=fn, sem=sem, inc=inc, val=val):
                        fn(e).then_inc(self.sems[sem], inc)
                        e.wait_ge(self.sems[sem], val)
                    self.waited[(eng, sem)] = val
                    self.wsince[eng] = True
                else:
                    self.cnt[sem] += inc
                    tok = (sem, self.cnt[sem])
                    def f(e, fn=fn, sem=sem, inc=inc):
                        fn(e).then_inc(self.sems[sem], inc)
            else:
                tok = (sem, self.cnt[sem] + 1)
                def f(e, fn=fn):
                    fn(e)
            self.plan[eng].append(f)
            for k in reads:
                self.readers.setdefault(k, []).append(tok)
            for k in writes:
                self.lastw[k] = tok if token else "PENDING"
                self.readers[k] = []
            return tok

        def settok(self, keys, tok):
            for k in keys:
                self.lastw[k] = tok

    bld = Bld()
    alive = []

    def sb(name, shape, dtype):
        cm = nc.sbuf_tensor(name, shape, dtype)
        h = cm.__enter__()
        alive.append((cm, h))
        return h

    def sb_free_back_to(h):
        idx = next(i for i, (cm, hh) in enumerate(alive) if hh is h)
        while len(alive) > idx:
            cm, hh = alive.pop()
            cm.__exit__(None, None, None)

    ident16 = sb("ident16", [128, 128], BF16)
    ident32 = sb("ident32", [128, 128], F32)
    kidx = sb("kidx", [128, max(NKB)], F32)
    bias_tbl = sb("bias_tbl", [128, TOTKB], F32)
    onescol = sb("onescol", [128, 1], F32)
    onesrow = sb("onesrow", [1, 128], F32)
    onesrow16 = sb("onesrow16", [1, 128], BF16)
    fcell = sb("fcell", [128, 1], F32)
    epscol = sb("epscol", [128, 1], F32)
    mcol = sb("mcol", [128, 1], F32)
    negm = sb("negm", [128, 1], F32)
    m2col = sb("m2col", [128, 1], F32)
    soft_lo = sb("soft_lo", [128, 1], F32)
    soft_hi = sb("soft_hi", [128, 1], F32)
    lneps = sb("lneps", [128, 1], F32)
    bq_sb = sb("bq_sb", [128, 8], F32)
    bk_sb = sb("bk_sb", [128, 8], F32)
    bv_sb = sb("bv_sb", [128, 8], F32)
    bo16 = sb("bo16", [1, D], BF16)
    scal = sb("scal", [128, 20], F32)
    LN1C = sb("LN1C", [128, 2 * (NKVT + NOT)], F32)
    LN2C = sb("LN2C", [128, 2 * NOT], F32)
    slope_c = sb("slope_c", [128, 1], F32)
    SC_SG, SC_SWG, SC_SU, SC_SWU, SC_SD, SC_SWD = 0, 1, 2, 3, 4, 5
    SC_T0, SC_T1 = 6, 7

    psum_cm = []
    def psb(name, shape, dtype):
        cm = nc.psum_tensor(name, shape, dtype)
        h = cm.__enter__()
        psum_cm.append(cm)
        return h
    psW0 = psb("psW0", [128, 1024], F32)
    psW1 = psb("psW1", [128, 1024], F32)
    ps4 = psb("ps4", [128, 512], F32)
    ps5 = psb("ps5", [128, 512], F32)
    psE = psb("psE", [128, 1024], F32)
    psEa = psE[:, 0:512]
    psEb = psE[:, 512:1024]

    bld.op("gpsimd", lambda e: e.memset(ident16[:], 0.0), writes=("ident16",))
    bld.op("gpsimd", lambda e: e.affine_select(out=ident16[:], in_=ident16[:],
           compare_op=ALU.not_equal, fill=1.0, base=0, pattern=[[-1, 128]],
           channel_multiplier=1), reads=("ident16",), writes=("ident16",))
    bld.op("gpsimd", lambda e: e.nop(), reads=("ident16",))
    bld.op("gpsimd", lambda e: e.iota(kidx[:], [[128, max(NKB)]], channel_multiplier=1,
           allow_small_or_imprecise_dtypes=True), writes=("kidx",))
    bld.op("vector", lambda e: e.memset(onescol[:], 1.0), writes=("onescol",))
    bld.op("vector", lambda e: e.memset(lneps[:], 1e-5), writes=("lneps",))
    bld.op("vector", lambda e: e.memset(onesrow[:], 1.0), writes=("onesrow",))
    bld.op("vector", lambda e: e.memset(onesrow16[:], 1.0), writes=("onesrow16",))
    bld.op("vector", lambda e: e.tensor_scalar(fcell[:], onescol[:], TINY, 1.0,
           ALU.add, ALU.subtract), reads=("onescol",), writes=("fcell",))
    bld.op("vector", lambda e: e.tensor_scalar(epscol[:], fcell[:], -1.0, TINY,
           ALU.mult, ALU.add), reads=("fcell",), writes=("epscol",))
    bld.op("vector", lambda e: e.tensor_scalar(mcol[:], fcell[:], MSCALE, MAGIC,
           ALU.mult, ALU.add), reads=("fcell",), writes=("mcol",))
    bld.op("vector", lambda e: e.tensor_scalar(negm[:], mcol[:], -1.0, None,
           ALU.mult), reads=("mcol",), writes=("negm",))
    bld.op("vector", lambda e: e.tensor_scalar(m2col[:], mcol[:], 2.0, None,
           ALU.mult), reads=("mcol",), writes=("m2col",))
    _esc = float(EPS_SOFT / TINY)
    bld.op("vector", lambda e: e.tensor_scalar(soft_lo[:], epscol[:], -_esc, 1.0,
           ALU.mult, ALU.add), reads=("epscol",), writes=("soft_lo",))
    bld.op("vector", lambda e: e.tensor_scalar(soft_hi[:], epscol[:], _esc, 1.0,
           ALU.mult, ALU.add), reads=("epscol",), writes=("soft_hi",))
    bld.op("vector", lambda e: e.tensor_copy(ident32[:], ident16[:]),
           reads=("ident16",), writes=("ident32",))
    bld.op("sync", lambda e: e.dma_start(out=bq_sb[:], in_=bq_r), writes=("bq_sb",), q="QS")
    bld.op("sync", lambda e: e.dma_start(out=bk_sb[:], in_=bk_r), writes=("bk_sb",), q="QS")
    bld.op("sync", lambda e: e.dma_start(out=bv_sb[:], in_=bv_r), writes=("bv_sb",), q="QS")


    for h in range(H):
        c32, corr = _dual(_EXP[h] * LN2_64)
        bld.op("vector", lambda e, c32=c32, corr=corr: e.tensor_scalar(
            scal[0:1, SC_T0:SC_T0 + 1], fcell[0:1, :], corr, c32, ALU.mult, ALU.add),
            reads=("fcell",), writes=("sc_t0",))
        bld.op("scalar", lambda e: e.activation(scal[0:1, SC_T1:SC_T1 + 1],
            scal[0:1, SC_T0:SC_T0 + 1], AF.Exp), reads=("sc_t0",), writes=("sc_t1",))
        bld.op("vector", lambda e: e.tensor_scalar(scal[0:1, SC_T0:SC_T0 + 1],
            scal[0:1, SC_T1:SC_T1 + 1], -1.0, None, ALU.mult),
            reads=("sc_t1",), writes=("sc_t0",))
        bld.op("tensor", lambda e: e.matmul(psEa[:, 0:1], onesrow[:],
            scal[0:1, SC_T0:SC_T0 + 1], start=True, stop=True),
            reads=("onesrow", "sc_t0"), writes=("psEa",))
        bld.op("vector", lambda e: e.tensor_copy(slope_c[:], psEa[:, 0:1]),
            reads=("psEa",), writes=("slope_c",))
        nkb, b0 = NKB[h], int(BOFF[h])
        bld.op("vector", lambda e, nkb=nkb, b0=b0: e.tensor_scalar(
            bias_tbl[:, b0:b0 + nkb], kidx[:, 0:nkb], slope_c[:], None, ALU.mult),
            reads=("kidx", "slope_c"), writes=(f"bias:{h}",))

    KPH = int(os.environ.get("KPH", "4"))

    class _Stop(Exception):
        pass
    try:
        if KPH < 1:
            raise _Stop()
        XLT = sb("XLT", [128, 8 * KVT], BF16)
        bo_f = sb("bo_f", [1, D], F32)
        bld.op("sync", lambda e: e.dma_start(out=bo_f[:], in_=bo_r), writes=("bo_f",), q="QS")
        bld.op("vector", lambda e: e.tensor_copy(bo16[:], bo_f[:]), reads=("bo_f",), writes=("bo16",))
        xlt_kv = XLT[:].rearrange("p (c n) -> p c n", c=8)
        KT = sb("KT", [128, KCOLS], BF16)
        VT = sb("VT", [128, VCOLS], BF16)
        Wk16 = sb("Wk16", [128, 8 * D], BF16)
        Wv16 = sb("Wv16", [128, 8 * D], BF16)
        stg = [sb(f"stg{i}", [128, 2048], F32) for i in range(2)]
        xin = [sb(f"xin{i}", [128, D], F32) for i in range(4)]
        sq = sb("sq", [128, D], F32)
        cols = sb("cols", [128, 12], F32)
        mean_acc = sb("mean_acc", [128, 64], F32)

        def ln_stats(src_key, src_ap, slot, colarr, ckey):
            p6 = (slot % 2) * 6
            cs = lambda i: cols[:, p6 + i:p6 + i + 1]
            kc = lambda i: f"c{slot % 2}_{i}"
            bld.op("vector", lambda e, cs=cs: e.reduce_sum(cs(0), src_ap, axis=AX.X),
                   reads=(src_key,), writes=(kc(0),))
            bld.op("scalar", lambda e, cs=cs: e.activation(sq[:], src_ap, AF.Square,
                   accum_out=cs(1)), reads=(src_key,), writes=("sq", kc(1)))
            bld.op("vector", lambda e, cs=cs: e.tensor_scalar(cs(2), cs(0),
                   float(1.0 / D), None, ALU.mult), reads=(kc(0),), writes=(kc(2),))
            bld.op("vector", lambda e, cs=cs: e.tensor_mul(cs(3), cs(2), cs(2)),
                   reads=(kc(2),), writes=(kc(3),))
            bld.op("vector", lambda e, cs=cs: e.scalar_tensor_tensor(cs(4), cs(1),
                   float(1.0 / D), cs(3), ALU.mult, ALU.subtract),
                   reads=(kc(1), kc(3)), writes=(kc(4),))
            bld.op("scalar", lambda e, cs=cs: e.activation(cs(5), cs(4), AF.Ln,
                   bias=lneps[:]), reads=(kc(4), "lneps"), writes=(kc(5),))
            bld.op("scalar", lambda e, cs=cs, colarr=colarr, slot=slot: e.activation(
                   colarr[:, 2 * slot:2 * slot + 1], cs(5), AF.Exp, scale=-0.5),
                   reads=(kc(5),), writes=(ckey + ":r",))
            bld.op("vector", lambda e, cs=cs, colarr=colarr, slot=slot:
                   e.scalar_tensor_tensor(colarr[:, 2 * slot + 1:2 * slot + 2], cs(2),
                   -1.0, colarr[:, 2 * slot:2 * slot + 1], ALU.mult, ALU.mult),
                   reads=(kc(2), ckey + ":r"), writes=(ckey + ":m",))

        NT_ALL = NKVT + NOT
        astat = sb("astat", [128, 4 * NT_ALL], F32)

        def ln_sum2(src_key, src_ap, slot):
            bld.op("vector", lambda e, src_ap=src_ap, slot=slot: e.reduce_sum(
                   astat[:, slot:slot + 1], src_ap, axis=AX.X),
                   reads=(src_key,), writes=(f"as:{slot}",))
            bld.op("scalar", lambda e, src_ap=src_ap, slot=slot: e.activation(
                   sq[:], src_ap, AF.Square,
                   accum_out=astat[:, NT_ALL + slot:NT_ALL + slot + 1]),
                   reads=(src_key,), writes=("sq", f"aq:{slot}"))

        def ln_batch(b, n):
            A2, A3c = 2 * NT_ALL, 3 * NT_ALL
            rs = tuple(f"as:{s}" for s in range(b, b + n))
            rq = tuple(f"aq:{s}" for s in range(b, b + n))
            bld.op("vector", lambda e, b=b, n=n: e.tensor_scalar(
                   astat[:, A2 + b:A2 + b + n], astat[:, b:b + n],
                   float(1.0 / D), None, ALU.mult),
                   reads=rs, writes=(f"am:{b}",))
            bld.op("vector", lambda e, b=b, n=n: e.tensor_mul(
                   astat[:, A3c + b:A3c + b + n], astat[:, A2 + b:A2 + b + n],
                   astat[:, A2 + b:A2 + b + n]),
                   reads=(f"am:{b}",), writes=(f"am2:{b}",))
            bld.op("vector", lambda e, b=b, n=n: e.scalar_tensor_tensor(
                   astat[:, b:b + n], astat[:, NT_ALL + b:NT_ALL + b + n],
                   float(1.0 / D), astat[:, A3c + b:A3c + b + n],
                   ALU.mult, ALU.subtract),
                   reads=rq + (f"am2:{b}",), writes=(f"av:{b}",))
            bld.op("scalar", lambda e, b=b, n=n: e.activation(
                   astat[:, NT_ALL + b:NT_ALL + b + n], astat[:, b:b + n],
                   AF.Ln, bias=lneps[:]),
                   reads=(f"av:{b}", "lneps"), writes=(f"al:{b}",))
            bld.op("scalar", lambda e, b=b, n=n: e.activation(
                   LN1C[:, b:b + n], astat[:, NT_ALL + b:NT_ALL + b + n],
                   AF.Exp, scale=-0.5),
                   reads=(f"al:{b}",), writes=(f"ar:{b}",))
            bld.op("vector", lambda e, b=b, n=n: e.scalar_tensor_tensor(
                   LN1C[:, NT_ALL + b:NT_ALL + b + n], astat[:, A2 + b:A2 + b + n],
                   -1.0, LN1C[:, b:b + n], ALU.mult, ALU.mult),
                   reads=(f"am:{b}", f"ar:{b}"), writes=(f"an:{b}",))

        def ln_apply_t(src_key, src_ap, slot, bkey, dst3, dstkey, tslot):
            lb = xin[(tslot + 2) % 4]
            lbk = f"xin{(tslot + 2) % 4}"
            bld.op("vector", lambda e, lb=lb, slot=slot: e.tensor_scalar(
                   lb[:], src_ap, LN1C[:, slot:slot + 1],
                   LN1C[:, NT_ALL + slot:NT_ALL + slot + 1], ALU.mult, ALU.add),
                   reads=(src_key, f"ar:{bkey}", f"an:{bkey}"), writes=(lbk,))
            for j in range(8):
                tok = bld.op("tensor", lambda e, j=j, lb=lb: e.transpose(
                    out=psE[:, j * 128:(j + 1) * 128], in_=lb[:, j * 128:(j + 1) * 128],
                    identity=ident32[:]),
                    reads=(lbk, "ident32") + (("psEa", "psEb") if j == 0 else ()),
                    writes=("psEa", "psEb") if j == 0 else (), token=(j == 7))
            bld.settok(["psEa", "psEb"], tok)
            bld.op("scalar", lambda e, dst3=dst3, tslot=tslot: e.activation(
                   dst3[:, :, tslot * 128:(tslot + 1) * 128],
                   psE[:].rearrange("p (c n) -> p c n", c=8), AF.Identity),
                   reads=("psEa", "psEb"), writes=(dstkey,))

        mean_jobs = []
        for ci in range(16):
            c, hcol = ci // 2, ci % 2
            mean_jobs.append((wgT_c[:, c, hcol * 2048:(hcol + 1) * 2048], ci))
        for ci in range(16):
            c, hcol = ci // 2, ci % 2
            mean_jobs.append((wuT_c[:, c, hcol * 2048:(hcol + 1) * 2048], 16 + ci))
        for ci in range(32):
            mean_jobs.append((wdT_c[:, ci, :], 32 + ci))
        mj_i = [0]

        def emit_mean_step():
            if mj_i[0] >= len(mean_jobs):
                return
            src, acol = mean_jobs[mj_i[0]]
            sgi = mj_i[0] % 2
            mj_i[0] += 1
            n = src.shape[-1]
            bld.op("sync", lambda e, src=src, sgi=sgi, n=n: e.dma_start(
                   out=stg[sgi][:, 0:n], in_=src), writes=(f"stg{sgi}",), q="QS")
            if n > 1024:
                bld.op("vector", lambda e, sgi=sgi, acol=acol, n=n: e.reduce_sum(
                       mean_acc[:, acol:acol + 1], stg[sgi][:, 0:n], axis=AX.X,
                       apply_absolute_value=True),
                       reads=(f"stg{sgi}",), writes=(f"macc:{acol}",))
            else:
                bld.op("scalar", lambda e, sgi=sgi, acol=acol, n=n: e.activation(
                       sq[:, 0:n], stg[sgi][:, 0:n], AF.Abs,
                       accum_out=mean_acc[:, acol:acol + 1]),
                       reads=(f"stg{sgi}",), writes=("sq", f"macc:{acol}"))

        def a_dma(t):
            xi = xin[t % 4]
            bld.op("sync", lambda e, t=t, xi=xi: e.dma_start(out=xi[:], in_=xk_t[t]),
                   writes=(f"xin{t % 4}",), q="QS")
        a_dma(0)
        for t in range(NKVT):
            emit_mean_step()
            if t + 1 < NKVT:
                a_dma(t + 1)
            ln_sum2(f"xin{t % 4}", xin[t % 4][:], t)
        ln_batch(0, NKVT)
        a_dma(0)
        for t in range(NKVT):
            emit_mean_step()
            if t + 1 < NKVT:
                a_dma(t + 1)
            ln_apply_t(f"xin{t % 4}", xin[t % 4][:], t, 0,
                       xlt_kv, f"xlt:{t}", t)

        if KPH < 2:
            raise _Stop()
        bld.op("gpsimd", lambda e: e.dma_start(
               out=Wk16[:].rearrange("p (c n) -> p c n", c=8), in_=WT_c["k"]),
               writes=("Wk16",), q="QG")
        bld.op("gpsimd", lambda e: e.dma_start(
               out=Wv16[:].rearrange("p (c n) -> p c n", c=8), in_=WT_c["v"]),
               writes=("Wv16",), q="QG")
        wk3 = Wk16[:].rearrange("p (c n) -> p c n", c=8)
        wv3 = Wv16[:].rearrange("p (c n) -> p c n", c=8)

        def nblocks(n):
            out, o = [], 0
            while o < n:
                w = min(512, n - o)
                out.append((o, w))
                o += w
            return out

        pg = [ps4, ps5]
        pgk = ["ps4", "ps5"]
        pi = [0]
        for c in range(8):
            for (o, w) in nblocks(KEPT_E[c]):
                emit_mean_step()
                bank = pi[0] % 2
                pi[0] += 1
                pk = pgk[bank]
                reads_xl = tuple(f"xlt:{ts}" for ts in range(o // 128, (o + w + 127) // 128))
                for dc in range(8):
                    tok = bld.op("tensor", lambda e, c=c, dc=dc, o=o, w=w, bank=bank:
                        e.matmul(pg[bank][:, 0:w], wk3[:, dc, c * 128:(c + 1) * 128],
                                 xlt_kv[:, dc, o:o + w], start=(dc == 0), stop=(dc == 7)),
                        reads=("Wk16",) + reads_xl + ((pk,) if dc == 0 else ()),
                        writes=(pk,) if dc == 0 else (), token=(dc == 7))
                bld.settok([pk], tok)
                bld.op("scalar", lambda e, c=c, o=o, w=w, bank=bank: e.activation(
                       KT[:, KOFF[c] + o:KOFF[c] + o + w], pg[bank][:, 0:w],
                       AF.Identity, bias=bk_sb[:, c:c + 1]),
                       reads=(pk, "bk_sb"), writes=(f"KT:{c}",))
        for t in range(NKVT):
            nh = NHT[t]
            if nh == 0:
                continue
            vt3 = VT[:, VOFF[t]:VOFF[t] + 65 * nh].rearrange("p (h x) -> p h x", x=65)
            bld.op("gpsimd", lambda e, vt3=vt3: e.memset(vt3[:, :, 64:65], 1.0),
                   writes=(f"vones:{t}",))
            for hb0 in range(0, nh, 8):
                hbn = min(8, nh - hb0)
                emit_mean_step()
                bank = pi[0] % 2
                pi[0] += 1
                pk = pgk[bank]
                for dc in range(8):
                    tok = bld.op("tensor", lambda e, t=t, dc=dc, hb0=hb0, hbn=hbn, bank=bank:
                        e.matmul(pg[bank][:, 0:64 * hbn], xlt_kv[:, dc, t * 128:(t + 1) * 128],
                                 wv3[:, dc, hb0 * 64:(hb0 + hbn) * 64],
                                 start=(dc == 0), stop=(dc == 7)),
                        reads=(f"xlt:{t}", "Wv16") + ((pk,) if dc == 0 else ()),
                        writes=(pk,) if dc == 0 else (), token=(dc == 7))
                bld.settok([pk], tok)
                bld.op("scalar", lambda e, t=t, vt3=vt3, hb0=hb0, hbn=hbn, bank=bank:
                       e.activation(vt3[:, hb0:hb0 + hbn, 0:64], pg[bank][:, 0:64 * hbn],
                                    AF.Identity),
                       reads=(pk,), writes=(f"VT:{t}",))

        def mean_final(col_lo, col_n, sc_s, sc_sw):
            kacc = tuple(f"macc:{a}" for a in range(col_lo, col_lo + col_n))
            bld.op("vector", lambda e: e.reduce_sum(cols[:, 9:10],
                   mean_acc[:, col_lo:col_lo + col_n], axis=AX.X),
                   reads=kacc, writes=("c_w",))
            bld.op("tensor", lambda e: e.matmul(psEa[0:1, 0:1], cols[:, 9:10],
                   onescol[:], start=True, stop=True),
                   reads=("c_w", "onescol"), writes=("psEa",))
            bld.op("vector", lambda e: e.tensor_scalar(cols[0:1, 10:11], psEa[0:1, 0:1],
                   float(2.0 ** -22), 1e-5, ALU.mult, ALU.max),
                   reads=("psEa",), writes=("c_mean1",))
            bld.op("vector", lambda e: e.reciprocal(cols[0:1, 11:12], cols[0:1, 10:11]),
                   reads=("c_mean1",), writes=("c_s1",))
            bld.op("tensor", lambda e: e.matmul(psEa[:, 0:2], onesrow[:],
                   cols[0:1, 10:12], start=True, stop=True),
                   reads=("c_mean1", "c_s1", "onesrow"), writes=("psEa",))
            bld.op("vector", lambda e, sc_sw=sc_sw: e.tensor_copy(
                   scal[:, sc_sw:sc_sw + 1], psEa[:, 0:1]),
                   reads=("psEa",), writes=(f"sc:{sc_sw}",))
            bld.op("vector", lambda e, sc_s=sc_s: e.tensor_copy(
                   scal[:, sc_s:sc_s + 1], psEa[:, 1:2]),
                   reads=("psEa",), writes=(f"sc:{sc_s}",))

        def a3_dma(t):
            xi = xin[t % 4]
            bld.op("sync", lambda e, t=t, xi=xi: e.dma_start(out=xi[:], in_=xo_t[t]),
                   writes=(f"xin{t % 4}",), q="QS")
        a3_dma(0)
        for t in range(NOT):
            if t + 1 < NOT:
                a3_dma(t + 1)
            ln_sum2(f"xin{t % 4}", xin[t % 4][:], NKVT + t)
        ln_batch(NKVT, NOT)
        while mj_i[0] < len(mean_jobs):
            emit_mean_step()
        mean_final(0, 16, SC_SG, SC_SWG)
        mean_final(16, 16, SC_SU, SC_SWU)
        mean_final(32, 32, SC_SD, SC_SWD)
        bld.op("vector", lambda e: e.tensor_scalar(scal[:, SC_SWD:SC_SWD + 1],
               scal[:, SC_SWD:SC_SWD + 1], 0.5, None, ALU.mult),
               reads=(f"sc:{SC_SWD}",), writes=(f"sc:{SC_SWD}",))

        if KPH < 3:
            raise _Stop()
        bld.barrier()
        xlt_own = XLT[:].rearrange("p (c n) -> p c n", c=8)
        a3_dma(0)
        for t in range(NOT):
            if t + 1 < NOT:
                a3_dma(t + 1)
            ln_apply_t(f"xin{t % 4}", xin[t % 4][:], NKVT + t, NKVT,
                       xlt_own, f"xlo:{t}", t)

        bld.op("gpsimd", lambda e: e.dma_start(
               out=Wk16[:].rearrange("p (c n) -> p c n", c=8), in_=WT_c["q"]),
               writes=("Wk16",), q="QG")
        bld.op("gpsimd", lambda e: e.dma_start(
               out=Wv16[:].rearrange("p (c n) -> p c n", c=8), in_=WT_c["o"]),
               writes=("Wv16",), q="QG")
        wq3, wo3 = wk3, wv3

        QTb = sb("QTb", [128, 8 * 1024], BF16)
        OT = sb("OT", [128, 8 * 1024], BF16)
        Pring = [sb(f"P{i}", [128, 1024], BF16) for i in range(3)]
        stg8 = [sb(f"stg8_{i}", [128, 2048], FP8) for i in range(2)]
        x1b = [sb(f"x1b{i}", [128, D], F32) for i in range(2)]
        park3 = [sb(f"pkp{i}", [128, 4, 256], F32) for i in range(2)]
        dstack = sb("dstack", [128, 1536], F32)
        den_pair = [dstack[0:4, 0:512], dstack[32:36, 0:512]]
        rec_pair = [dstack[0:4, 512:1024], dstack[0:4, 1024:1536]]
        denrow = [dstack[64:65, 0:512], dstack[96:97, 0:512]]
        selT = sb("selT", [128, 8], F32)
        sel_h = [sb(f"sel{i}", [4, 128], F32) for i in range(2)]
        bld.op("gpsimd", lambda e: e.memset(selT[:], 0.0), writes=("selT",))
        for half in range(2):
            bld.op("gpsimd", lambda e, half=half: e.memset(
                   selT[0:64, 4 * half + half:4 * half + half + 1], 1.0),
                   reads=("selT",), writes=("selT",))
            bld.op("gpsimd", lambda e, half=half: e.memset(
                   selT[64:128, 4 * half + 2 + half:4 * half + 3 + half], 1.0),
                   reads=("selT",), writes=("selT",))
        for half in range(2):
            bld.op("tensor", lambda e, half=half: e.transpose(
                   out=psEa[0:4, 0:128], in_=selT[:, 4 * half:4 * half + 4],
                   identity=ident32[:]), reads=("selT", "ident32"), writes=("psEa",))
            bld.op("vector", lambda e, half=half: e.tensor_copy(
                   sel_h[half][:], psEa[0:4, 0:128]),
                   reads=("psEa",), writes=(f"sel{half}",))

        tern_jobs = []
        for ci in range(16):
            c, hcol = ci // 2, ci % 2
            tern_jobs.append(("g", wgT_c[:, c, hcol * 2048:(hcol + 1) * 2048],
                              wg8d[:, ci * 2048:(ci + 1) * 2048], SC_SG))
        for ci in range(16):
            c, hcol = ci // 2, ci % 2
            tern_jobs.append(("u", wuT_c[:, c, hcol * 2048:(hcol + 1) * 2048],
                              wu8d[:, ci * 2048:(ci + 1) * 2048], SC_SU))
        for ci in range(32):
            tern_jobs.append(("d", wdT_c[:, ci, :], wd8d[:, ci * 1024:(ci + 1) * 1024], SC_SD))
        tj_i = [0]
        tl_i = [0]

        def emit_tern_load():
            if tl_i[0] >= len(tern_jobs):
                return
            _, src, _, _ = tern_jobs[tl_i[0]]
            sgi = tl_i[0] % 2
            tl_i[0] += 1
            n = src.shape[-1]
            sk = f"stg{sgi}"
            h2 = n // 2
            bld.op("sync", lambda e, src=src, sgi=sgi, h2=h2: e.dma_start(
                   out=stg[sgi][:, 0:h2], in_=src[:, 0:h2]),
                   writes=(sk,), q="QS")
            bld.op("sync", lambda e, src=src, sgi=sgi, h2=h2, n=n: e.dma_start(
                   out=stg[sgi][:, h2:n], in_=src[:, h2:n]),
                   writes=(sk + "h",), q="QS")

        def emit_tern_step():
            if tj_i[0] >= len(tern_jobs):
                return
            while tl_i[0] <= min(tj_i[0] + 1, len(tern_jobs) - 1):
                emit_tern_load()
            tag, src, dst, sc_s = tern_jobs[tj_i[0]]
            sgi = tj_i[0] % 2
            tj_i[0] += 1
            n = src.shape[-1]
            sk, s8k = f"stg{sgi}", f"stg8_{sgi}"
            s_ap = scal[:, sc_s:sc_s + 1]
            if tag != "d":
                bld.op("vector", lambda e, sgi=sgi, n=n, s_ap=s_ap: e.tensor_scalar(
                       stg[sgi][:, 0:n], stg[sgi][:, 0:n], s_ap, 1.49, ALU.mult, ALU.min),
                       reads=(sk, sk + "h", f"sc:{sc_s}"), writes=(sk,))
                bld.op("vector", lambda e, sgi=sgi, n=n: e.tensor_scalar(
                       stg[sgi][:, 0:n], stg[sgi][:, 0:n], -1.49, mcol[:], ALU.max, ALU.add),
                       reads=(sk, sk + "h", "mcol"), writes=(sk,))
                bld.op("scalar", lambda e, sgi=sgi, n=n: e.activation(
                       stg8[sgi][:, 0:n], stg[sgi][:, 0:n], AF.Identity, bias=negm[:]),
                       reads=(sk, sk + "h", "negm"), writes=(s8k,))
            else:
                bld.op("vector", lambda e, sgi=sgi, n=n, s_ap=s_ap: e.tensor_scalar(
                       stg[sgi][:, 0:n], stg[sgi][:, 0:n], s_ap, 1.49, ALU.mult, ALU.min),
                       reads=(sk, sk + "h", f"sc:{sc_s}"), writes=(sk,))
                bld.op("vector", lambda e, sgi=sgi, n=n: e.tensor_scalar(
                       stg[sgi][:, n:2 * n], stg[sgi][:, 0:n], -1.49, soft_lo[:],
                       ALU.max, ALU.mult), reads=(sk, sk + "h", "soft_lo"), writes=(sk,))
                bld.op("vector", lambda e, sgi=sgi, n=n: e.tensor_scalar(
                       stg[sgi][:, 0:n], stg[sgi][:, 0:n], -1.49, soft_hi[:],
                       ALU.max, ALU.mult), reads=(sk, sk + "h", "soft_hi"), writes=(sk,))
                bld.op("vector", lambda e, sgi=sgi, n=n: e.tensor_scalar(
                       stg[sgi][:, n:2 * n], stg[sgi][:, n:2 * n], mcol[:], None,
                       ALU.add), reads=(sk, sk + "h", "mcol"), writes=(sk,))
                bld.op("vector", lambda e, sgi=sgi, n=n: e.tensor_scalar(
                       stg[sgi][:, 0:n], stg[sgi][:, 0:n], mcol[:], None,
                       ALU.add), reads=(sk, sk + "h", "mcol"), writes=(sk,))
                bld.op("vector", lambda e, sgi=sgi, n=n: e.scalar_tensor_tensor(
                       stg8[sgi][:, 0:n], stg[sgi][:, n:2 * n], m2col[:],
                       stg[sgi][:, 0:n], ALU.subtract, ALU.add),
                       reads=(sk, sk + "h", "m2col"), writes=(s8k,))
            bld.op("sync", lambda e, dst=dst, sgi=sgi, n=n: e.dma_start(
                   out=dst, in_=stg8[sgi][:, 0:n]), reads=(s8k,),
                   writes=(f"w8d:{tag}",), q="QS")

        def o_ap(h, half):
            if h % 2 == 0:
                return (pg[half], pgk[half], pg[half][:], 0)
            return (psW1, "psW1a" if half == 0 else "psW1b",
                    psW1[:, half * 512:(half + 1) * 512], half * 512)

        def emit_evac(qp, h):
            c, off = h // 2, 64 * (h % 2)
            pr = c % 2
            pp, ppk = park3[pr], f"pp{pr}"
            for half in range(2):
                _, opk, oview, _ = o_ap(h, half)
                bld.op("vector", lambda e, oview=oview, pp=pp, off=off, half=half:
                       e.tensor_copy(pp[off:off + 64, half * 2:half * 2 + 2, :],
                                     oview[0:64, :].rearrange("p (a n) -> p a n", a=2)),
                       reads=(opk,), writes=(ppk,))
                dr = half
                bld.op("vector", lambda e, oview=oview, dr=dr: e.tensor_copy(
                       denrow[dr], oview[64:65, :]),
                       reads=(opk,), writes=(f"dnr{dr}",))
                r = 2 * (h % 2) + half
                bld.op("sync", lambda e, dr=dr, pr=pr, r=r: e.dma_start(
                       out=den_pair[pr][r:r + 1, :], in_=denrow[dr]),
                       reads=(f"dnr{dr}",), writes=(f"dp{pr}:{r}",), q="QS")

        def emit_recip(c):
            pr = c % 2
            bld.op("vector", lambda e, pr=pr: e.reciprocal(rec_pair[pr],
                   den_pair[pr]),
                   reads=tuple(f"dp{pr}:{r}" for r in range(4)), writes=(f"rp{pr}",))

        def emit_div(c, half, hnow):
            pr = c % 2
            pp, ppk = park3[pr], f"pp{pr}"
            if hnow % 2 == 0:
                rbv, rbk = psW1[:, 0:512], "psW1a"
            else:
                rbv, rbk = ps4[:], "ps4"
            bld.op("tensor", lambda e, pr=pr, half=half, rbv=rbv: e.matmul(
                   rbv, sel_h[half][:], rec_pair[pr], start=True, stop=True),
                   reads=(f"sel{half}", f"rp{pr}", rbk), writes=(rbk,))
            rb3 = rbv.rearrange("p (a n) -> p a n", a=2)
            bld.op("vector", lambda e, pp=pp, half=half, rb3=rb3: e.tensor_mul(
                   pp[:, half * 2:half * 2 + 2, :],
                   pp[:, half * 2:half * 2 + 2, :], rb3),
                   reads=(ppk, rbk), writes=(ppk,))
            bld.op("vector", lambda e, pp=pp, c=c, half=half: e.tensor_scalar(
                   OT[:, c * 1024 + half * 512:c * 1024 + (half + 1) * 512]
                       .rearrange("p (a n) -> p a n", a=2),
                   pp[:, half * 2:half * 2 + 2, :], bv_sb[:, c:c + 1], None, ALU.add),
                   reads=(ppk, "bv_sb"), writes=(f"OT:{c}",))

        for qp in range(2):
            for c in range(8):
                pw = psW0 if c % 2 == 0 else psW1
                pwk = ("psW0a", "psW0b") if c % 2 == 0 else ("psW1a", "psW1b")
                for half in range(2):
                    for dc in range(8):
                        tok = bld.op("tensor", lambda e, c=c, dc=dc, qp=qp, half=half, pw=pw:
                            e.matmul(pw[:, half * 512:(half + 1) * 512],
                                     wq3[:, dc, c * 128:(c + 1) * 128],
                                     xlt_own[:, dc, qp * 1024 + half * 512:qp * 1024 + (half + 1) * 512],
                                     start=(dc == 0), stop=(dc == 7)),
                            reads=("Wk16",) + tuple(f"xlo:{ts}" for ts in
                                   range(qp * 8 + half * 4, qp * 8 + half * 4 + 4)) +
                                   ((pwk[half],) if dc == 0 else ()),
                            writes=(pwk[half],) if dc == 0 else (), token=(dc == 7))
                    bld.settok([pwk[half]], tok)
                bld.op("scalar", lambda e, c=c, pw=pw: e.activation(
                       QTb[:, c * 1024:(c + 1) * 1024], pw[:], AF.Identity,
                       bias=bq_sb[:, c:c + 1]),
                       reads=(pwk[0], pwk[1], "bq_sb"), writes=(f"QTb:{c}",))
                emit_tern_step()
            for h in range(H):
                emit_tern_step()
                emit_tern_step()
                c, off = h // 2, 64 * (h % 2)
                nkb = NKB[h]

                def emit_S(kb, c=c, off=off, h=h):
                    if kb % 2 == 0:
                        sbv, sk0, sk1 = psW0, "psW0a", "psW0b"
                    else:
                        sbv, sk0, sk1 = psE, "psEa", "psEb"
                    for half in range(2):
                        tok = bld.op("tensor", lambda e, c=c, off=off, kb=kb, half=half, sbv=sbv:
                            e.matmul(sbv[:, half * 512:(half + 1) * 512],
                                     KT[off:off + 64, KOFF[c] + kb * 128:KOFF[c] + (kb + 1) * 128],
                                     QTb[off:off + 64, c * 1024 + half * 512:c * 1024 + (half + 1) * 512],
                                     start=True, stop=True),
                            reads=(f"KT:{c}", f"QTb:{c}", sk0 if half == 0 else sk1),
                            writes=(sk0 if half == 0 else sk1,), token=(half == 1))
                    bld.settok([sk0, sk1], tok)
                    pidx = kb % 3
                    bld.op("scalar", lambda e, kb=kb, h=h, pidx=pidx, sbv=sbv: e.activation(
                           Pring[pidx][:], sbv[:], AF.Exp, scale=0.125,
                           bias=bias_tbl[:, BOFF[h] + kb:BOFF[h] + kb + 1]),
                           reads=(sk0, sk1, f"bias:{h}"), writes=(f"P:{pidx}",))

                def emit_PV(kb, h=h, nkb=nkb):
                    pidx = kb % 3
                    for half in range(2):
                        _, opk, oview, _ = o_ap(h, half)
                        vslice = VT[:, VOFF[kb] + h * 65:VOFF[kb] + (h + 1) * 65]
                        tok = bld.op("tensor", lambda e, vslice=vslice, kb=kb, half=half,
                            oview=oview, pidx=pidx, nkb=nkb:
                            e.matmul(oview[0:65, :], vslice,
                                     Pring[pidx][:, half * 512:(half + 1) * 512],
                                     start=(kb == 0), stop=(kb == nkb - 1)),
                            reads=(f"VT:{kb}", f"vones:{kb}", f"P:{pidx}") +
                                  ((opk,) if kb == 0 else ()),
                            writes=(opk,) if kb == 0 else (), token=(kb == nkb - 1))
                        if kb == nkb - 1:
                            bld.settok([opk], tok)

                emit_S(0)
                if nkb > 1:
                    emit_S(1)
                for kb in range(nkb):
                    if kb + 2 < nkb:
                        emit_S(kb + 2)
                    emit_PV(kb)
                if h >= 1:
                    emit_evac(qp, h - 1)
                if h >= 3:
                    emit_div((h - 3) // 2, (h - 3) % 2, h)
                if h >= 2 and h % 2 == 0:
                    emit_recip(h // 2 - 1)
            emit_evac(qp, H - 1)
            emit_recip(H // 2 - 1)
            emit_div(6, 1, 16)
            emit_div(7, 0, 17)
            emit_div(7, 1, 18)
            for tt in range(8):
                gt = qp * 8 + tt
                xi = xin[gt % 4]
                xik = f"xin{gt % 4}"
                bld.op("sync", lambda e, gt=gt, xi=xi: e.dma_start(out=xi[:], in_=xo_t[gt]),
                       writes=(xik,), q="QS")
                for half in range(2):
                    bank = half
                    pk = pgk[bank]
                    for ac in range(8):
                        bld.op("tensor", lambda e, ac=ac, tt=tt, half=half, bank=bank:
                            e.matmul(pg[bank][:], OT[:, ac * 1024 + tt * 128:ac * 1024 + (tt + 1) * 128],
                                     wo3[:, ac, half * 512:(half + 1) * 512],
                                     start=(ac == 0), stop=False),
                            reads=(f"OT:{ac}", "Wv16") + ((pk,) if ac == 0 else ()),
                            writes=(pk,) if ac == 0 else (), token=False)
                    tok = bld.op("tensor", lambda e, half=half, bank=bank:
                        e.matmul(pg[bank][:], onesrow16[:], bo16[:, half * 512:(half + 1) * 512],
                                 start=False, stop=True),
                        reads=("onesrow16", "bo16"), writes=(), token=True)
                    bld.settok([pk], tok)
                x1k = f"x1b{tt % 2}"
                for half in range(2):
                    bld.op("vector", lambda e, tt=tt, half=half, xi=xi: e.tensor_add(
                           x1b[tt % 2][:, half * 512:(half + 1) * 512],
                           pg[half][:], xi[:, half * 512:(half + 1) * 512]),
                           reads=(pgk[half], xik), writes=(x1k,))
                ln_stats(x1k, x1b[tt % 2][:], gt, LN2C, f"ln2c:{gt}")
                bld.op("sync", lambda e, gt=gt, tt=tt: e.dma_start(
                       out=x1_t[gt], in_=x1b[tt % 2][:]), reads=(x1k,),
                       writes=(f"x1scr:{gt}",), q="QS")
                emit_tern_step()
        while tj_i[0] < len(tern_jobs):
            emit_tern_step()

        if KPH < 4:
            raise _Stop()
        bld.barrier()
        sb_free_back_to(XLT)
        if os.environ.get("KDBG"):
            print("alive after free:", [hh.name if hasattr(hh,'name') else str(hh) for cm,hh in alive])
        wg8 = sb("wg8", [128, 8 * 4096], FP8)
        wu8 = sb("wu8", [128, 8 * 4096], FP8)
        wd8 = sb("wd8", [128, 32 * 1024], FP8)
        gu = sb("gu", [128, 32 * TG], F32)
        hT = sb("hT", [128, 32 * TG], BF16)
        xqT = [sb(f"xqT{i}", [128, 8 * TG], BF16) for i in range(2)]
        x1in = [sb(f"x1in{i}", [128, D], F32) for i in range(6)]
        xhat = sb("xhat", [128, D], F32)
        gsil = [sb(f"gsil{i}", [128, TG], F32) for i in range(2)]
        s2b = sb("s2b", [128, 2 * TG], F32)
        s2row = sb("s2row", [1, 2 * TG], F32)
        Mx = sb("Mx", [128, TG], F32)
        gabs = [sb(f"gabs{i}", [128, TG], F32) for i in range(2)]
        outb = [sb(f"outb{i}", [128, D], F32) for i in range(2)]
        dcols = sb("dcols", [128, 12], F32)

        wg83 = wg8[:].rearrange("p (c n) -> p c n", c=8)
        wu83 = wu8[:].rearrange("p (c n) -> p c n", c=8)
        wd83 = wd8[:].rearrange("p (c n) -> p c n", c=32)

        gbanks = [(ps4[:], "ps4"), (ps5[:], "ps5"), (psEa, "psEa"),
                  (psW0[:, 0:512], "psW0a")]
        dbanks = [(psW0[:, 512:1024], "psW0b"), (psW1[:, 0:512], "psW1a")]

        def emit_lnxq(tg):
            for ti in range(2):
                gt = tg * 2 + ti
                xk = f"x1in{gt % 6}"
                bld.op("sync", lambda e, gt=gt: e.dma_start(out=x1in[gt % 6][:], in_=x1_t[gt]),
                       reads=(f"x1scr:{gt}",), writes=(xk,), q="QS")
                bld.op("vector", lambda e, gt=gt: e.tensor_scalar(xhat[:], x1in[gt % 6][:],
                       LN2C[:, 2 * gt:2 * gt + 1], LN2C[:, 2 * gt + 1:2 * gt + 2],
                       ALU.mult, ALU.add),
                       reads=(xk, f"ln2c:{gt}:r", f"ln2c:{gt}:m"), writes=("xhat",))
                bld.op("vector", lambda e: e.reduce_max(dcols[:, 0:1], xhat[:], axis=AX.X,
                       apply_absolute_value=True), reads=("xhat",), writes=("d_m1",))
                bld.op("vector", lambda e: e.tensor_scalar(dcols[:, 0:1], dcols[:, 0:1],
                       1e-5, None, ALU.max), reads=("d_m1",), writes=("d_m1",))
                bld.op("vector", lambda e: e.reciprocal(dcols[:, 1:2], dcols[:, 0:1]),
                       reads=("d_m1",), writes=("d_rm1",))
                bld.op("vector", lambda e: e.tensor_scalar(dcols[:, 2:3], dcols[:, 1:2],
                       127.0, None, ALU.mult), reads=("d_rm1",), writes=("d_s1",))
                bld.op("vector", lambda e: e.tensor_scalar(dcols[:, 3:4], dcols[:, 0:1],
                       float(1.0 / 127.0), None, ALU.mult), reads=("d_m1",), writes=("d_sv1",))
                bld.op("vector", lambda e: e.tensor_scalar(xhat[:], xhat[:],
                       dcols[:, 2:3], mcol[:], ALU.mult, ALU.add),
                       reads=("xhat", "d_s1", "mcol"), writes=("xhat",))
                bld.op("vector", lambda e: e.tensor_scalar(xhat[:], xhat[:],
                       mcol[:], dcols[:, 3:4], ALU.subtract, ALU.mult),
                       reads=("xhat", "d_sv1", "mcol"), writes=("xhat",))
                xq3 = xqT[tg % 2][:].rearrange("p (c n) -> p c n", c=8)
                for rnd in range(2):
                    for j in range(4):
                        jj = rnd * 4 + j
                        tok = bld.op("tensor", lambda e, j=j, jj=jj: e.transpose(
                            out=psEb[:, j * 128:(j + 1) * 128],
                            in_=xhat[:, jj * 128:(jj + 1) * 128],
                            identity=ident32[:]),
                            reads=("xhat", "ident32") + (("psEb",) if j == 0 else ()),
                            writes=("psEb",) if j == 0 else (), token=(j == 3))
                    bld.settok(["psEb"], tok)
                    bld.op("scalar", lambda e, ti=ti, rnd=rnd, xq3=xq3: e.activation(
                           xq3[:, rnd * 4:(rnd + 1) * 4, ti * 128:(ti + 1) * 128],
                           psEb[:].rearrange("p (c n) -> p c n", c=4), AF.Identity),
                           reads=("psEb",), writes=(f"xqT{tg % 2}",))

        def emit_gateup(tg):
            xq3 = xqT[tg % 2][:].rearrange("p (c n) -> p c n", c=8)
            xqk = f"xqT{tg % 2}"
            for ic in range(32):
                if ic % 2 == 0:
                    drain_hT(2)
                bview, pk = gbanks[ic % 4]
                for dc in range(8):
                    bld.op("tensor", lambda e, ic=ic, dc=dc, bview=bview, xq3=xq3: e.matmul(
                           bview[:, 0:TG], wg83[:, dc, ic * 128:(ic + 1) * 128],
                           xq3[:, dc, :], start=(dc == 0), stop=(dc == 7)),
                           reads=("wg8", xqk) + ((pk,) if dc == 0 else ()),
                           writes=(pk,) if dc == 0 else (), token=False)
                for dc in range(8):
                    tok = bld.op("tensor", lambda e, ic=ic, dc=dc, bview=bview, xq3=xq3: e.matmul(
                           bview[:, TG:2 * TG], wu83[:, dc, ic * 128:(ic + 1) * 128],
                           xq3[:, dc, :], start=(dc == 0), stop=(dc == 7)),
                           reads=("wu8", xqk), writes=(), token=(dc == 7))
                bld.settok([pk], tok)
                gk = f"gsil{ic % 2}"
                bld.op("scalar", lambda e, ic=ic, bview=bview: e.activation(
                       gsil[ic % 2][:], bview[:, 0:TG], AF.Silu,
                       scale=scal[:, SC_SWG:SC_SWG + 1]),
                       reads=(pk, f"sc:{SC_SWG}"), writes=(gk,))
                bld.op("vector", lambda e, ic=ic, bview=bview: e.scalar_tensor_tensor(
                       gu[:, ic * TG:(ic + 1) * TG], bview[:, TG:2 * TG],
                       scal[:, SC_SWU:SC_SWU + 1], gsil[ic % 2][:], ALU.mult, ALU.mult),
                       reads=(pk, f"sc:{SC_SWU}", gk), writes=(f"gu:{ic}",))
                if ic == 0:
                    bld.op("scalar", lambda e: e.activation(Mx[:], gu[:, 0:TG], AF.Abs),
                           reads=("gu:0",), writes=("Mx",))
                else:
                    gak = f"gabs{ic % 2}"
                    bld.op("scalar", lambda e, ic=ic: e.activation(
                           gabs[ic % 2][:], gu[:, ic * TG:(ic + 1) * TG], AF.Abs),
                           reads=(f"gu:{ic}",), writes=(gak,))
                    bld.op("vector", lambda e, ic=ic: e.tensor_max(Mx[:], Mx[:],
                           gabs[ic % 2][:]), reads=(gak, "Mx"), writes=("Mx",))

        def emit_quant_h(tg):
            for ti in range(2):
                tok = bld.op("tensor", lambda e, ti=ti: e.transpose(
                       out=psW1[:, 512 + ti * 128:512 + (ti + 1) * 128],
                       in_=Mx[:, ti * 128:(ti + 1) * 128], identity=ident32[:]),
                       reads=("Mx", "ident32") + (("psW1b",) if ti == 0 else ()),
                       writes=("psW1b",) if ti == 0 else (), token=True)
            bld.settok(["psW1b"], tok)
            for ti in range(2):
                bld.op("vector", lambda e, ti=ti: e.reduce_max(dcols[:, 4 + ti:5 + ti],
                       psW1[:, 512 + ti * 128:512 + (ti + 1) * 128], axis=AX.X),
                       reads=("psW1b",), writes=(f"d_m2:{ti}",))
                bld.op("vector", lambda e, ti=ti: e.tensor_scalar(dcols[:, 4 + ti:5 + ti],
                       dcols[:, 4 + ti:5 + ti], 1e-5, None, ALU.max),
                       reads=(f"d_m2:{ti}",), writes=(f"d_m2:{ti}",))
                bld.op("vector", lambda e, ti=ti: e.reciprocal(dcols[:, 8 + ti:9 + ti],
                       dcols[:, 4 + ti:5 + ti]), reads=(f"d_m2:{ti}",), writes=(f"d_rm2:{ti}",))
                bld.op("vector", lambda e, ti=ti: e.tensor_scalar(dcols[:, 10 + ti:11 + ti],
                       dcols[:, 8 + ti:9 + ti], 127.0, None, ALU.mult),
                       reads=(f"d_rm2:{ti}",), writes=(f"d_s2:{ti}",))
                bld.op("vector", lambda e, ti=ti: e.scalar_tensor_tensor(
                       dcols[:, 6 + ti:7 + ti], dcols[:, 4 + ti:5 + ti],
                       float(1.0 / 127.0), scal[:, SC_SWD:SC_SWD + 1], ALU.mult, ALU.mult),
                       reads=(f"d_m2:{ti}", f"sc:{SC_SWD}"), writes=(f"dcmb:{ti}",))
            for ti in range(2):
                tok = bld.op("tensor", lambda e, ti=ti: e.transpose(
                       out=psW1[0:1, 512 + 256 + ti * 128:512 + 256 + (ti + 1) * 128],
                       in_=dcols[:, 10 + ti:11 + ti], identity=ident32[:]),
                       reads=(f"d_s2:{ti}", "ident32") + (("psW1b",) if ti == 0 else ()),
                       writes=("psW1b",) if ti == 0 else (), token=True)
            bld.settok(["psW1b"], tok)
            bld.op("vector", lambda e: e.tensor_copy(s2row[0:1, 0:TG], psW1[0:1, 768:768 + TG]),
                   reads=("psW1b",), writes=("s2row",))
            bld.op("vector", lambda e: e.tensor_copy(s2row[0:1, TG:2 * TG], s2row[0:1, 0:TG]),
                   reads=("s2row",), writes=("s2row",))
            bld.op("tensor", lambda e: e.matmul(psW1[:, 512:1024], onesrow[:],
                   s2row[:], start=True, stop=True),
                   reads=("s2row", "onesrow"), writes=("psW1b",))
            bld.op("scalar", lambda e: e.activation(s2b[:], psW1[:, 512:1024],
                   AF.Identity), reads=("psW1b",), writes=("s2b",))

        pending_hT = []

        def emit_quant_hT(tg):
            for ic in range(0, 32, 2):
                def th1(ic=ic):
                    bld.op("vector", lambda e, ic=ic: e.tensor_mul(
                           gu[:, ic * TG:(ic + 2) * TG],
                           gu[:, ic * TG:(ic + 2) * TG], s2b[:]),
                           reads=(f"gu:{ic}", f"gu:{ic + 1}", "s2b"),
                           writes=(f"gu:{ic}", f"gu:{ic + 1}"))
                def th2(ic=ic):
                    bld.op("vector", lambda e, ic=ic: e.tensor_scalar(
                           hT[:, ic * TG:(ic + 2) * TG], gu[:, ic * TG:(ic + 2) * TG],
                           mcol[:], mcol[:], ALU.add, ALU.subtract),
                           reads=(f"gu:{ic}", f"gu:{ic + 1}", "mcol"), writes=("hT",))
                pending_hT.append(th1)
                pending_hT.append(th2)

        def drain_hT(k=1):
            for _ in range(k):
                if pending_hT:
                    pending_hT.pop(0)()

        def emit_down(tg):
            for ti in range(2):
                gt = tg * 2 + ti
                xk = f"x1in{gt % 6}"
                ob = outb[gt % 2]
                obk = f"outb{gt % 2}"
                for half in range(2):
                    bview, pk = dbanks[half]
                    for ic in range(32):
                        tok = bld.op("tensor", lambda e, ic=ic, ti=ti, half=half, bview=bview:
                            e.matmul(bview[:],
                                     hT[:, ic * TG + ti * 128:ic * TG + (ti + 1) * 128],
                                     wd83[:, ic, half * 512:(half + 1) * 512],
                                     start=(ic == 0), stop=(ic == 31)),
                            reads=("hT", "wd8") + ((pk,) if ic == 0 else ()),
                            writes=(pk,) if ic == 0 else (), token=(ic == 31))
                    bld.settok([pk], tok)
                    bld.op("vector", lambda e, ti=ti, half=half, bview=bview, ob=ob, gt=gt:
                        e.scalar_tensor_tensor(ob[:, half * 512:(half + 1) * 512],
                            bview[:], dcols[:, 6 + ti:7 + ti],
                            x1in[gt % 6][:, half * 512:(half + 1) * 512],
                            ALU.mult, ALU.add),
                        reads=(pk, f"dcmb:{ti}", xk), writes=(obk,))
                bld.op("sync", lambda e, gt=gt, ob=ob: e.dma_start(out=y_t[gt], in_=ob[:]),
                       reads=(obk,), q="QS")

        emit_lnxq(0)
        emit_lnxq(1)
        bld.op("sync", lambda e: e.dma_start(out=wg8[:], in_=wg8d),
               reads=("w8d:g",), writes=("wg8",), q="QS")
        bld.op("sync", lambda e: e.dma_start(out=wu8[:], in_=wu8d),
               reads=("w8d:u",), writes=("wu8",), q="QS")
        bld.op("sync", lambda e: e.dma_start(out=wd8[:], in_=wd8d),
               reads=("w8d:d",), writes=("wd8",), q="QS")
        for tg in range(NTG):
            emit_gateup(tg)
            if tg > 0:
                emit_down(tg - 1)
            emit_quant_h(tg)
            if tg + 2 < NTG:
                emit_lnxq(tg + 2)
            emit_quant_hT(tg)
        drain_hT(32)
        emit_down(NTG - 1)
    except _Stop:
        pass

    for eng in ("tensor", "vector", "scalar", "gpsimd"):
        bld.op(eng, lambda e: e.nop(), token=True)
    for eng in ("tensor", "scalar"):
        sem = bld.semof[eng]
        v = bld.cnt[sem]
        bld.plan[eng].append(lambda e, sem=sem, v=v: e.wait_ge(bld.sems[sem], v))
    for i in range(8):
        v = bld.cnt[f"QS{i}"]
        if v:
            bld.plan["sync"].append(lambda e, i=i, v=v: e.wait_ge(bld.sems[f"QS{i}"], v))
    for i in range(2):
        v = bld.cnt[f"QG{i}"]
        if v:
            bld.plan["sync"].append(lambda e, i=i, v=v: e.wait_ge(bld.sems[f"QG{i}"], v))

    import contextlib
    with contextlib.ExitStack() as stk:
        sems = {}
        for nm in list(bld.cnt.keys()):
            sems[nm] = stk.enter_context(nc.semaphore("s" + nm))
        blk = stk.enter_context(nc.Block())
        bld.sems = sems

        @blk.sync
        def _(e):
            for f in bld.plan["sync"]:
                f(e)

        @blk.tensor
        def _(e):
            for f in bld.plan["tensor"]:
                f(e)

        @blk.vector
        def _(e):
            for f in bld.plan["vector"]:
                f(e)

        @blk.scalar
        def _(e):
            for f in bld.plan["scalar"]:
                f(e)

        @blk.gpsimd
        def _(e):
            for f in bld.plan["gpsimd"]:
                f(e)

    while alive:
        cm, h = alive.pop()
        cm.__exit__(None, None, None)
    for cm in reversed(psum_cm):
        cm.__exit__(None, None, None)
    return nc


_NC = None


def kernel(L0_attn_x, L0_attn_Wq, L0_attn_bq, L0_attn_Wk, L0_attn_bk,
           L0_attn_Wv, L0_attn_bv, L0_attn_Wo, L0_attn_bo,
           L0_ffn_w_gate, L0_ffn_w_up, L0_ffn_w_down):
    global _NC
    if _NC is None:
        _NC = _build()
    nc = _NC
    f32 = np.float32
    x = np.ascontiguousarray(L0_attn_x, dtype=f32)
    WqT = np.ascontiguousarray(np.asarray(L0_attn_Wq, f32).T)
    WkT = np.ascontiguousarray(np.asarray(L0_attn_Wk, f32).T)
    WvT = np.ascontiguousarray(np.asarray(L0_attn_Wv, f32).T)
    WoT = np.ascontiguousarray(np.asarray(L0_attn_Wo, f32).T)
    bq_r = np.ascontiguousarray(np.asarray(L0_attn_bq, f32).reshape(8, 128).T)
    bk_r = np.ascontiguousarray(np.asarray(L0_attn_bk, f32).reshape(8, 128).T)
    bv_r = np.ascontiguousarray(np.asarray(L0_attn_bv, f32).reshape(8, 128).T)
    bo_r = np.ascontiguousarray(np.asarray(L0_attn_bo, f32).reshape(1, D))
    wgT = np.ascontiguousarray(np.asarray(L0_ffn_w_gate, f32).T)
    wuT = np.ascontiguousarray(np.asarray(L0_ffn_w_up, f32).T)
    wdT = np.ascontiguousarray(np.asarray(L0_ffn_w_down, f32).T)

    in_maps = []
    for core in range(NCORE):
        b, par = core // 2, core % 2
        blocks = [par + 2 * i for i in range(NQB)]
        x_own = np.ascontiguousarray(
            np.concatenate([x[b, qb * 512:(qb + 1) * 512] for qb in blocks], axis=0))
        x_kv = np.ascontiguousarray(x[b, :KVT])
        in_maps.append({
            "x_own": x_own, "x_kv": x_kv,
            "WqT": WqT, "WkT": WkT, "WvT": WvT, "WoT": WoT,
            "bq_r": bq_r, "bk_r": bk_r, "bv_r": bv_r, "bo_r": bo_r,
            "wgT": wgT, "wuT": wuT, "wdT": wdT,
        })
    res = run_bass_kernel_spmd(nc, in_maps, core_ids=list(range(NCORE)))
    out = np.empty((B, L, D), dtype=f32)
    for core in range(NCORE):
        b, par = core // 2, core % 2
        yv = res.results[core]["y"]
        for i in range(NQB):
            qb = par + 2 * i
            out[b, qb * 512:(qb + 1) * 512] = yv[i * 512:(i + 1) * 512]
    return out


# version 25
# speedup vs baseline: 1.0226x; 1.0226x over previous
import os
import numpy as np
import ml_dtypes
import concourse.bass as bass
import concourse.mybir as mybir
from concourse.bass_utils import run_bass_kernel_spmd

F32 = mybir.dt.float32
BF16 = mybir.dt.bfloat16
FP8 = mybir.dt.float8e4
AF = mybir.ActivationFunctionType
ALU = mybir.AluOpType
AX = mybir.AxisListType

B, L, D, H, HD, INNER = 4, 4096, 1024, 16, 64, 4096
NCORE = 8
NOWN = 2048
TG = 256
NTG = NOWN // TG
NQB = 4

_EXP = [(-4.0 + h * (3.7 / 15.0)) for h in range(H)]
SLOPES64 = [float(2.0 ** e) for e in _EXP]
KEPT = [min(int(np.ceil(130.0 / s / 128.0)) * 128, L) for s in SLOPES64]
KVT = max(KEPT)
NKVT = KVT // 128
NOT = NOWN // 128
NKB = [k // 128 for k in KEPT]
BOFF = np.concatenate([[0], np.cumsum(NKB)]).astype(int)
TOTKB = int(BOFF[-1])
KEPT_E = [KEPT[2 * c] for c in range(8)]
KOFF = np.concatenate([[0], np.cumsum(KEPT_E)]).astype(int)
KCOLS = int(KOFF[-1])
NHT = [sum(1 for h in range(H) if KEPT[h] > 128 * t) for t in range(NKVT)]
VOFF = np.concatenate([[0], np.cumsum([65 * n for n in NHT])]).astype(int)
VCOLS = int(VOFF[-1])

TINY = float(2.0 ** -26)
MAGIC = float(1.5 * 2 ** 23)
MSCALE = float(1.5 * 2 ** 78)
EPS_SOFT = 1e-5
LN2_64 = float(np.log(np.float64(2.0)))


def _dual(c64):
    c32 = float(np.float32(c64))
    corr = float(np.float32((c64 - c32) / TINY))
    return c32, corr


def _build():
    nc = bass.Bass("TRN2", target_bir_lowering=False)
    x_own = nc.dram_tensor("x_own", [NOWN, D], F32, kind="ExternalInput").ap()
    x_kv = nc.dram_tensor("x_kv", [KVT, D], F32, kind="ExternalInput").ap()
    WqT = nc.dram_tensor("WqT", [D, D], F32, kind="ExternalInput").ap()
    WkT = nc.dram_tensor("WkT", [D, D], F32, kind="ExternalInput").ap()
    WvT = nc.dram_tensor("WvT", [D, D], F32, kind="ExternalInput").ap()
    WoT = nc.dram_tensor("WoT", [D, D], F32, kind="ExternalInput").ap()
    bq_r = nc.dram_tensor("bq_r", [128, 8], F32, kind="ExternalInput").ap()
    bk_r = nc.dram_tensor("bk_r", [128, 8], F32, kind="ExternalInput").ap()
    bv_r = nc.dram_tensor("bv_r", [128, 8], F32, kind="ExternalInput").ap()
    bo_r = nc.dram_tensor("bo_r", [1, D], F32, kind="ExternalInput").ap()
    wgT = nc.dram_tensor("wgT", [D, INNER], F32, kind="ExternalInput").ap()
    wuT = nc.dram_tensor("wuT", [D, INNER], F32, kind="ExternalInput").ap()
    wdT = nc.dram_tensor("wdT", [INNER, D], F32, kind="ExternalInput").ap()
    y = nc.dram_tensor("y", [NOWN, D], F32, kind="ExternalOutput").ap()
    x1scr = nc.dram_tensor("x1scr", [NOWN, D], F32, kind="Internal").ap()
    wg8d = nc.dram_tensor("wg8d", [128, 8 * 4096], FP8, kind="Internal").ap()
    wu8d = nc.dram_tensor("wu8d", [128, 8 * 4096], FP8, kind="Internal").ap()
    wd8d = nc.dram_tensor("wd8d", [128, 32 * 1024], FP8, kind="Internal").ap()

    xo_t = x_own.rearrange("(t p) d -> t p d", p=128)
    xk_t = x_kv.rearrange("(t p) d -> t p d", p=128)
    y_t = y.rearrange("(t p) d -> t p d", p=128)
    x1_t = x1scr.rearrange("(t p) d -> t p d", p=128)
    wgT_c = wgT.rearrange("(c p) n -> p c n", p=128)
    wuT_c = wuT.rearrange("(c p) n -> p c n", p=128)
    wdT_c = wdT.rearrange("(c p) n -> p c n", p=128)
    WT_c = {"q": WqT.rearrange("(c p) n -> p c n", p=128),
            "k": WkT.rearrange("(c p) n -> p c n", p=128),
            "v": WvT.rearrange("(c p) n -> p c n", p=128),
            "o": WoT.rearrange("(c p) n -> p c n", p=128)}

    class Bld:
        def __init__(self):
            self.plan = {"sync": [], "tensor": [], "vector": [], "scalar": [], "gpsimd": []}
            self.cnt = {"PE": 0, "DVE": 0, "ACT": 0, "POOL": 0}
            for i in range(8):
                self.cnt[f"QS{i}"] = 0
            for i in range(2):
                self.cnt[f"QG{i}"] = 0
            self.rr = {"QS": 0, "QG": 0}
            self.semof = {"tensor": "PE", "vector": "DVE", "scalar": "ACT", "gpsimd": "POOL"}
            self.waited = {}
            self.lastw = {}
            self.readers = {}
            self.floor = {}
            self.sems = None
            self.wsince = {}
            self.lasttok = {}

        def _wait(self, eng, sem, val):
            if val <= 0:
                return
            if self.waited.get((eng, sem), 0) >= val:
                return
            self.waited[(eng, sem)] = val
            self.wsince[eng] = True
            def f(e, sem=sem, val=val):
                e.wait_ge(self.sems[sem], val)
            self.plan[eng].append(f)

        def barrier(self):
            for s, v in self.cnt.items():
                self.floor[s] = v

        def dep(self, eng, reads=(), war=()):
            need = dict(self.floor)
            for k in reads:
                t = self.lastw.get(k)
                if t is not None:
                    assert t != "PENDING", f"read of in-flight group buffer {k}"
                    need[t[0]] = max(need.get(t[0], 0), t[1])
            for k in war:
                t = self.lastw.get(k)
                if t is not None and t != "PENDING":
                    need[t[0]] = max(need.get(t[0], 0), t[1])
                for (s, v) in self.readers.get(k, ()):
                    need[s] = max(need.get(s, 0), v)
            for s, v in need.items():
                self._wait(eng, s, v)

        def op(self, eng, fn, reads=(), writes=(), token=True, q=None):
            self.dep(eng, reads, war=writes)
            if q:
                nl = 8 if q == "QS" else 2
                lane = self.rr[q]
                self.rr[q] = (lane + 1) % nl
                sem = f"{q}{lane}"
                self._wait(eng, sem, self.cnt[sem])
            else:
                sem = self.semof[eng]
            if token:
                inc = 16 if q else 1
                if q is None and eng in ("tensor", "scalar"):
                    if not self.wsince.get(eng, True):
                        pv = self.lasttok[eng]
                        if self.waited.get((eng, sem), 0) < pv:
                            self.waited[(eng, sem)] = pv
                            def fw(e, sem=sem, pv=pv):
                                e.wait_ge(self.sems[sem], pv)
                            self.plan[eng].append(fw)
                    self.cnt[sem] += inc
                    tok = (sem, self.cnt[sem])
                    self.wsince[eng] = False
                    self.lasttok[eng] = self.cnt[sem]
                    def f(e, fn=fn, sem=sem, inc=inc):
                        fn(e).then_inc(self.sems[sem], inc)
                elif q is None:
                    self.cnt[sem] += inc
                    tok = (sem, self.cnt[sem])
                    val = self.cnt[sem]
                    def f(e, fn=fn, sem=sem, inc=inc, val=val):
                        fn(e).then_inc(self.sems[sem], inc)
                        e.wait_ge(self.sems[sem], val)
                    self.waited[(eng, sem)] = val
                    self.wsince[eng] = True
                else:
                    self.cnt[sem] += inc
                    tok = (sem, self.cnt[sem])
                    def f(e, fn=fn, sem=sem, inc=inc):
                        fn(e).then_inc(self.sems[sem], inc)
            else:
                tok = (sem, self.cnt[sem] + 1)
                def f(e, fn=fn):
                    fn(e)
            self.plan[eng].append(f)
            for k in reads:
                self.readers.setdefault(k, []).append(tok)
            for k in writes:
                self.lastw[k] = tok if token else "PENDING"
                self.readers[k] = []
            return tok

        def settok(self, keys, tok):
            for k in keys:
                self.lastw[k] = tok

    bld = Bld()
    alive = []

    def sb(name, shape, dtype):
        cm = nc.sbuf_tensor(name, shape, dtype)
        h = cm.__enter__()
        alive.append((cm, h))
        return h

    def sb_free_back_to(h):
        idx = next(i for i, (cm, hh) in enumerate(alive) if hh is h)
        while len(alive) > idx:
            cm, hh = alive.pop()
            cm.__exit__(None, None, None)

    ident16 = sb("ident16", [128, 128], BF16)
    ident32 = sb("ident32", [128, 128], F32)
    kidx = sb("kidx", [128, max(NKB)], F32)
    bias_tbl = sb("bias_tbl", [128, TOTKB], F32)
    onescol = sb("onescol", [128, 1], F32)
    onesrow = sb("onesrow", [1, 128], F32)
    onesrow16 = sb("onesrow16", [1, 128], BF16)
    fcell = sb("fcell", [128, 1], F32)
    epscol = sb("epscol", [128, 1], F32)
    mcol = sb("mcol", [128, 1], F32)
    negm = sb("negm", [128, 1], F32)
    m2col = sb("m2col", [128, 1], F32)
    soft_lo = sb("soft_lo", [128, 1], F32)
    soft_hi = sb("soft_hi", [128, 1], F32)
    lneps = sb("lneps", [128, 1], F32)
    bq_sb = sb("bq_sb", [128, 8], F32)
    bk_sb = sb("bk_sb", [128, 8], F32)
    bv_sb = sb("bv_sb", [128, 8], F32)
    bo16 = sb("bo16", [1, D], BF16)
    scal = sb("scal", [128, 20], F32)
    LN1C = sb("LN1C", [128, 2 * (NKVT + NOT)], F32)
    LN2C = sb("LN2C", [128, 2 * NOT], F32)
    slope_c = sb("slope_c", [128, 1], F32)
    SC_SG, SC_SWG, SC_SU, SC_SWU, SC_SD, SC_SWD = 0, 1, 2, 3, 4, 5
    SC_T0, SC_T1 = 6, 7

    psum_cm = []
    def psb(name, shape, dtype):
        cm = nc.psum_tensor(name, shape, dtype)
        h = cm.__enter__()
        psum_cm.append(cm)
        return h
    psW0 = psb("psW0", [128, 1024], F32)
    psW1 = psb("psW1", [128, 1024], F32)
    ps4 = psb("ps4", [128, 512], F32)
    ps5 = psb("ps5", [128, 512], F32)
    psE = psb("psE", [128, 1024], F32)
    psEa = psE[:, 0:512]
    psEb = psE[:, 512:1024]

    bld.op("gpsimd", lambda e: e.memset(ident16[:], 0.0), writes=("ident16",))
    bld.op("gpsimd", lambda e: e.affine_select(out=ident16[:], in_=ident16[:],
           compare_op=ALU.not_equal, fill=1.0, base=0, pattern=[[-1, 128]],
           channel_multiplier=1), reads=("ident16",), writes=("ident16",))
    bld.op("gpsimd", lambda e: e.nop(), reads=("ident16",))
    bld.op("gpsimd", lambda e: e.iota(kidx[:], [[128, max(NKB)]], channel_multiplier=1,
           allow_small_or_imprecise_dtypes=True), writes=("kidx",))
    bld.op("vector", lambda e: e.memset(onescol[:], 1.0), writes=("onescol",))
    bld.op("vector", lambda e: e.memset(lneps[:], 1e-5), writes=("lneps",))
    bld.op("vector", lambda e: e.memset(onesrow[:], 1.0), writes=("onesrow",))
    bld.op("vector", lambda e: e.memset(onesrow16[:], 1.0), writes=("onesrow16",))
    bld.op("vector", lambda e: e.tensor_scalar(fcell[:], onescol[:], TINY, 1.0,
           ALU.add, ALU.subtract), reads=("onescol",), writes=("fcell",))
    bld.op("vector", lambda e: e.tensor_scalar(epscol[:], fcell[:], -1.0, TINY,
           ALU.mult, ALU.add), reads=("fcell",), writes=("epscol",))
    bld.op("vector", lambda e: e.tensor_scalar(mcol[:], fcell[:], MSCALE, MAGIC,
           ALU.mult, ALU.add), reads=("fcell",), writes=("mcol",))
    bld.op("vector", lambda e: e.tensor_scalar(negm[:], mcol[:], -1.0, None,
           ALU.mult), reads=("mcol",), writes=("negm",))
    bld.op("vector", lambda e: e.tensor_scalar(m2col[:], mcol[:], 2.0, None,
           ALU.mult), reads=("mcol",), writes=("m2col",))
    _esc = float(EPS_SOFT / TINY)
    bld.op("vector", lambda e: e.tensor_scalar(soft_lo[:], epscol[:], -_esc, 1.0,
           ALU.mult, ALU.add), reads=("epscol",), writes=("soft_lo",))
    bld.op("vector", lambda e: e.tensor_scalar(soft_hi[:], epscol[:], _esc, 1.0,
           ALU.mult, ALU.add), reads=("epscol",), writes=("soft_hi",))
    bld.op("vector", lambda e: e.tensor_copy(ident32[:], ident16[:]),
           reads=("ident16",), writes=("ident32",))
    bld.op("sync", lambda e: e.dma_start(out=bq_sb[:], in_=bq_r), writes=("bq_sb",), q="QS")
    bld.op("sync", lambda e: e.dma_start(out=bk_sb[:], in_=bk_r), writes=("bk_sb",), q="QS")
    bld.op("sync", lambda e: e.dma_start(out=bv_sb[:], in_=bv_r), writes=("bv_sb",), q="QS")


    for h in range(H):
        c32, corr = _dual(_EXP[h] * LN2_64)
        bld.op("vector", lambda e, c32=c32, corr=corr: e.tensor_scalar(
            scal[0:1, SC_T0:SC_T0 + 1], fcell[0:1, :], corr, c32, ALU.mult, ALU.add),
            reads=("fcell",), writes=("sc_t0",))
        bld.op("scalar", lambda e: e.activation(scal[0:1, SC_T1:SC_T1 + 1],
            scal[0:1, SC_T0:SC_T0 + 1], AF.Exp), reads=("sc_t0",), writes=("sc_t1",))
        bld.op("vector", lambda e: e.tensor_scalar(scal[0:1, SC_T0:SC_T0 + 1],
            scal[0:1, SC_T1:SC_T1 + 1], -1.0, None, ALU.mult),
            reads=("sc_t1",), writes=("sc_t0",))
        bld.op("tensor", lambda e: e.matmul(psEa[:, 0:1], onesrow[:],
            scal[0:1, SC_T0:SC_T0 + 1], start=True, stop=True),
            reads=("onesrow", "sc_t0"), writes=("psEa",))
        bld.op("vector", lambda e: e.tensor_copy(slope_c[:], psEa[:, 0:1]),
            reads=("psEa",), writes=("slope_c",))
        nkb, b0 = NKB[h], int(BOFF[h])
        bld.op("vector", lambda e, nkb=nkb, b0=b0: e.tensor_scalar(
            bias_tbl[:, b0:b0 + nkb], kidx[:, 0:nkb], slope_c[:], None, ALU.mult),
            reads=("kidx", "slope_c"), writes=(f"bias:{h}",))

    KPH = int(os.environ.get("KPH", "4"))

    class _Stop(Exception):
        pass
    try:
        if KPH < 1:
            raise _Stop()
        XLT = sb("XLT", [128, 8 * KVT], BF16)
        bo_f = sb("bo_f", [1, D], F32)
        bld.op("sync", lambda e: e.dma_start(out=bo_f[:], in_=bo_r), writes=("bo_f",), q="QS")
        bld.op("vector", lambda e: e.tensor_copy(bo16[:], bo_f[:]), reads=("bo_f",), writes=("bo16",))
        xlt_kv = XLT[:].rearrange("p (c n) -> p c n", c=8)
        KT = sb("KT", [128, KCOLS], BF16)
        VT = sb("VT", [128, VCOLS], BF16)
        Wk16 = sb("Wk16", [128, 8 * D], BF16)
        Wv16 = sb("Wv16", [128, 8 * D], BF16)
        stg = [sb(f"stg{i}", [128, 2048], F32) for i in range(2)]
        xin = [sb(f"xin{i}", [128, D], F32) for i in range(4)]
        sq = sb("sq", [128, D], F32)
        cols = sb("cols", [128, 12], F32)
        mean_acc = sb("mean_acc", [128, 64], F32)

        def ln_stats(src_key, src_ap, slot, colarr, ckey):
            p6 = (slot % 2) * 6
            cs = lambda i: cols[:, p6 + i:p6 + i + 1]
            kc = lambda i: f"c{slot % 2}_{i}"
            bld.op("vector", lambda e, cs=cs: e.reduce_sum(cs(0), src_ap, axis=AX.X),
                   reads=(src_key,), writes=(kc(0),))
            bld.op("scalar", lambda e, cs=cs: e.activation(sq[:], src_ap, AF.Square,
                   accum_out=cs(1)), reads=(src_key,), writes=("sq", kc(1)))
            bld.op("vector", lambda e, cs=cs: e.tensor_scalar(cs(2), cs(0),
                   float(1.0 / D), None, ALU.mult), reads=(kc(0),), writes=(kc(2),))
            bld.op("vector", lambda e, cs=cs: e.tensor_mul(cs(3), cs(2), cs(2)),
                   reads=(kc(2),), writes=(kc(3),))
            bld.op("vector", lambda e, cs=cs: e.scalar_tensor_tensor(cs(4), cs(1),
                   float(1.0 / D), cs(3), ALU.mult, ALU.subtract),
                   reads=(kc(1), kc(3)), writes=(kc(4),))
            bld.op("scalar", lambda e, cs=cs: e.activation(cs(5), cs(4), AF.Ln,
                   bias=lneps[:]), reads=(kc(4), "lneps"), writes=(kc(5),))
            bld.op("scalar", lambda e, cs=cs, colarr=colarr, slot=slot: e.activation(
                   colarr[:, 2 * slot:2 * slot + 1], cs(5), AF.Exp, scale=-0.5),
                   reads=(kc(5),), writes=(ckey + ":r",))
            bld.op("vector", lambda e, cs=cs, colarr=colarr, slot=slot:
                   e.scalar_tensor_tensor(colarr[:, 2 * slot + 1:2 * slot + 2], cs(2),
                   -1.0, colarr[:, 2 * slot:2 * slot + 1], ALU.mult, ALU.mult),
                   reads=(kc(2), ckey + ":r"), writes=(ckey + ":m",))

        NT_ALL = NKVT + NOT
        astat = sb("astat", [128, 4 * NT_ALL], F32)

        def ln_sum2(src_key, src_ap, slot):
            bld.op("vector", lambda e, src_ap=src_ap, slot=slot: e.reduce_sum(
                   astat[:, slot:slot + 1], src_ap, axis=AX.X),
                   reads=(src_key,), writes=(f"as:{slot}",))
            bld.op("scalar", lambda e, src_ap=src_ap, slot=slot: e.activation(
                   sq[:], src_ap, AF.Square,
                   accum_out=astat[:, NT_ALL + slot:NT_ALL + slot + 1]),
                   reads=(src_key,), writes=("sq", f"aq:{slot}"))

        def ln_batch(b, n):
            A2, A3c = 2 * NT_ALL, 3 * NT_ALL
            rs = tuple(f"as:{s}" for s in range(b, b + n))
            rq = tuple(f"aq:{s}" for s in range(b, b + n))
            bld.op("vector", lambda e, b=b, n=n: e.tensor_scalar(
                   astat[:, A2 + b:A2 + b + n], astat[:, b:b + n],
                   float(1.0 / D), None, ALU.mult),
                   reads=rs, writes=(f"am:{b}",))
            bld.op("vector", lambda e, b=b, n=n: e.tensor_mul(
                   astat[:, A3c + b:A3c + b + n], astat[:, A2 + b:A2 + b + n],
                   astat[:, A2 + b:A2 + b + n]),
                   reads=(f"am:{b}",), writes=(f"am2:{b}",))
            bld.op("vector", lambda e, b=b, n=n: e.scalar_tensor_tensor(
                   astat[:, b:b + n], astat[:, NT_ALL + b:NT_ALL + b + n],
                   float(1.0 / D), astat[:, A3c + b:A3c + b + n],
                   ALU.mult, ALU.subtract),
                   reads=rq + (f"am2:{b}",), writes=(f"av:{b}",))
            bld.op("scalar", lambda e, b=b, n=n: e.activation(
                   astat[:, NT_ALL + b:NT_ALL + b + n], astat[:, b:b + n],
                   AF.Ln, bias=lneps[:]),
                   reads=(f"av:{b}", "lneps"), writes=(f"al:{b}",))
            bld.op("scalar", lambda e, b=b, n=n: e.activation(
                   LN1C[:, b:b + n], astat[:, NT_ALL + b:NT_ALL + b + n],
                   AF.Exp, scale=-0.5),
                   reads=(f"al:{b}",), writes=(f"ar:{b}",))
            bld.op("vector", lambda e, b=b, n=n: e.scalar_tensor_tensor(
                   LN1C[:, NT_ALL + b:NT_ALL + b + n], astat[:, A2 + b:A2 + b + n],
                   -1.0, LN1C[:, b:b + n], ALU.mult, ALU.mult),
                   reads=(f"am:{b}", f"ar:{b}"), writes=(f"an:{b}",))

        def ln_apply_t(src_key, src_ap, slot, bkey, dst3, dstkey, tslot):
            lb = xin[(tslot + 2) % 4]
            lbk = f"xin{(tslot + 2) % 4}"
            bld.op("vector", lambda e, lb=lb, slot=slot: e.tensor_scalar(
                   lb[:], src_ap, LN1C[:, slot:slot + 1],
                   LN1C[:, NT_ALL + slot:NT_ALL + slot + 1], ALU.mult, ALU.add),
                   reads=(src_key, f"ar:{bkey}", f"an:{bkey}"), writes=(lbk,))
            for j in range(8):
                tok = bld.op("tensor", lambda e, j=j, lb=lb: e.transpose(
                    out=psE[:, j * 128:(j + 1) * 128], in_=lb[:, j * 128:(j + 1) * 128],
                    identity=ident32[:]),
                    reads=(lbk, "ident32") + (("psEa", "psEb") if j == 0 else ()),
                    writes=("psEa", "psEb") if j == 0 else (), token=(j == 7))
            bld.settok(["psEa", "psEb"], tok)
            bld.op("scalar", lambda e, dst3=dst3, tslot=tslot: e.activation(
                   dst3[:, :, tslot * 128:(tslot + 1) * 128],
                   psE[:].rearrange("p (c n) -> p c n", c=8), AF.Identity),
                   reads=("psEa", "psEb"), writes=(dstkey,))

        mean_jobs = []
        for ci in range(16):
            c, hcol = ci // 2, ci % 2
            mean_jobs.append((wgT_c[:, c, hcol * 2048:(hcol + 1) * 2048], ci))
        for ci in range(16):
            c, hcol = ci // 2, ci % 2
            mean_jobs.append((wuT_c[:, c, hcol * 2048:(hcol + 1) * 2048], 16 + ci))
        for ci in range(32):
            mean_jobs.append((wdT_c[:, ci, :], 32 + ci))
        mj_i = [0]

        ml_i = [0]

        def emit_mean_load():
            if ml_i[0] >= len(mean_jobs):
                return
            src, _ = mean_jobs[ml_i[0]]
            sgi = ml_i[0] % 2
            ml_i[0] += 1
            n = src.shape[-1]
            sk = f"stg{sgi}"
            h2 = n // 2
            bld.op("sync", lambda e, src=src, sgi=sgi, h2=h2: e.dma_start(
                   out=stg[sgi][:, 0:h2], in_=src[:, 0:h2]),
                   writes=(sk,), q="QS")
            bld.op("sync", lambda e, src=src, sgi=sgi, h2=h2, n=n: e.dma_start(
                   out=stg[sgi][:, h2:n], in_=src[:, h2:n]),
                   writes=(sk + "h",), q="QS")

        def emit_mean_step():
            if mj_i[0] >= len(mean_jobs):
                return
            while ml_i[0] <= min(mj_i[0] + 1, len(mean_jobs) - 1):
                emit_mean_load()
            src, acol = mean_jobs[mj_i[0]]
            sgi = mj_i[0] % 2
            mj_i[0] += 1
            n = src.shape[-1]
            if n > 1024:
                bld.op("vector", lambda e, sgi=sgi, acol=acol, n=n: e.reduce_sum(
                       mean_acc[:, acol:acol + 1], stg[sgi][:, 0:n], axis=AX.X,
                       apply_absolute_value=True),
                       reads=(f"stg{sgi}", f"stg{sgi}h"), writes=(f"macc:{acol}",))
            else:
                bld.op("scalar", lambda e, sgi=sgi, acol=acol, n=n: e.activation(
                       sq[:, 0:n], stg[sgi][:, 0:n], AF.Abs,
                       accum_out=mean_acc[:, acol:acol + 1]),
                       reads=(f"stg{sgi}", f"stg{sgi}h"), writes=("sq", f"macc:{acol}"))

        def a_dma(t):
            xi = xin[t % 4]
            bld.op("sync", lambda e, t=t, xi=xi: e.dma_start(out=xi[:], in_=xk_t[t]),
                   writes=(f"xin{t % 4}",), q="QS")
        a_dma(0)
        for t in range(NKVT):
            emit_mean_step()
            if t + 1 < NKVT:
                a_dma(t + 1)
            ln_sum2(f"xin{t % 4}", xin[t % 4][:], t)
        ln_batch(0, NKVT)
        a_dma(0)
        for t in range(NKVT):
            emit_mean_step()
            if t + 1 < NKVT:
                a_dma(t + 1)
            ln_apply_t(f"xin{t % 4}", xin[t % 4][:], t, 0,
                       xlt_kv, f"xlt:{t}", t)

        if KPH < 2:
            raise _Stop()
        bld.op("gpsimd", lambda e: e.dma_start(
               out=Wk16[:].rearrange("p (c n) -> p c n", c=8), in_=WT_c["k"]),
               writes=("Wk16",), q="QG")
        bld.op("gpsimd", lambda e: e.dma_start(
               out=Wv16[:].rearrange("p (c n) -> p c n", c=8), in_=WT_c["v"]),
               writes=("Wv16",), q="QG")
        wk3 = Wk16[:].rearrange("p (c n) -> p c n", c=8)
        wv3 = Wv16[:].rearrange("p (c n) -> p c n", c=8)

        def nblocks(n):
            out, o = [], 0
            while o < n:
                w = min(512, n - o)
                out.append((o, w))
                o += w
            return out

        pg = [ps4, ps5]
        pgk = ["ps4", "ps5"]
        pi = [0]
        for c in range(8):
            for (o, w) in nblocks(KEPT_E[c]):
                emit_mean_step()
                bank = pi[0] % 2
                pi[0] += 1
                pk = pgk[bank]
                reads_xl = tuple(f"xlt:{ts}" for ts in range(o // 128, (o + w + 127) // 128))
                for dc in range(8):
                    tok = bld.op("tensor", lambda e, c=c, dc=dc, o=o, w=w, bank=bank:
                        e.matmul(pg[bank][:, 0:w], wk3[:, dc, c * 128:(c + 1) * 128],
                                 xlt_kv[:, dc, o:o + w], start=(dc == 0), stop=(dc == 7)),
                        reads=("Wk16",) + reads_xl + ((pk,) if dc == 0 else ()),
                        writes=(pk,) if dc == 0 else (), token=(dc == 7))
                bld.settok([pk], tok)
                bld.op("scalar", lambda e, c=c, o=o, w=w, bank=bank: e.activation(
                       KT[:, KOFF[c] + o:KOFF[c] + o + w], pg[bank][:, 0:w],
                       AF.Identity, bias=bk_sb[:, c:c + 1]),
                       reads=(pk, "bk_sb"), writes=(f"KT:{c}",))
        for t in range(NKVT):
            nh = NHT[t]
            if nh == 0:
                continue
            vt3 = VT[:, VOFF[t]:VOFF[t] + 65 * nh].rearrange("p (h x) -> p h x", x=65)
            bld.op("gpsimd", lambda e, vt3=vt3: e.memset(vt3[:, :, 64:65], 1.0),
                   writes=(f"vones:{t}",))
            for hb0 in range(0, nh, 8):
                hbn = min(8, nh - hb0)
                emit_mean_step()
                bank = pi[0] % 2
                pi[0] += 1
                pk = pgk[bank]
                for dc in range(8):
                    tok = bld.op("tensor", lambda e, t=t, dc=dc, hb0=hb0, hbn=hbn, bank=bank:
                        e.matmul(pg[bank][:, 0:64 * hbn], xlt_kv[:, dc, t * 128:(t + 1) * 128],
                                 wv3[:, dc, hb0 * 64:(hb0 + hbn) * 64],
                                 start=(dc == 0), stop=(dc == 7)),
                        reads=(f"xlt:{t}", "Wv16") + ((pk,) if dc == 0 else ()),
                        writes=(pk,) if dc == 0 else (), token=(dc == 7))
                bld.settok([pk], tok)
                bld.op("scalar", lambda e, t=t, vt3=vt3, hb0=hb0, hbn=hbn, bank=bank:
                       e.activation(vt3[:, hb0:hb0 + hbn, 0:64], pg[bank][:, 0:64 * hbn],
                                    AF.Identity),
                       reads=(pk,), writes=(f"VT:{t}",))

        def mean_final(col_lo, col_n, sc_s, sc_sw):
            kacc = tuple(f"macc:{a}" for a in range(col_lo, col_lo + col_n))
            bld.op("vector", lambda e: e.reduce_sum(cols[:, 9:10],
                   mean_acc[:, col_lo:col_lo + col_n], axis=AX.X),
                   reads=kacc, writes=("c_w",))
            bld.op("tensor", lambda e: e.matmul(psEa[0:1, 0:1], cols[:, 9:10],
                   onescol[:], start=True, stop=True),
                   reads=("c_w", "onescol"), writes=("psEa",))
            bld.op("vector", lambda e: e.tensor_scalar(cols[0:1, 10:11], psEa[0:1, 0:1],
                   float(2.0 ** -22), 1e-5, ALU.mult, ALU.max),
                   reads=("psEa",), writes=("c_mean1",))
            bld.op("vector", lambda e: e.reciprocal(cols[0:1, 11:12], cols[0:1, 10:11]),
                   reads=("c_mean1",), writes=("c_s1",))
            bld.op("tensor", lambda e: e.matmul(psEa[:, 0:2], onesrow[:],
                   cols[0:1, 10:12], start=True, stop=True),
                   reads=("c_mean1", "c_s1", "onesrow"), writes=("psEa",))
            bld.op("vector", lambda e, sc_sw=sc_sw: e.tensor_copy(
                   scal[:, sc_sw:sc_sw + 1], psEa[:, 0:1]),
                   reads=("psEa",), writes=(f"sc:{sc_sw}",))
            bld.op("vector", lambda e, sc_s=sc_s: e.tensor_copy(
                   scal[:, sc_s:sc_s + 1], psEa[:, 1:2]),
                   reads=("psEa",), writes=(f"sc:{sc_s}",))

        def a3_dma(t):
            xi = xin[t % 4]
            bld.op("sync", lambda e, t=t, xi=xi: e.dma_start(out=xi[:], in_=xo_t[t]),
                   writes=(f"xin{t % 4}",), q="QS")
        a3_dma(0)
        for t in range(NOT):
            if t + 1 < NOT:
                a3_dma(t + 1)
            ln_sum2(f"xin{t % 4}", xin[t % 4][:], NKVT + t)
        ln_batch(NKVT, NOT)
        while mj_i[0] < len(mean_jobs):
            emit_mean_step()
        mean_final(0, 16, SC_SG, SC_SWG)
        mean_final(16, 16, SC_SU, SC_SWU)
        mean_final(32, 32, SC_SD, SC_SWD)
        bld.op("vector", lambda e: e.tensor_scalar(scal[:, SC_SWD:SC_SWD + 1],
               scal[:, SC_SWD:SC_SWD + 1], 0.5, None, ALU.mult),
               reads=(f"sc:{SC_SWD}",), writes=(f"sc:{SC_SWD}",))

        if KPH < 3:
            raise _Stop()
        bld.barrier()
        xlt_own = XLT[:].rearrange("p (c n) -> p c n", c=8)
        a3_dma(0)
        for t in range(NOT):
            if t + 1 < NOT:
                a3_dma(t + 1)
            ln_apply_t(f"xin{t % 4}", xin[t % 4][:], NKVT + t, NKVT,
                       xlt_own, f"xlo:{t}", t)

        bld.op("gpsimd", lambda e: e.dma_start(
               out=Wk16[:].rearrange("p (c n) -> p c n", c=8), in_=WT_c["q"]),
               writes=("Wk16",), q="QG")
        bld.op("gpsimd", lambda e: e.dma_start(
               out=Wv16[:].rearrange("p (c n) -> p c n", c=8), in_=WT_c["o"]),
               writes=("Wv16",), q="QG")
        wq3, wo3 = wk3, wv3

        QTb = sb("QTb", [128, 8 * 1024], BF16)
        OT = sb("OT", [128, 8 * 1024], BF16)
        Pring = [sb(f"P{i}", [128, 1024], BF16) for i in range(3)]
        stg8 = [sb(f"stg8_{i}", [128, 2048], FP8) for i in range(2)]
        x1b = [sb(f"x1b{i}", [128, D], F32) for i in range(2)]
        park3 = [sb(f"pkp{i}", [128, 4, 256], F32) for i in range(2)]
        dstack = sb("dstack", [128, 1536], F32)
        den_pair = [dstack[0:4, 0:512], dstack[32:36, 0:512]]
        rec_pair = [dstack[0:4, 512:1024], dstack[0:4, 1024:1536]]
        denrow = [dstack[64:65, 0:512], dstack[96:97, 0:512]]
        selT = sb("selT", [128, 8], F32)
        sel_h = [sb(f"sel{i}", [4, 128], F32) for i in range(2)]
        bld.op("gpsimd", lambda e: e.memset(selT[:], 0.0), writes=("selT",))
        for half in range(2):
            bld.op("gpsimd", lambda e, half=half: e.memset(
                   selT[0:64, 4 * half + half:4 * half + half + 1], 1.0),
                   reads=("selT",), writes=("selT",))
            bld.op("gpsimd", lambda e, half=half: e.memset(
                   selT[64:128, 4 * half + 2 + half:4 * half + 3 + half], 1.0),
                   reads=("selT",), writes=("selT",))
        for half in range(2):
            bld.op("tensor", lambda e, half=half: e.transpose(
                   out=psEa[0:4, 0:128], in_=selT[:, 4 * half:4 * half + 4],
                   identity=ident32[:]), reads=("selT", "ident32"), writes=("psEa",))
            bld.op("vector", lambda e, half=half: e.tensor_copy(
                   sel_h[half][:], psEa[0:4, 0:128]),
                   reads=("psEa",), writes=(f"sel{half}",))

        tern_jobs = []
        for ci in range(16):
            c, hcol = ci // 2, ci % 2
            tern_jobs.append(("g", wgT_c[:, c, hcol * 2048:(hcol + 1) * 2048],
                              wg8d[:, ci * 2048:(ci + 1) * 2048], SC_SG))
        for ci in range(16):
            c, hcol = ci // 2, ci % 2
            tern_jobs.append(("u", wuT_c[:, c, hcol * 2048:(hcol + 1) * 2048],
                              wu8d[:, ci * 2048:(ci + 1) * 2048], SC_SU))
        for ci in range(32):
            tern_jobs.append(("d", wdT_c[:, ci, :], wd8d[:, ci * 1024:(ci + 1) * 1024], SC_SD))
        tj_i = [0]
        tl_i = [0]

        def emit_tern_load():
            if tl_i[0] >= len(tern_jobs):
                return
            _, src, _, _ = tern_jobs[tl_i[0]]
            sgi = tl_i[0] % 2
            tl_i[0] += 1
            n = src.shape[-1]
            sk = f"stg{sgi}"
            h2 = n // 2
            bld.op("sync", lambda e, src=src, sgi=sgi, h2=h2: e.dma_start(
                   out=stg[sgi][:, 0:h2], in_=src[:, 0:h2]),
                   writes=(sk,), q="QS")
            bld.op("sync", lambda e, src=src, sgi=sgi, h2=h2, n=n: e.dma_start(
                   out=stg[sgi][:, h2:n], in_=src[:, h2:n]),
                   writes=(sk + "h",), q="QS")

        def emit_tern_step():
            if tj_i[0] >= len(tern_jobs):
                return
            while tl_i[0] <= min(tj_i[0] + 1, len(tern_jobs) - 1):
                emit_tern_load()
            tag, src, dst, sc_s = tern_jobs[tj_i[0]]
            sgi = tj_i[0] % 2
            tj_i[0] += 1
            n = src.shape[-1]
            sk, s8k = f"stg{sgi}", f"stg8_{sgi}"
            s_ap = scal[:, sc_s:sc_s + 1]
            if tag != "d":
                bld.op("vector", lambda e, sgi=sgi, n=n, s_ap=s_ap: e.tensor_scalar(
                       stg[sgi][:, 0:n], stg[sgi][:, 0:n], s_ap, 1.49, ALU.mult, ALU.min),
                       reads=(sk, sk + "h", f"sc:{sc_s}"), writes=(sk,))
                bld.op("vector", lambda e, sgi=sgi, n=n: e.tensor_scalar(
                       stg[sgi][:, 0:n], stg[sgi][:, 0:n], -1.49, mcol[:], ALU.max, ALU.add),
                       reads=(sk, sk + "h", "mcol"), writes=(sk,))
                bld.op("scalar", lambda e, sgi=sgi, n=n: e.activation(
                       stg8[sgi][:, 0:n], stg[sgi][:, 0:n], AF.Identity, bias=negm[:]),
                       reads=(sk, sk + "h", "negm"), writes=(s8k,))
            else:
                bld.op("vector", lambda e, sgi=sgi, n=n, s_ap=s_ap: e.tensor_scalar(
                       stg[sgi][:, 0:n], stg[sgi][:, 0:n], s_ap, 1.49, ALU.mult, ALU.min),
                       reads=(sk, sk + "h", f"sc:{sc_s}"), writes=(sk,))
                bld.op("vector", lambda e, sgi=sgi, n=n: e.tensor_scalar(
                       stg[sgi][:, n:2 * n], stg[sgi][:, 0:n], -1.49, soft_lo[:],
                       ALU.max, ALU.mult), reads=(sk, sk + "h", "soft_lo"), writes=(sk,))
                bld.op("vector", lambda e, sgi=sgi, n=n: e.tensor_scalar(
                       stg[sgi][:, 0:n], stg[sgi][:, 0:n], -1.49, soft_hi[:],
                       ALU.max, ALU.mult), reads=(sk, sk + "h", "soft_hi"), writes=(sk,))
                bld.op("vector", lambda e, sgi=sgi, n=n: e.tensor_scalar(
                       stg[sgi][:, n:2 * n], stg[sgi][:, n:2 * n], mcol[:], None,
                       ALU.add), reads=(sk, sk + "h", "mcol"), writes=(sk,))
                bld.op("vector", lambda e, sgi=sgi, n=n: e.tensor_scalar(
                       stg[sgi][:, 0:n], stg[sgi][:, 0:n], mcol[:], None,
                       ALU.add), reads=(sk, sk + "h", "mcol"), writes=(sk,))
                bld.op("vector", lambda e, sgi=sgi, n=n: e.scalar_tensor_tensor(
                       stg8[sgi][:, 0:n], stg[sgi][:, n:2 * n], m2col[:],
                       stg[sgi][:, 0:n], ALU.subtract, ALU.add),
                       reads=(sk, sk + "h", "m2col"), writes=(s8k,))
            bld.op("sync", lambda e, dst=dst, sgi=sgi, n=n: e.dma_start(
                   out=dst, in_=stg8[sgi][:, 0:n]), reads=(s8k,),
                   writes=(f"w8d:{tag}",), q="QS")

        def o_ap(h, half):
            if h % 2 == 0:
                return (pg[half], pgk[half], pg[half][:], 0)
            return (psW1, "psW1a" if half == 0 else "psW1b",
                    psW1[:, half * 512:(half + 1) * 512], half * 512)

        def emit_evac(qp, h):
            c, off = h // 2, 64 * (h % 2)
            pr = c % 2
            pp, ppk = park3[pr], f"pp{pr}"
            for half in range(2):
                _, opk, oview, _ = o_ap(h, half)
                bld.op("vector", lambda e, oview=oview, pp=pp, off=off, half=half:
                       e.tensor_copy(pp[off:off + 64, half * 2:half * 2 + 2, :],
                                     oview[0:64, :].rearrange("p (a n) -> p a n", a=2)),
                       reads=(opk,), writes=(ppk,))
                dr = half
                bld.op("vector", lambda e, oview=oview, dr=dr: e.tensor_copy(
                       denrow[dr], oview[64:65, :]),
                       reads=(opk,), writes=(f"dnr{dr}",))
                r = 2 * (h % 2) + half
                bld.op("sync", lambda e, dr=dr, pr=pr, r=r: e.dma_start(
                       out=den_pair[pr][r:r + 1, :], in_=denrow[dr]),
                       reads=(f"dnr{dr}",), writes=(f"dp{pr}:{r}",), q="QS")

        def emit_recip(c):
            pr = c % 2
            bld.op("vector", lambda e, pr=pr: e.reciprocal(rec_pair[pr],
                   den_pair[pr]),
                   reads=tuple(f"dp{pr}:{r}" for r in range(4)), writes=(f"rp{pr}",))

        def emit_div(c, half, hnow):
            pr = c % 2
            pp, ppk = park3[pr], f"pp{pr}"
            if hnow % 2 == 0:
                rbv, rbk = psW1[:, 0:512], "psW1a"
            else:
                rbv, rbk = ps4[:], "ps4"
            bld.op("tensor", lambda e, pr=pr, half=half, rbv=rbv: e.matmul(
                   rbv, sel_h[half][:], rec_pair[pr], start=True, stop=True),
                   reads=(f"sel{half}", f"rp{pr}", rbk), writes=(rbk,))
            rb3 = rbv.rearrange("p (a n) -> p a n", a=2)
            bld.op("vector", lambda e, pp=pp, half=half, rb3=rb3: e.tensor_mul(
                   pp[:, half * 2:half * 2 + 2, :],
                   pp[:, half * 2:half * 2 + 2, :], rb3),
                   reads=(ppk, rbk), writes=(ppk,))
            bld.op("vector", lambda e, pp=pp, c=c, half=half: e.tensor_scalar(
                   OT[:, c * 1024 + half * 512:c * 1024 + (half + 1) * 512]
                       .rearrange("p (a n) -> p a n", a=2),
                   pp[:, half * 2:half * 2 + 2, :], bv_sb[:, c:c + 1], None, ALU.add),
                   reads=(ppk, "bv_sb"), writes=(f"OT:{c}",))

        for qp in range(2):
            for c in range(8):
                pw = psW0 if c % 2 == 0 else psW1
                pwk = ("psW0a", "psW0b") if c % 2 == 0 else ("psW1a", "psW1b")
                for half in range(2):
                    for dc in range(8):
                        tok = bld.op("tensor", lambda e, c=c, dc=dc, qp=qp, half=half, pw=pw:
                            e.matmul(pw[:, half * 512:(half + 1) * 512],
                                     wq3[:, dc, c * 128:(c + 1) * 128],
                                     xlt_own[:, dc, qp * 1024 + half * 512:qp * 1024 + (half + 1) * 512],
                                     start=(dc == 0), stop=(dc == 7)),
                            reads=("Wk16",) + tuple(f"xlo:{ts}" for ts in
                                   range(qp * 8 + half * 4, qp * 8 + half * 4 + 4)) +
                                   ((pwk[half],) if dc == 0 else ()),
                            writes=(pwk[half],) if dc == 0 else (), token=(dc == 7))
                    bld.settok([pwk[half]], tok)
                bld.op("scalar", lambda e, c=c, pw=pw: e.activation(
                       QTb[:, c * 1024:(c + 1) * 1024], pw[:], AF.Identity,
                       bias=bq_sb[:, c:c + 1]),
                       reads=(pwk[0], pwk[1], "bq_sb"), writes=(f"QTb:{c}",))
                emit_tern_step()
            for h in range(H):
                emit_tern_step()
                emit_tern_step()
                c, off = h // 2, 64 * (h % 2)
                nkb = NKB[h]

                def emit_S(kb, c=c, off=off, h=h):
                    if kb % 2 == 0:
                        sbv, sk0, sk1 = psW0, "psW0a", "psW0b"
                    else:
                        sbv, sk0, sk1 = psE, "psEa", "psEb"
                    for half in range(2):
                        tok = bld.op("tensor", lambda e, c=c, off=off, kb=kb, half=half, sbv=sbv:
                            e.matmul(sbv[:, half * 512:(half + 1) * 512],
                                     KT[off:off + 64, KOFF[c] + kb * 128:KOFF[c] + (kb + 1) * 128],
                                     QTb[off:off + 64, c * 1024 + half * 512:c * 1024 + (half + 1) * 512],
                                     start=True, stop=True),
                            reads=(f"KT:{c}", f"QTb:{c}", sk0 if half == 0 else sk1),
                            writes=(sk0 if half == 0 else sk1,), token=(half == 1))
                    bld.settok([sk0, sk1], tok)
                    pidx = kb % 3
                    bld.op("scalar", lambda e, kb=kb, h=h, pidx=pidx, sbv=sbv: e.activation(
                           Pring[pidx][:], sbv[:], AF.Exp, scale=0.125,
                           bias=bias_tbl[:, BOFF[h] + kb:BOFF[h] + kb + 1]),
                           reads=(sk0, sk1, f"bias:{h}"), writes=(f"P:{pidx}",))

                def emit_PV(kb, h=h, nkb=nkb):
                    pidx = kb % 3
                    for half in range(2):
                        _, opk, oview, _ = o_ap(h, half)
                        vslice = VT[:, VOFF[kb] + h * 65:VOFF[kb] + (h + 1) * 65]
                        tok = bld.op("tensor", lambda e, vslice=vslice, kb=kb, half=half,
                            oview=oview, pidx=pidx, nkb=nkb:
                            e.matmul(oview[0:65, :], vslice,
                                     Pring[pidx][:, half * 512:(half + 1) * 512],
                                     start=(kb == 0), stop=(kb == nkb - 1)),
                            reads=(f"VT:{kb}", f"vones:{kb}", f"P:{pidx}") +
                                  ((opk,) if kb == 0 else ()),
                            writes=(opk,) if kb == 0 else (), token=(kb == nkb - 1))
                        if kb == nkb - 1:
                            bld.settok([opk], tok)

                emit_S(0)
                if nkb > 1:
                    emit_S(1)
                for kb in range(nkb):
                    if kb + 2 < nkb:
                        emit_S(kb + 2)
                    emit_PV(kb)
                if h >= 1:
                    emit_evac(qp, h - 1)
                if h >= 3:
                    emit_div((h - 3) // 2, (h - 3) % 2, h)
                if h >= 2 and h % 2 == 0:
                    emit_recip(h // 2 - 1)
            emit_evac(qp, H - 1)
            emit_recip(H // 2 - 1)
            emit_div(6, 1, 16)
            emit_div(7, 0, 17)
            emit_div(7, 1, 18)
            for tt in range(8):
                gt = qp * 8 + tt
                xi = xin[gt % 4]
                xik = f"xin{gt % 4}"
                bld.op("sync", lambda e, gt=gt, xi=xi: e.dma_start(out=xi[:], in_=xo_t[gt]),
                       writes=(xik,), q="QS")
                for half in range(2):
                    bank = half
                    pk = pgk[bank]
                    for ac in range(8):
                        bld.op("tensor", lambda e, ac=ac, tt=tt, half=half, bank=bank:
                            e.matmul(pg[bank][:], OT[:, ac * 1024 + tt * 128:ac * 1024 + (tt + 1) * 128],
                                     wo3[:, ac, half * 512:(half + 1) * 512],
                                     start=(ac == 0), stop=False),
                            reads=(f"OT:{ac}", "Wv16") + ((pk,) if ac == 0 else ()),
                            writes=(pk,) if ac == 0 else (), token=False)
                    tok = bld.op("tensor", lambda e, half=half, bank=bank:
                        e.matmul(pg[bank][:], onesrow16[:], bo16[:, half * 512:(half + 1) * 512],
                                 start=False, stop=True),
                        reads=("onesrow16", "bo16"), writes=(), token=True)
                    bld.settok([pk], tok)
                x1k = f"x1b{tt % 2}"
                for half in range(2):
                    bld.op("vector", lambda e, tt=tt, half=half, xi=xi: e.tensor_add(
                           x1b[tt % 2][:, half * 512:(half + 1) * 512],
                           pg[half][:], xi[:, half * 512:(half + 1) * 512]),
                           reads=(pgk[half], xik), writes=(x1k,))
                ln_stats(x1k, x1b[tt % 2][:], gt, LN2C, f"ln2c:{gt}")
                bld.op("sync", lambda e, gt=gt, tt=tt: e.dma_start(
                       out=x1_t[gt], in_=x1b[tt % 2][:]), reads=(x1k,),
                       writes=(f"x1scr:{gt}",), q="QS")
                emit_tern_step()
        while tj_i[0] < len(tern_jobs):
            emit_tern_step()

        if KPH < 4:
            raise _Stop()
        bld.barrier()
        sb_free_back_to(XLT)
        if os.environ.get("KDBG"):
            print("alive after free:", [hh.name if hasattr(hh,'name') else str(hh) for cm,hh in alive])
        wg8 = sb("wg8", [128, 8 * 4096], FP8)
        wu8 = sb("wu8", [128, 8 * 4096], FP8)
        wd8 = sb("wd8", [128, 32 * 1024], FP8)
        gu = sb("gu", [128, 32 * TG], F32)
        hT = sb("hT", [128, 32 * TG], BF16)
        xqT = [sb(f"xqT{i}", [128, 8 * TG], BF16) for i in range(2)]
        x1in = [sb(f"x1in{i}", [128, D], F32) for i in range(6)]
        xhat = sb("xhat", [128, D], F32)
        gsil = [sb(f"gsil{i}", [128, TG], F32) for i in range(2)]
        s2b = sb("s2b", [128, 2 * TG], F32)
        s2row = sb("s2row", [1, 2 * TG], F32)
        Mx = sb("Mx", [128, TG], F32)
        gabs = [sb(f"gabs{i}", [128, TG], F32) for i in range(2)]
        outb = [sb(f"outb{i}", [128, D], F32) for i in range(2)]
        dcols = sb("dcols", [128, 12], F32)

        wg83 = wg8[:].rearrange("p (c n) -> p c n", c=8)
        wu83 = wu8[:].rearrange("p (c n) -> p c n", c=8)
        wd83 = wd8[:].rearrange("p (c n) -> p c n", c=32)

        gbanks = [(ps4[:], "ps4"), (ps5[:], "ps5"), (psEa, "psEa"),
                  (psW0[:, 0:512], "psW0a")]
        dbanks = [(psW0[:, 512:1024], "psW0b"), (psW1[:, 0:512], "psW1a")]

        def emit_lnxq(tg):
            for ti in range(2):
                gt = tg * 2 + ti
                xk = f"x1in{gt % 6}"
                bld.op("sync", lambda e, gt=gt: e.dma_start(out=x1in[gt % 6][:], in_=x1_t[gt]),
                       reads=(f"x1scr:{gt}",), writes=(xk,), q="QS")
                bld.op("vector", lambda e, gt=gt: e.tensor_scalar(xhat[:], x1in[gt % 6][:],
                       LN2C[:, 2 * gt:2 * gt + 1], LN2C[:, 2 * gt + 1:2 * gt + 2],
                       ALU.mult, ALU.add),
                       reads=(xk, f"ln2c:{gt}:r", f"ln2c:{gt}:m"), writes=("xhat",))
                bld.op("vector", lambda e: e.reduce_max(dcols[:, 0:1], xhat[:], axis=AX.X,
                       apply_absolute_value=True), reads=("xhat",), writes=("d_m1",))
                bld.op("vector", lambda e: e.tensor_scalar(dcols[:, 0:1], dcols[:, 0:1],
                       1e-5, None, ALU.max), reads=("d_m1",), writes=("d_m1",))
                bld.op("vector", lambda e: e.reciprocal(dcols[:, 1:2], dcols[:, 0:1]),
                       reads=("d_m1",), writes=("d_rm1",))
                bld.op("vector", lambda e: e.tensor_scalar(dcols[:, 2:3], dcols[:, 1:2],
                       127.0, None, ALU.mult), reads=("d_rm1",), writes=("d_s1",))
                bld.op("vector", lambda e: e.tensor_scalar(dcols[:, 3:4], dcols[:, 0:1],
                       float(1.0 / 127.0), None, ALU.mult), reads=("d_m1",), writes=("d_sv1",))
                bld.op("vector", lambda e: e.tensor_scalar(xhat[:], xhat[:],
                       dcols[:, 2:3], mcol[:], ALU.mult, ALU.add),
                       reads=("xhat", "d_s1", "mcol"), writes=("xhat",))
                bld.op("vector", lambda e: e.tensor_scalar(xhat[:], xhat[:],
                       mcol[:], dcols[:, 3:4], ALU.subtract, ALU.mult),
                       reads=("xhat", "d_sv1", "mcol"), writes=("xhat",))
                xq3 = xqT[tg % 2][:].rearrange("p (c n) -> p c n", c=8)
                for rnd in range(2):
                    for j in range(4):
                        jj = rnd * 4 + j
                        tok = bld.op("tensor", lambda e, j=j, jj=jj: e.transpose(
                            out=psEb[:, j * 128:(j + 1) * 128],
                            in_=xhat[:, jj * 128:(jj + 1) * 128],
                            identity=ident32[:]),
                            reads=("xhat", "ident32") + (("psEb",) if j == 0 else ()),
                            writes=("psEb",) if j == 0 else (), token=(j == 3))
                    bld.settok(["psEb"], tok)
                    bld.op("scalar", lambda e, ti=ti, rnd=rnd, xq3=xq3: e.activation(
                           xq3[:, rnd * 4:(rnd + 1) * 4, ti * 128:(ti + 1) * 128],
                           psEb[:].rearrange("p (c n) -> p c n", c=4), AF.Identity),
                           reads=("psEb",), writes=(f"xqT{tg % 2}",))

        def emit_gateup(tg):
            xq3 = xqT[tg % 2][:].rearrange("p (c n) -> p c n", c=8)
            xqk = f"xqT{tg % 2}"
            for ic in range(32):
                if ic % 2 == 0:
                    drain_hT(2)
                bview, pk = gbanks[ic % 4]
                for dc in range(8):
                    bld.op("tensor", lambda e, ic=ic, dc=dc, bview=bview, xq3=xq3: e.matmul(
                           bview[:, 0:TG], wg83[:, dc, ic * 128:(ic + 1) * 128],
                           xq3[:, dc, :], start=(dc == 0), stop=(dc == 7)),
                           reads=("wg8", xqk) + ((pk,) if dc == 0 else ()),
                           writes=(pk,) if dc == 0 else (), token=False)
                for dc in range(8):
                    tok = bld.op("tensor", lambda e, ic=ic, dc=dc, bview=bview, xq3=xq3: e.matmul(
                           bview[:, TG:2 * TG], wu83[:, dc, ic * 128:(ic + 1) * 128],
                           xq3[:, dc, :], start=(dc == 0), stop=(dc == 7)),
                           reads=("wu8", xqk), writes=(), token=(dc == 7))
                bld.settok([pk], tok)
                gk = f"gsil{ic % 2}"
                bld.op("scalar", lambda e, ic=ic, bview=bview: e.activation(
                       gsil[ic % 2][:], bview[:, 0:TG], AF.Silu,
                       scale=scal[:, SC_SWG:SC_SWG + 1]),
                       reads=(pk, f"sc:{SC_SWG}"), writes=(gk,))
                bld.op("vector", lambda e, ic=ic, bview=bview: e.scalar_tensor_tensor(
                       gu[:, ic * TG:(ic + 1) * TG], bview[:, TG:2 * TG],
                       scal[:, SC_SWU:SC_SWU + 1], gsil[ic % 2][:], ALU.mult, ALU.mult),
                       reads=(pk, f"sc:{SC_SWU}", gk), writes=(f"gu:{ic}",))
                if ic == 0:
                    bld.op("scalar", lambda e: e.activation(Mx[:], gu[:, 0:TG], AF.Abs),
                           reads=("gu:0",), writes=("Mx",))
                else:
                    gak = f"gabs{ic % 2}"
                    bld.op("scalar", lambda e, ic=ic: e.activation(
                           gabs[ic % 2][:], gu[:, ic * TG:(ic + 1) * TG], AF.Abs),
                           reads=(f"gu:{ic}",), writes=(gak,))
                    bld.op("vector", lambda e, ic=ic: e.tensor_max(Mx[:], Mx[:],
                           gabs[ic % 2][:]), reads=(gak, "Mx"), writes=("Mx",))

        def emit_quant_h(tg):
            for ti in range(2):
                tok = bld.op("tensor", lambda e, ti=ti: e.transpose(
                       out=psW1[:, 512 + ti * 128:512 + (ti + 1) * 128],
                       in_=Mx[:, ti * 128:(ti + 1) * 128], identity=ident32[:]),
                       reads=("Mx", "ident32") + (("psW1b",) if ti == 0 else ()),
                       writes=("psW1b",) if ti == 0 else (), token=True)
            bld.settok(["psW1b"], tok)
            for ti in range(2):
                bld.op("vector", lambda e, ti=ti: e.reduce_max(dcols[:, 4 + ti:5 + ti],
                       psW1[:, 512 + ti * 128:512 + (ti + 1) * 128], axis=AX.X),
                       reads=("psW1b",), writes=(f"d_m2:{ti}",))
                bld.op("vector", lambda e, ti=ti: e.tensor_scalar(dcols[:, 4 + ti:5 + ti],
                       dcols[:, 4 + ti:5 + ti], 1e-5, None, ALU.max),
                       reads=(f"d_m2:{ti}",), writes=(f"d_m2:{ti}",))
                bld.op("vector", lambda e, ti=ti: e.reciprocal(dcols[:, 8 + ti:9 + ti],
                       dcols[:, 4 + ti:5 + ti]), reads=(f"d_m2:{ti}",), writes=(f"d_rm2:{ti}",))
                bld.op("vector", lambda e, ti=ti: e.tensor_scalar(dcols[:, 10 + ti:11 + ti],
                       dcols[:, 8 + ti:9 + ti], 127.0, None, ALU.mult),
                       reads=(f"d_rm2:{ti}",), writes=(f"d_s2:{ti}",))
                bld.op("vector", lambda e, ti=ti: e.scalar_tensor_tensor(
                       dcols[:, 6 + ti:7 + ti], dcols[:, 4 + ti:5 + ti],
                       float(1.0 / 127.0), scal[:, SC_SWD:SC_SWD + 1], ALU.mult, ALU.mult),
                       reads=(f"d_m2:{ti}", f"sc:{SC_SWD}"), writes=(f"dcmb:{ti}",))
            for ti in range(2):
                tok = bld.op("tensor", lambda e, ti=ti: e.transpose(
                       out=psW1[0:1, 512 + 256 + ti * 128:512 + 256 + (ti + 1) * 128],
                       in_=dcols[:, 10 + ti:11 + ti], identity=ident32[:]),
                       reads=(f"d_s2:{ti}", "ident32") + (("psW1b",) if ti == 0 else ()),
                       writes=("psW1b",) if ti == 0 else (), token=True)
            bld.settok(["psW1b"], tok)
            bld.op("vector", lambda e: e.tensor_copy(s2row[0:1, 0:TG], psW1[0:1, 768:768 + TG]),
                   reads=("psW1b",), writes=("s2row",))
            bld.op("vector", lambda e: e.tensor_copy(s2row[0:1, TG:2 * TG], s2row[0:1, 0:TG]),
                   reads=("s2row",), writes=("s2row",))
            bld.op("tensor", lambda e: e.matmul(psW1[:, 512:1024], onesrow[:],
                   s2row[:], start=True, stop=True),
                   reads=("s2row", "onesrow"), writes=("psW1b",))
            bld.op("scalar", lambda e: e.activation(s2b[:], psW1[:, 512:1024],
                   AF.Identity), reads=("psW1b",), writes=("s2b",))

        pending_hT = []

        def emit_quant_hT(tg):
            for ic in range(0, 32, 2):
                def th1(ic=ic):
                    bld.op("vector", lambda e, ic=ic: e.tensor_mul(
                           gu[:, ic * TG:(ic + 2) * TG],
                           gu[:, ic * TG:(ic + 2) * TG], s2b[:]),
                           reads=(f"gu:{ic}", f"gu:{ic + 1}", "s2b"),
                           writes=(f"gu:{ic}", f"gu:{ic + 1}"))
                def th2(ic=ic):
                    bld.op("vector", lambda e, ic=ic: e.tensor_scalar(
                           hT[:, ic * TG:(ic + 2) * TG], gu[:, ic * TG:(ic + 2) * TG],
                           mcol[:], mcol[:], ALU.add, ALU.subtract),
                           reads=(f"gu:{ic}", f"gu:{ic + 1}", "mcol"), writes=("hT",))
                pending_hT.append(th1)
                pending_hT.append(th2)

        def drain_hT(k=1):
            for _ in range(k):
                if pending_hT:
                    pending_hT.pop(0)()

        def emit_down(tg):
            for ti in range(2):
                gt = tg * 2 + ti
                xk = f"x1in{gt % 6}"
                ob = outb[gt % 2]
                obk = f"outb{gt % 2}"
                for half in range(2):
                    bview, pk = dbanks[half]
                    for ic in range(32):
                        tok = bld.op("tensor", lambda e, ic=ic, ti=ti, half=half, bview=bview:
                            e.matmul(bview[:],
                                     hT[:, ic * TG + ti * 128:ic * TG + (ti + 1) * 128],
                                     wd83[:, ic, half * 512:(half + 1) * 512],
                                     start=(ic == 0), stop=(ic == 31)),
                            reads=("hT", "wd8") + ((pk,) if ic == 0 else ()),
                            writes=(pk,) if ic == 0 else (), token=(ic == 31))
                    bld.settok([pk], tok)
                    bld.op("vector", lambda e, ti=ti, half=half, bview=bview, ob=ob, gt=gt:
                        e.scalar_tensor_tensor(ob[:, half * 512:(half + 1) * 512],
                            bview[:], dcols[:, 6 + ti:7 + ti],
                            x1in[gt % 6][:, half * 512:(half + 1) * 512],
                            ALU.mult, ALU.add),
                        reads=(pk, f"dcmb:{ti}", xk), writes=(obk,))
                bld.op("sync", lambda e, gt=gt, ob=ob: e.dma_start(out=y_t[gt], in_=ob[:]),
                       reads=(obk,), q="QS")

        emit_lnxq(0)
        emit_lnxq(1)
        bld.op("sync", lambda e: e.dma_start(out=wg8[:], in_=wg8d),
               reads=("w8d:g",), writes=("wg8",), q="QS")
        bld.op("sync", lambda e: e.dma_start(out=wu8[:], in_=wu8d),
               reads=("w8d:u",), writes=("wu8",), q="QS")
        bld.op("sync", lambda e: e.dma_start(out=wd8[:], in_=wd8d),
               reads=("w8d:d",), writes=("wd8",), q="QS")
        for tg in range(NTG):
            emit_gateup(tg)
            if tg > 0:
                emit_down(tg - 1)
            emit_quant_h(tg)
            if tg + 2 < NTG:
                emit_lnxq(tg + 2)
            emit_quant_hT(tg)
        drain_hT(32)
        emit_down(NTG - 1)
    except _Stop:
        pass

    for eng in ("tensor", "vector", "scalar", "gpsimd"):
        bld.op(eng, lambda e: e.nop(), token=True)
    for eng in ("tensor", "scalar"):
        sem = bld.semof[eng]
        v = bld.cnt[sem]
        bld.plan[eng].append(lambda e, sem=sem, v=v: e.wait_ge(bld.sems[sem], v))
    for i in range(8):
        v = bld.cnt[f"QS{i}"]
        if v:
            bld.plan["sync"].append(lambda e, i=i, v=v: e.wait_ge(bld.sems[f"QS{i}"], v))
    for i in range(2):
        v = bld.cnt[f"QG{i}"]
        if v:
            bld.plan["sync"].append(lambda e, i=i, v=v: e.wait_ge(bld.sems[f"QG{i}"], v))

    import contextlib
    with contextlib.ExitStack() as stk:
        sems = {}
        for nm in list(bld.cnt.keys()):
            sems[nm] = stk.enter_context(nc.semaphore("s" + nm))
        blk = stk.enter_context(nc.Block())
        bld.sems = sems

        @blk.sync
        def _(e):
            for f in bld.plan["sync"]:
                f(e)

        @blk.tensor
        def _(e):
            for f in bld.plan["tensor"]:
                f(e)

        @blk.vector
        def _(e):
            for f in bld.plan["vector"]:
                f(e)

        @blk.scalar
        def _(e):
            for f in bld.plan["scalar"]:
                f(e)

        @blk.gpsimd
        def _(e):
            for f in bld.plan["gpsimd"]:
                f(e)

    while alive:
        cm, h = alive.pop()
        cm.__exit__(None, None, None)
    for cm in reversed(psum_cm):
        cm.__exit__(None, None, None)
    return nc


_NC = None


def kernel(L0_attn_x, L0_attn_Wq, L0_attn_bq, L0_attn_Wk, L0_attn_bk,
           L0_attn_Wv, L0_attn_bv, L0_attn_Wo, L0_attn_bo,
           L0_ffn_w_gate, L0_ffn_w_up, L0_ffn_w_down):
    global _NC
    if _NC is None:
        _NC = _build()
    nc = _NC
    f32 = np.float32
    x = np.ascontiguousarray(L0_attn_x, dtype=f32)
    WqT = np.ascontiguousarray(np.asarray(L0_attn_Wq, f32).T)
    WkT = np.ascontiguousarray(np.asarray(L0_attn_Wk, f32).T)
    WvT = np.ascontiguousarray(np.asarray(L0_attn_Wv, f32).T)
    WoT = np.ascontiguousarray(np.asarray(L0_attn_Wo, f32).T)
    bq_r = np.ascontiguousarray(np.asarray(L0_attn_bq, f32).reshape(8, 128).T)
    bk_r = np.ascontiguousarray(np.asarray(L0_attn_bk, f32).reshape(8, 128).T)
    bv_r = np.ascontiguousarray(np.asarray(L0_attn_bv, f32).reshape(8, 128).T)
    bo_r = np.ascontiguousarray(np.asarray(L0_attn_bo, f32).reshape(1, D))
    wgT = np.ascontiguousarray(np.asarray(L0_ffn_w_gate, f32).T)
    wuT = np.ascontiguousarray(np.asarray(L0_ffn_w_up, f32).T)
    wdT = np.ascontiguousarray(np.asarray(L0_ffn_w_down, f32).T)

    in_maps = []
    for core in range(NCORE):
        b, par = core // 2, core % 2
        blocks = [par + 2 * i for i in range(NQB)]
        x_own = np.ascontiguousarray(
            np.concatenate([x[b, qb * 512:(qb + 1) * 512] for qb in blocks], axis=0))
        x_kv = np.ascontiguousarray(x[b, :KVT])
        in_maps.append({
            "x_own": x_own, "x_kv": x_kv,
            "WqT": WqT, "WkT": WkT, "WvT": WvT, "WoT": WoT,
            "bq_r": bq_r, "bk_r": bk_r, "bv_r": bv_r, "bo_r": bo_r,
            "wgT": wgT, "wuT": wuT, "wdT": wdT,
        })
    res = run_bass_kernel_spmd(nc, in_maps, core_ids=list(range(NCORE)))
    out = np.empty((B, L, D), dtype=f32)
    for core in range(NCORE):
        b, par = core // 2, core % 2
        yv = res.results[core]["y"]
        for i in range(NQB):
            qb = par + 2 * i
            out[b, qb * 512:(qb + 1) * 512] = yv[i * 512:(i + 1) * 512]
    return out


# version 26
# speedup vs baseline: 1.0254x; 1.0027x over previous
import os
import numpy as np
import ml_dtypes
import concourse.bass as bass
import concourse.mybir as mybir
from concourse.bass_utils import run_bass_kernel_spmd

F32 = mybir.dt.float32
BF16 = mybir.dt.bfloat16
FP8 = mybir.dt.float8e4
AF = mybir.ActivationFunctionType
ALU = mybir.AluOpType
AX = mybir.AxisListType

B, L, D, H, HD, INNER = 4, 4096, 1024, 16, 64, 4096
NCORE = 8
NOWN = 2048
TG = 256
NTG = NOWN // TG
NQB = 4

_EXP = [(-4.0 + h * (3.7 / 15.0)) for h in range(H)]
SLOPES64 = [float(2.0 ** e) for e in _EXP]
KEPT = [min(int(np.ceil(130.0 / s / 128.0)) * 128, L) for s in SLOPES64]
KVT = max(KEPT)
NKVT = KVT // 128
NOT = NOWN // 128
NKB = [k // 128 for k in KEPT]
BOFF = np.concatenate([[0], np.cumsum(NKB)]).astype(int)
TOTKB = int(BOFF[-1])
KEPT_E = [KEPT[2 * c] for c in range(8)]
KOFF = np.concatenate([[0], np.cumsum(KEPT_E)]).astype(int)
KCOLS = int(KOFF[-1])
NHT = [sum(1 for h in range(H) if KEPT[h] > 128 * t) for t in range(NKVT)]
VOFF = np.concatenate([[0], np.cumsum([65 * n for n in NHT])]).astype(int)
VCOLS = int(VOFF[-1])

TINY = float(2.0 ** -26)
MAGIC = float(1.5 * 2 ** 23)
MSCALE = float(1.5 * 2 ** 78)
EPS_SOFT = 1e-5
LN2_64 = float(np.log(np.float64(2.0)))


def _dual(c64):
    c32 = float(np.float32(c64))
    corr = float(np.float32((c64 - c32) / TINY))
    return c32, corr


def _build():
    nc = bass.Bass("TRN2", target_bir_lowering=False)
    x_own = nc.dram_tensor("x_own", [NOWN, D], F32, kind="ExternalInput").ap()
    x_kv = nc.dram_tensor("x_kv", [KVT, D], F32, kind="ExternalInput").ap()
    WqT = nc.dram_tensor("WqT", [D, D], F32, kind="ExternalInput").ap()
    WkT = nc.dram_tensor("WkT", [D, D], F32, kind="ExternalInput").ap()
    WvT = nc.dram_tensor("WvT", [D, D], F32, kind="ExternalInput").ap()
    WoT = nc.dram_tensor("WoT", [D, D], F32, kind="ExternalInput").ap()
    bq_r = nc.dram_tensor("bq_r", [128, 8], F32, kind="ExternalInput").ap()
    bk_r = nc.dram_tensor("bk_r", [128, 8], F32, kind="ExternalInput").ap()
    bv_r = nc.dram_tensor("bv_r", [128, 8], F32, kind="ExternalInput").ap()
    bo_r = nc.dram_tensor("bo_r", [1, D], F32, kind="ExternalInput").ap()
    wgT = nc.dram_tensor("wgT", [D, INNER], F32, kind="ExternalInput").ap()
    wuT = nc.dram_tensor("wuT", [D, INNER], F32, kind="ExternalInput").ap()
    wdT = nc.dram_tensor("wdT", [INNER, D], F32, kind="ExternalInput").ap()
    y = nc.dram_tensor("y", [NOWN, D], F32, kind="ExternalOutput").ap()
    x1scr = nc.dram_tensor("x1scr", [NOWN, D], F32, kind="Internal").ap()
    wg8d = nc.dram_tensor("wg8d", [128, 8 * 4096], FP8, kind="Internal").ap()
    wu8d = nc.dram_tensor("wu8d", [128, 8 * 4096], FP8, kind="Internal").ap()
    wd8d = nc.dram_tensor("wd8d", [128, 32 * 1024], FP8, kind="Internal").ap()

    xo_t = x_own.rearrange("(t p) d -> t p d", p=128)
    xk_t = x_kv.rearrange("(t p) d -> t p d", p=128)
    y_t = y.rearrange("(t p) d -> t p d", p=128)
    x1_t = x1scr.rearrange("(t p) d -> t p d", p=128)
    wgT_c = wgT.rearrange("(c p) n -> p c n", p=128)
    wuT_c = wuT.rearrange("(c p) n -> p c n", p=128)
    wdT_c = wdT.rearrange("(c p) n -> p c n", p=128)
    WT_c = {"q": WqT.rearrange("(c p) n -> p c n", p=128),
            "k": WkT.rearrange("(c p) n -> p c n", p=128),
            "v": WvT.rearrange("(c p) n -> p c n", p=128),
            "o": WoT.rearrange("(c p) n -> p c n", p=128)}

    class Bld:
        def __init__(self):
            self.plan = {"sync": [], "tensor": [], "vector": [], "scalar": [], "gpsimd": []}
            self.cnt = {"PE": 0, "DVE": 0, "ACT": 0, "POOL": 0}
            for i in range(8):
                self.cnt[f"QS{i}"] = 0
            for i in range(2):
                self.cnt[f"QG{i}"] = 0
            self.rr = {"QS": 0, "QG": 0}
            self.semof = {"tensor": "PE", "vector": "DVE", "scalar": "ACT", "gpsimd": "POOL"}
            self.waited = {}
            self.lastw = {}
            self.readers = {}
            self.floor = {}
            self.sems = None
            self.wsince = {}
            self.lasttok = {}

        def _wait(self, eng, sem, val):
            if val <= 0:
                return
            if self.waited.get((eng, sem), 0) >= val:
                return
            self.waited[(eng, sem)] = val
            self.wsince[eng] = True
            def f(e, sem=sem, val=val):
                e.wait_ge(self.sems[sem], val)
            self.plan[eng].append(f)

        def barrier(self):
            for s, v in self.cnt.items():
                self.floor[s] = v

        def dep(self, eng, reads=(), war=()):
            need = dict(self.floor)
            for k in reads:
                t = self.lastw.get(k)
                if t is not None:
                    assert t != "PENDING", f"read of in-flight group buffer {k}"
                    need[t[0]] = max(need.get(t[0], 0), t[1])
            for k in war:
                t = self.lastw.get(k)
                if t is not None and t != "PENDING":
                    need[t[0]] = max(need.get(t[0], 0), t[1])
                for (s, v) in self.readers.get(k, ()):
                    need[s] = max(need.get(s, 0), v)
            for s, v in need.items():
                self._wait(eng, s, v)

        def op(self, eng, fn, reads=(), writes=(), token=True, q=None):
            self.dep(eng, reads, war=writes)
            if q:
                nl = 8 if q == "QS" else 2
                lane = self.rr[q]
                self.rr[q] = (lane + 1) % nl
                sem = f"{q}{lane}"
                self._wait(eng, sem, self.cnt[sem])
            else:
                sem = self.semof[eng]
            if token:
                inc = 16 if q else 1
                if q is None and eng in ("tensor", "scalar"):
                    if not self.wsince.get(eng, True):
                        pv = self.lasttok[eng]
                        if self.waited.get((eng, sem), 0) < pv:
                            self.waited[(eng, sem)] = pv
                            def fw(e, sem=sem, pv=pv):
                                e.wait_ge(self.sems[sem], pv)
                            self.plan[eng].append(fw)
                    self.cnt[sem] += inc
                    tok = (sem, self.cnt[sem])
                    self.wsince[eng] = False
                    self.lasttok[eng] = self.cnt[sem]
                    def f(e, fn=fn, sem=sem, inc=inc):
                        fn(e).then_inc(self.sems[sem], inc)
                elif q is None:
                    self.cnt[sem] += inc
                    tok = (sem, self.cnt[sem])
                    val = self.cnt[sem]
                    def f(e, fn=fn, sem=sem, inc=inc, val=val):
                        fn(e).then_inc(self.sems[sem], inc)
                        e.wait_ge(self.sems[sem], val)
                    self.waited[(eng, sem)] = val
                    self.wsince[eng] = True
                else:
                    self.cnt[sem] += inc
                    tok = (sem, self.cnt[sem])
                    def f(e, fn=fn, sem=sem, inc=inc):
                        fn(e).then_inc(self.sems[sem], inc)
            else:
                tok = (sem, self.cnt[sem] + 1)
                def f(e, fn=fn):
                    fn(e)
            self.plan[eng].append(f)
            for k in reads:
                self.readers.setdefault(k, []).append(tok)
            for k in writes:
                self.lastw[k] = tok if token else "PENDING"
                self.readers[k] = []
            return tok

        def settok(self, keys, tok):
            for k in keys:
                self.lastw[k] = tok

    bld = Bld()
    alive = []

    def sb(name, shape, dtype):
        cm = nc.sbuf_tensor(name, shape, dtype)
        h = cm.__enter__()
        alive.append((cm, h))
        return h

    def sb_free_back_to(h):
        idx = next(i for i, (cm, hh) in enumerate(alive) if hh is h)
        while len(alive) > idx:
            cm, hh = alive.pop()
            cm.__exit__(None, None, None)

    ident16 = sb("ident16", [128, 128], BF16)
    ident32 = sb("ident32", [128, 128], F32)
    kidx = sb("kidx", [128, max(NKB)], F32)
    bias_tbl = sb("bias_tbl", [128, TOTKB], F32)
    onescol = sb("onescol", [128, 1], F32)
    onesrow = sb("onesrow", [1, 128], F32)
    onesrow16 = sb("onesrow16", [1, 128], BF16)
    fcell = sb("fcell", [128, 1], F32)
    epscol = sb("epscol", [128, 1], F32)
    mcol = sb("mcol", [128, 1], F32)
    negm = sb("negm", [128, 1], F32)
    m2col = sb("m2col", [128, 1], F32)
    soft_lo = sb("soft_lo", [128, 1], F32)
    soft_hi = sb("soft_hi", [128, 1], F32)
    lneps = sb("lneps", [128, 1], F32)
    bq_sb = sb("bq_sb", [128, 8], F32)
    bk_sb = sb("bk_sb", [128, 8], F32)
    bv_sb = sb("bv_sb", [128, 8], F32)
    bo16 = sb("bo16", [1, D], BF16)
    scal = sb("scal", [128, 20], F32)
    LN1C = sb("LN1C", [128, 2 * (NKVT + NOT)], F32)
    LN2C = sb("LN2C", [128, 2 * NOT], F32)
    slope_c = sb("slope_c", [128, 1], F32)
    SC_SG, SC_SWG, SC_SU, SC_SWU, SC_SD, SC_SWD = 0, 1, 2, 3, 4, 5
    SC_T0, SC_T1 = 6, 7

    psum_cm = []
    def psb(name, shape, dtype):
        cm = nc.psum_tensor(name, shape, dtype)
        h = cm.__enter__()
        psum_cm.append(cm)
        return h
    psW0 = psb("psW0", [128, 1024], F32)
    psW1 = psb("psW1", [128, 1024], F32)
    ps4 = psb("ps4", [128, 512], F32)
    ps5 = psb("ps5", [128, 512], F32)
    psE = psb("psE", [128, 1024], F32)
    psEa = psE[:, 0:512]
    psEb = psE[:, 512:1024]

    bld.op("gpsimd", lambda e: e.memset(ident16[:], 0.0), writes=("ident16",))
    bld.op("gpsimd", lambda e: e.affine_select(out=ident16[:], in_=ident16[:],
           compare_op=ALU.not_equal, fill=1.0, base=0, pattern=[[-1, 128]],
           channel_multiplier=1), reads=("ident16",), writes=("ident16",))
    bld.op("gpsimd", lambda e: e.nop(), reads=("ident16",))
    bld.op("gpsimd", lambda e: e.iota(kidx[:], [[128, max(NKB)]], channel_multiplier=1,
           allow_small_or_imprecise_dtypes=True), writes=("kidx",))
    bld.op("vector", lambda e: e.memset(onescol[:], 1.0), writes=("onescol",))
    bld.op("vector", lambda e: e.memset(lneps[:], 1e-5), writes=("lneps",))
    bld.op("vector", lambda e: e.memset(onesrow[:], 1.0), writes=("onesrow",))
    bld.op("vector", lambda e: e.memset(onesrow16[:], 1.0), writes=("onesrow16",))
    bld.op("vector", lambda e: e.tensor_scalar(fcell[:], onescol[:], TINY, 1.0,
           ALU.add, ALU.subtract), reads=("onescol",), writes=("fcell",))
    bld.op("vector", lambda e: e.tensor_scalar(epscol[:], fcell[:], -1.0, TINY,
           ALU.mult, ALU.add), reads=("fcell",), writes=("epscol",))
    bld.op("vector", lambda e: e.tensor_scalar(mcol[:], fcell[:], MSCALE, MAGIC,
           ALU.mult, ALU.add), reads=("fcell",), writes=("mcol",))
    bld.op("vector", lambda e: e.tensor_scalar(negm[:], mcol[:], -1.0, None,
           ALU.mult), reads=("mcol",), writes=("negm",))
    bld.op("vector", lambda e: e.tensor_scalar(m2col[:], mcol[:], 2.0, None,
           ALU.mult), reads=("mcol",), writes=("m2col",))
    _esc = float(EPS_SOFT / TINY)
    bld.op("vector", lambda e: e.tensor_scalar(soft_lo[:], epscol[:], -_esc, 1.0,
           ALU.mult, ALU.add), reads=("epscol",), writes=("soft_lo",))
    bld.op("vector", lambda e: e.tensor_scalar(soft_hi[:], epscol[:], _esc, 1.0,
           ALU.mult, ALU.add), reads=("epscol",), writes=("soft_hi",))
    bld.op("vector", lambda e: e.tensor_copy(ident32[:], ident16[:]),
           reads=("ident16",), writes=("ident32",))
    bld.op("sync", lambda e: e.dma_start(out=bq_sb[:], in_=bq_r), writes=("bq_sb",), q="QS")
    bld.op("sync", lambda e: e.dma_start(out=bk_sb[:], in_=bk_r), writes=("bk_sb",), q="QS")
    bld.op("sync", lambda e: e.dma_start(out=bv_sb[:], in_=bv_r), writes=("bv_sb",), q="QS")


    for h in range(H):
        c32, corr = _dual(_EXP[h] * LN2_64)
        bld.op("vector", lambda e, c32=c32, corr=corr: e.tensor_scalar(
            scal[0:1, SC_T0:SC_T0 + 1], fcell[0:1, :], corr, c32, ALU.mult, ALU.add),
            reads=("fcell",), writes=("sc_t0",))
        bld.op("scalar", lambda e: e.activation(scal[0:1, SC_T1:SC_T1 + 1],
            scal[0:1, SC_T0:SC_T0 + 1], AF.Exp), reads=("sc_t0",), writes=("sc_t1",))
        bld.op("vector", lambda e: e.tensor_scalar(scal[0:1, SC_T0:SC_T0 + 1],
            scal[0:1, SC_T1:SC_T1 + 1], -1.0, None, ALU.mult),
            reads=("sc_t1",), writes=("sc_t0",))
        bld.op("tensor", lambda e: e.matmul(psEa[:, 0:1], onesrow[:],
            scal[0:1, SC_T0:SC_T0 + 1], start=True, stop=True),
            reads=("onesrow", "sc_t0"), writes=("psEa",))
        bld.op("vector", lambda e: e.tensor_copy(slope_c[:], psEa[:, 0:1]),
            reads=("psEa",), writes=("slope_c",))
        nkb, b0 = NKB[h], int(BOFF[h])
        bld.op("vector", lambda e, nkb=nkb, b0=b0: e.tensor_scalar(
            bias_tbl[:, b0:b0 + nkb], kidx[:, 0:nkb], slope_c[:], None, ALU.mult),
            reads=("kidx", "slope_c"), writes=(f"bias:{h}",))

    KPH = int(os.environ.get("KPH", "4"))

    class _Stop(Exception):
        pass
    try:
        if KPH < 1:
            raise _Stop()
        XLT = sb("XLT", [128, 8 * KVT], BF16)
        bo_f = sb("bo_f", [1, D], F32)
        bld.op("sync", lambda e: e.dma_start(out=bo_f[:], in_=bo_r), writes=("bo_f",), q="QS")
        bld.op("vector", lambda e: e.tensor_copy(bo16[:], bo_f[:]), reads=("bo_f",), writes=("bo16",))
        xlt_kv = XLT[:].rearrange("p (c n) -> p c n", c=8)
        KT = sb("KT", [128, KCOLS], BF16)
        VT = sb("VT", [128, VCOLS], BF16)
        Wk16 = sb("Wk16", [128, 8 * D], BF16)
        Wv16 = sb("Wv16", [128, 8 * D], BF16)
        stg = [sb(f"stg{i}", [128, 2048], F32) for i in range(2)]
        xin = [sb(f"xin{i}", [128, D], F32) for i in range(4)]
        sq = sb("sq", [128, D], F32)
        cols = sb("cols", [128, 12], F32)
        mean_acc = sb("mean_acc", [128, 64], F32)

        def ln_stats(src_key, src_ap, slot, colarr, ckey):
            p6 = (slot % 2) * 6
            cs = lambda i: cols[:, p6 + i:p6 + i + 1]
            kc = lambda i: f"c{slot % 2}_{i}"
            bld.op("vector", lambda e, cs=cs: e.reduce_sum(cs(0), src_ap, axis=AX.X),
                   reads=(src_key,), writes=(kc(0),))
            bld.op("scalar", lambda e, cs=cs: e.activation(sq[:], src_ap, AF.Square,
                   accum_out=cs(1)), reads=(src_key,), writes=("sq", kc(1)))
            bld.op("vector", lambda e, cs=cs: e.tensor_scalar(cs(2), cs(0),
                   float(1.0 / D), None, ALU.mult), reads=(kc(0),), writes=(kc(2),))
            bld.op("vector", lambda e, cs=cs: e.tensor_mul(cs(3), cs(2), cs(2)),
                   reads=(kc(2),), writes=(kc(3),))
            bld.op("vector", lambda e, cs=cs: e.scalar_tensor_tensor(cs(4), cs(1),
                   float(1.0 / D), cs(3), ALU.mult, ALU.subtract),
                   reads=(kc(1), kc(3)), writes=(kc(4),))
            bld.op("scalar", lambda e, cs=cs: e.activation(cs(5), cs(4), AF.Ln,
                   bias=lneps[:]), reads=(kc(4), "lneps"), writes=(kc(5),))
            bld.op("scalar", lambda e, cs=cs, colarr=colarr, slot=slot: e.activation(
                   colarr[:, 2 * slot:2 * slot + 1], cs(5), AF.Exp, scale=-0.5),
                   reads=(kc(5),), writes=(ckey + ":r",))
            bld.op("vector", lambda e, cs=cs, colarr=colarr, slot=slot:
                   e.scalar_tensor_tensor(colarr[:, 2 * slot + 1:2 * slot + 2], cs(2),
                   -1.0, colarr[:, 2 * slot:2 * slot + 1], ALU.mult, ALU.mult),
                   reads=(kc(2), ckey + ":r"), writes=(ckey + ":m",))

        NT_ALL = NKVT + NOT
        astat = sb("astat", [128, 4 * NT_ALL], F32)

        def ln_sum2(src_key, src_ap, slot):
            bld.op("vector", lambda e, src_ap=src_ap, slot=slot: e.reduce_sum(
                   astat[:, slot:slot + 1], src_ap, axis=AX.X),
                   reads=(src_key,), writes=(f"as:{slot}",))
            bld.op("scalar", lambda e, src_ap=src_ap, slot=slot: e.activation(
                   sq[:], src_ap, AF.Square,
                   accum_out=astat[:, NT_ALL + slot:NT_ALL + slot + 1]),
                   reads=(src_key,), writes=("sq", f"aq:{slot}"))

        def ln_batch(b, n):
            A2, A3c = 2 * NT_ALL, 3 * NT_ALL
            rs = tuple(f"as:{s}" for s in range(b, b + n))
            rq = tuple(f"aq:{s}" for s in range(b, b + n))
            bld.op("vector", lambda e, b=b, n=n: e.tensor_scalar(
                   astat[:, A2 + b:A2 + b + n], astat[:, b:b + n],
                   float(1.0 / D), None, ALU.mult),
                   reads=rs, writes=(f"am:{b}",))
            bld.op("vector", lambda e, b=b, n=n: e.tensor_mul(
                   astat[:, A3c + b:A3c + b + n], astat[:, A2 + b:A2 + b + n],
                   astat[:, A2 + b:A2 + b + n]),
                   reads=(f"am:{b}",), writes=(f"am2:{b}",))
            bld.op("vector", lambda e, b=b, n=n: e.scalar_tensor_tensor(
                   astat[:, b:b + n], astat[:, NT_ALL + b:NT_ALL + b + n],
                   float(1.0 / D), astat[:, A3c + b:A3c + b + n],
                   ALU.mult, ALU.subtract),
                   reads=rq + (f"am2:{b}",), writes=(f"av:{b}",))
            bld.op("scalar", lambda e, b=b, n=n: e.activation(
                   astat[:, NT_ALL + b:NT_ALL + b + n], astat[:, b:b + n],
                   AF.Ln, bias=lneps[:]),
                   reads=(f"av:{b}", "lneps"), writes=(f"al:{b}",))
            bld.op("scalar", lambda e, b=b, n=n: e.activation(
                   LN1C[:, b:b + n], astat[:, NT_ALL + b:NT_ALL + b + n],
                   AF.Exp, scale=-0.5),
                   reads=(f"al:{b}",), writes=(f"ar:{b}",))
            bld.op("vector", lambda e, b=b, n=n: e.scalar_tensor_tensor(
                   LN1C[:, NT_ALL + b:NT_ALL + b + n], astat[:, A2 + b:A2 + b + n],
                   -1.0, LN1C[:, b:b + n], ALU.mult, ALU.mult),
                   reads=(f"am:{b}", f"ar:{b}"), writes=(f"an:{b}",))

        def ln_apply_t(src_key, src_ap, slot, bkey, dst3, dstkey, tslot):
            lb = xin[(tslot + 2) % 4]
            lbk = f"xin{(tslot + 2) % 4}"
            bld.op("vector", lambda e, lb=lb, slot=slot: e.tensor_scalar(
                   lb[:], src_ap, LN1C[:, slot:slot + 1],
                   LN1C[:, NT_ALL + slot:NT_ALL + slot + 1], ALU.mult, ALU.add),
                   reads=(src_key, f"ar:{bkey}", f"an:{bkey}"), writes=(lbk,))
            for j in range(8):
                tok = bld.op("tensor", lambda e, j=j, lb=lb: e.transpose(
                    out=psE[:, j * 128:(j + 1) * 128], in_=lb[:, j * 128:(j + 1) * 128],
                    identity=ident32[:]),
                    reads=(lbk, "ident32") + (("psEa", "psEb") if j == 0 else ()),
                    writes=("psEa", "psEb") if j == 0 else (), token=(j == 7))
            bld.settok(["psEa", "psEb"], tok)
            bld.op("scalar", lambda e, dst3=dst3, tslot=tslot: e.activation(
                   dst3[:, :, tslot * 128:(tslot + 1) * 128],
                   psE[:].rearrange("p (c n) -> p c n", c=8), AF.Identity),
                   reads=("psEa", "psEb"), writes=(dstkey,))

        mean_jobs = []
        for ci in range(16):
            c, hcol = ci // 2, ci % 2
            mean_jobs.append((wgT_c[:, c, hcol * 2048:(hcol + 1) * 2048], ci))
        for ci in range(16):
            c, hcol = ci // 2, ci % 2
            mean_jobs.append((wuT_c[:, c, hcol * 2048:(hcol + 1) * 2048], 16 + ci))
        for ci in range(32):
            mean_jobs.append((wdT_c[:, ci, :], 32 + ci))
        mj_i = [0]

        ml_i = [0]

        def emit_mean_load():
            if ml_i[0] >= len(mean_jobs):
                return
            src, _ = mean_jobs[ml_i[0]]
            sgi = ml_i[0] % 2
            ml_i[0] += 1
            n = src.shape[-1]
            sk = f"stg{sgi}"
            h2 = n // 2
            bld.op("sync", lambda e, src=src, sgi=sgi, h2=h2: e.dma_start(
                   out=stg[sgi][:, 0:h2], in_=src[:, 0:h2]),
                   writes=(sk,), q="QS")
            bld.op("sync", lambda e, src=src, sgi=sgi, h2=h2, n=n: e.dma_start(
                   out=stg[sgi][:, h2:n], in_=src[:, h2:n]),
                   writes=(sk + "h",), q="QS")

        def emit_mean_step():
            if mj_i[0] >= len(mean_jobs):
                return
            while ml_i[0] <= min(mj_i[0] + 1, len(mean_jobs) - 1):
                emit_mean_load()
            src, acol = mean_jobs[mj_i[0]]
            sgi = mj_i[0] % 2
            mj_i[0] += 1
            n = src.shape[-1]
            if n > 1024:
                bld.op("vector", lambda e, sgi=sgi, acol=acol, n=n: e.reduce_sum(
                       mean_acc[:, acol:acol + 1], stg[sgi][:, 0:n], axis=AX.X,
                       apply_absolute_value=True),
                       reads=(f"stg{sgi}", f"stg{sgi}h"), writes=(f"macc:{acol}",))
            else:
                bld.op("scalar", lambda e, sgi=sgi, acol=acol, n=n: e.activation(
                       sq[:, 0:n], stg[sgi][:, 0:n], AF.Abs,
                       accum_out=mean_acc[:, acol:acol + 1]),
                       reads=(f"stg{sgi}", f"stg{sgi}h"), writes=("sq", f"macc:{acol}"))

        def a_dma(t):
            xi = xin[t % 4]
            bld.op("sync", lambda e, t=t, xi=xi: e.dma_start(out=xi[:], in_=xk_t[t]),
                   writes=(f"xin{t % 4}",), q="QS")
        a_dma(0)
        for t in range(NKVT):
            emit_mean_step()
            if t + 1 < NKVT:
                a_dma(t + 1)
            ln_sum2(f"xin{t % 4}", xin[t % 4][:], t)
        ln_batch(0, NKVT)
        a_dma(0)
        for t in range(NKVT):
            emit_mean_step()
            if t + 1 < NKVT:
                a_dma(t + 1)
            ln_apply_t(f"xin{t % 4}", xin[t % 4][:], t, 0,
                       xlt_kv, f"xlt:{t}", t)

        if KPH < 2:
            raise _Stop()
        bld.op("gpsimd", lambda e: e.dma_start(
               out=Wk16[:].rearrange("p (c n) -> p c n", c=8), in_=WT_c["k"]),
               writes=("Wk16",), q="QG")
        bld.op("gpsimd", lambda e: e.dma_start(
               out=Wv16[:].rearrange("p (c n) -> p c n", c=8), in_=WT_c["v"]),
               writes=("Wv16",), q="QG")
        wk3 = Wk16[:].rearrange("p (c n) -> p c n", c=8)
        wv3 = Wv16[:].rearrange("p (c n) -> p c n", c=8)

        def nblocks(n):
            out, o = [], 0
            while o < n:
                w = min(512, n - o)
                out.append((o, w))
                o += w
            return out

        pg = [ps4, ps5]
        pgk = ["ps4", "ps5"]
        pi = [0]
        for c in range(8):
            for (o, w) in nblocks(KEPT_E[c]):
                emit_mean_step()
                bank = pi[0] % 2
                pi[0] += 1
                pk = pgk[bank]
                reads_xl = tuple(f"xlt:{ts}" for ts in range(o // 128, (o + w + 127) // 128))
                for dc in range(8):
                    tok = bld.op("tensor", lambda e, c=c, dc=dc, o=o, w=w, bank=bank:
                        e.matmul(pg[bank][:, 0:w], wk3[:, dc, c * 128:(c + 1) * 128],
                                 xlt_kv[:, dc, o:o + w], start=(dc == 0), stop=(dc == 7)),
                        reads=("Wk16",) + reads_xl + ((pk,) if dc == 0 else ()),
                        writes=(pk,) if dc == 0 else (), token=(dc == 7))
                bld.settok([pk], tok)
                bld.op("scalar", lambda e, c=c, o=o, w=w, bank=bank: e.activation(
                       KT[:, KOFF[c] + o:KOFF[c] + o + w], pg[bank][:, 0:w],
                       AF.Identity, bias=bk_sb[:, c:c + 1]),
                       reads=(pk, "bk_sb"), writes=(f"KT:{c}",))
        for t in range(NKVT):
            nh = NHT[t]
            if nh == 0:
                continue
            vt3 = VT[:, VOFF[t]:VOFF[t] + 65 * nh].rearrange("p (h x) -> p h x", x=65)
            bld.op("gpsimd", lambda e, vt3=vt3: e.memset(vt3[:, :, 64:65], 1.0),
                   writes=(f"vones:{t}",))
            for hb0 in range(0, nh, 8):
                hbn = min(8, nh - hb0)
                emit_mean_step()
                bank = pi[0] % 2
                pi[0] += 1
                pk = pgk[bank]
                for dc in range(8):
                    tok = bld.op("tensor", lambda e, t=t, dc=dc, hb0=hb0, hbn=hbn, bank=bank:
                        e.matmul(pg[bank][:, 0:64 * hbn], xlt_kv[:, dc, t * 128:(t + 1) * 128],
                                 wv3[:, dc, hb0 * 64:(hb0 + hbn) * 64],
                                 start=(dc == 0), stop=(dc == 7)),
                        reads=(f"xlt:{t}", "Wv16") + ((pk,) if dc == 0 else ()),
                        writes=(pk,) if dc == 0 else (), token=(dc == 7))
                bld.settok([pk], tok)
                bld.op("scalar", lambda e, t=t, vt3=vt3, hb0=hb0, hbn=hbn, bank=bank:
                       e.activation(vt3[:, hb0:hb0 + hbn, 0:64], pg[bank][:, 0:64 * hbn],
                                    AF.Identity),
                       reads=(pk,), writes=(f"VT:{t}",))

        def mean_final(col_lo, col_n, sc_s, sc_sw):
            kacc = tuple(f"macc:{a}" for a in range(col_lo, col_lo + col_n))
            bld.op("vector", lambda e: e.reduce_sum(cols[:, 9:10],
                   mean_acc[:, col_lo:col_lo + col_n], axis=AX.X),
                   reads=kacc, writes=("c_w",))
            bld.op("tensor", lambda e: e.matmul(psEa[0:1, 0:1], cols[:, 9:10],
                   onescol[:], start=True, stop=True),
                   reads=("c_w", "onescol"), writes=("psEa",))
            bld.op("vector", lambda e: e.tensor_scalar(cols[0:1, 10:11], psEa[0:1, 0:1],
                   float(2.0 ** -22), 1e-5, ALU.mult, ALU.max),
                   reads=("psEa",), writes=("c_mean1",))
            bld.op("vector", lambda e: e.reciprocal(cols[0:1, 11:12], cols[0:1, 10:11]),
                   reads=("c_mean1",), writes=("c_s1",))
            bld.op("tensor", lambda e: e.matmul(psEa[:, 0:2], onesrow[:],
                   cols[0:1, 10:12], start=True, stop=True),
                   reads=("c_mean1", "c_s1", "onesrow"), writes=("psEa",))
            bld.op("vector", lambda e, sc_sw=sc_sw: e.tensor_copy(
                   scal[:, sc_sw:sc_sw + 1], psEa[:, 0:1]),
                   reads=("psEa",), writes=(f"sc:{sc_sw}",))
            bld.op("vector", lambda e, sc_s=sc_s: e.tensor_copy(
                   scal[:, sc_s:sc_s + 1], psEa[:, 1:2]),
                   reads=("psEa",), writes=(f"sc:{sc_s}",))

        def a3_dma(t):
            xi = xin[t % 4]
            bld.op("sync", lambda e, t=t, xi=xi: e.dma_start(out=xi[:], in_=xo_t[t]),
                   writes=(f"xin{t % 4}",), q="QS")
        a3_dma(0)
        for t in range(NOT):
            if t + 1 < NOT:
                a3_dma(t + 1)
            ln_sum2(f"xin{t % 4}", xin[t % 4][:], NKVT + t)
        ln_batch(NKVT, NOT)
        while mj_i[0] < len(mean_jobs):
            emit_mean_step()
        mean_final(0, 16, SC_SG, SC_SWG)
        mean_final(16, 16, SC_SU, SC_SWU)
        mean_final(32, 32, SC_SD, SC_SWD)
        bld.op("vector", lambda e: e.tensor_scalar(scal[:, SC_SWD:SC_SWD + 1],
               scal[:, SC_SWD:SC_SWD + 1], 0.5, None, ALU.mult),
               reads=(f"sc:{SC_SWD}",), writes=(f"sc:{SC_SWD}",))

        if KPH < 3:
            raise _Stop()
        bld.barrier()
        xlt_own = XLT[:].rearrange("p (c n) -> p c n", c=8)
        a3_dma(0)
        for t in range(NOT):
            if t + 1 < NOT:
                a3_dma(t + 1)
            ln_apply_t(f"xin{t % 4}", xin[t % 4][:], NKVT + t, NKVT,
                       xlt_own, f"xlo:{t}", t)

        bld.op("gpsimd", lambda e: e.dma_start(
               out=Wk16[:].rearrange("p (c n) -> p c n", c=8), in_=WT_c["q"]),
               writes=("Wk16",), q="QG")
        bld.op("gpsimd", lambda e: e.dma_start(
               out=Wv16[:].rearrange("p (c n) -> p c n", c=8), in_=WT_c["o"]),
               writes=("Wv16",), q="QG")
        wq3, wo3 = wk3, wv3

        QTb = sb("QTb", [128, 8 * 1024], BF16)
        OT = sb("OT", [128, 8 * 1024], BF16)
        Pring = [sb(f"P{i}", [128, 1024], BF16) for i in range(3)]
        stg8 = [sb(f"stg8_{i}", [128, 2048], FP8) for i in range(2)]
        x1b = [sb(f"x1b{i}", [128, D], F32) for i in range(2)]
        park3 = [sb(f"pkp{i}", [128, 4, 256], F32) for i in range(2)]
        dstack = sb("dstack", [128, 1536], F32)
        den_pair = [dstack[0:4, 0:512], dstack[32:36, 0:512]]
        rec_pair = [dstack[0:4, 512:1024], dstack[0:4, 1024:1536]]
        denrow = [dstack[64:65, 0:512], dstack[96:97, 0:512]]
        selT = sb("selT", [128, 8], F32)
        sel_h = [sb(f"sel{i}", [4, 128], F32) for i in range(2)]
        bld.op("gpsimd", lambda e: e.memset(selT[:], 0.0), writes=("selT",))
        for half in range(2):
            bld.op("gpsimd", lambda e, half=half: e.memset(
                   selT[0:64, 4 * half + half:4 * half + half + 1], 1.0),
                   reads=("selT",), writes=("selT",))
            bld.op("gpsimd", lambda e, half=half: e.memset(
                   selT[64:128, 4 * half + 2 + half:4 * half + 3 + half], 1.0),
                   reads=("selT",), writes=("selT",))
        for half in range(2):
            bld.op("tensor", lambda e, half=half: e.transpose(
                   out=psEa[0:4, 0:128], in_=selT[:, 4 * half:4 * half + 4],
                   identity=ident32[:]), reads=("selT", "ident32"), writes=("psEa",))
            bld.op("vector", lambda e, half=half: e.tensor_copy(
                   sel_h[half][:], psEa[0:4, 0:128]),
                   reads=("psEa",), writes=(f"sel{half}",))

        tern_jobs = []
        for ci in range(16):
            c, hcol = ci // 2, ci % 2
            tern_jobs.append(("g", wgT_c[:, c, hcol * 2048:(hcol + 1) * 2048],
                              wg8d[:, ci * 2048:(ci + 1) * 2048], SC_SG))
        for ci in range(16):
            c, hcol = ci // 2, ci % 2
            tern_jobs.append(("u", wuT_c[:, c, hcol * 2048:(hcol + 1) * 2048],
                              wu8d[:, ci * 2048:(ci + 1) * 2048], SC_SU))
        for ci in range(32):
            tern_jobs.append(("d", wdT_c[:, ci, :], wd8d[:, ci * 1024:(ci + 1) * 1024], SC_SD))
        tj_i = [0]
        tl_i = [0]

        def emit_tern_load():
            if tl_i[0] >= len(tern_jobs):
                return
            _, src, _, _ = tern_jobs[tl_i[0]]
            sgi = tl_i[0] % 2
            tl_i[0] += 1
            n = src.shape[-1]
            sk = f"stg{sgi}"
            h2 = n // 2
            bld.op("sync", lambda e, src=src, sgi=sgi, h2=h2: e.dma_start(
                   out=stg[sgi][:, 0:h2], in_=src[:, 0:h2]),
                   writes=(sk,), q="QS")
            bld.op("sync", lambda e, src=src, sgi=sgi, h2=h2, n=n: e.dma_start(
                   out=stg[sgi][:, h2:n], in_=src[:, h2:n]),
                   writes=(sk + "h",), q="QS")

        def emit_tern_step():
            if tj_i[0] >= len(tern_jobs):
                return
            while tl_i[0] <= min(tj_i[0] + 1, len(tern_jobs) - 1):
                emit_tern_load()
            tag, src, dst, sc_s = tern_jobs[tj_i[0]]
            sgi = tj_i[0] % 2
            tj_i[0] += 1
            n = src.shape[-1]
            sk, s8k = f"stg{sgi}", f"stg8_{sgi}"
            s_ap = scal[:, sc_s:sc_s + 1]
            if tag != "d":
                bld.op("vector", lambda e, sgi=sgi, n=n, s_ap=s_ap: e.tensor_scalar(
                       stg[sgi][:, 0:n], stg[sgi][:, 0:n], s_ap, 1.49, ALU.mult, ALU.min),
                       reads=(sk, sk + "h", f"sc:{sc_s}"), writes=(sk,))
                bld.op("vector", lambda e, sgi=sgi, n=n: e.tensor_scalar(
                       stg[sgi][:, 0:n], stg[sgi][:, 0:n], -1.49, mcol[:], ALU.max, ALU.add),
                       reads=(sk, sk + "h", "mcol"), writes=(sk,))
                bld.op("scalar", lambda e, sgi=sgi, n=n: e.activation(
                       stg8[sgi][:, 0:n], stg[sgi][:, 0:n], AF.Identity, bias=negm[:]),
                       reads=(sk, sk + "h", "negm"), writes=(s8k,))
            else:
                bld.op("vector", lambda e, sgi=sgi, n=n, s_ap=s_ap: e.tensor_scalar(
                       stg[sgi][:, 0:n], stg[sgi][:, 0:n], s_ap, 1.49, ALU.mult, ALU.min),
                       reads=(sk, sk + "h", f"sc:{sc_s}"), writes=(sk,))
                bld.op("vector", lambda e, sgi=sgi, n=n: e.tensor_scalar(
                       stg[sgi][:, n:2 * n], stg[sgi][:, 0:n], -1.49, soft_lo[:],
                       ALU.max, ALU.mult), reads=(sk, sk + "h", "soft_lo"), writes=(sk,))
                bld.op("vector", lambda e, sgi=sgi, n=n: e.tensor_scalar(
                       stg[sgi][:, 0:n], stg[sgi][:, 0:n], -1.49, soft_hi[:],
                       ALU.max, ALU.mult), reads=(sk, sk + "h", "soft_hi"), writes=(sk,))
                bld.op("vector", lambda e, sgi=sgi, n=n: e.tensor_scalar(
                       stg[sgi][:, n:2 * n], stg[sgi][:, n:2 * n], mcol[:], None,
                       ALU.add), reads=(sk, sk + "h", "mcol"), writes=(sk,))
                bld.op("vector", lambda e, sgi=sgi, n=n: e.tensor_scalar(
                       stg[sgi][:, 0:n], stg[sgi][:, 0:n], mcol[:], None,
                       ALU.add), reads=(sk, sk + "h", "mcol"), writes=(sk,))
                bld.op("vector", lambda e, sgi=sgi, n=n: e.scalar_tensor_tensor(
                       stg8[sgi][:, 0:n], stg[sgi][:, n:2 * n], m2col[:],
                       stg[sgi][:, 0:n], ALU.subtract, ALU.add),
                       reads=(sk, sk + "h", "m2col"), writes=(s8k,))
            bld.op("sync", lambda e, dst=dst, sgi=sgi, n=n: e.dma_start(
                   out=dst, in_=stg8[sgi][:, 0:n]), reads=(s8k,),
                   writes=(f"w8d:{tag}",), q="QS")

        def o_ap(h, half):
            if h % 2 == 0:
                return (pg[half], pgk[half], pg[half][:], 0)
            return (psW1, "psW1a" if half == 0 else "psW1b",
                    psW1[:, half * 512:(half + 1) * 512], half * 512)

        def emit_evac(qp, h):
            c, off = h // 2, 64 * (h % 2)
            pr = c % 2
            pp, ppk = park3[pr], f"pp{pr}"
            for half in range(2):
                _, opk, oview, _ = o_ap(h, half)
                bld.op("vector", lambda e, oview=oview, pp=pp, off=off, half=half:
                       e.tensor_copy(pp[off:off + 64, half * 2:half * 2 + 2, :],
                                     oview[0:64, :].rearrange("p (a n) -> p a n", a=2)),
                       reads=(opk,), writes=(ppk,))
                dr = half
                bld.op("vector", lambda e, oview=oview, dr=dr: e.tensor_copy(
                       denrow[dr], oview[64:65, :]),
                       reads=(opk,), writes=(f"dnr{dr}",))
                r = 2 * (h % 2) + half
                bld.op("sync", lambda e, dr=dr, pr=pr, r=r: e.dma_start(
                       out=den_pair[pr][r:r + 1, :], in_=denrow[dr]),
                       reads=(f"dnr{dr}",), writes=(f"dp{pr}:{r}",), q="QS")

        def emit_recip(c):
            pr = c % 2
            bld.op("vector", lambda e, pr=pr: e.reciprocal(rec_pair[pr],
                   den_pair[pr]),
                   reads=tuple(f"dp{pr}:{r}" for r in range(4)), writes=(f"rp{pr}",))

        def emit_div(c, half, hnow):
            pr = c % 2
            pp, ppk = park3[pr], f"pp{pr}"
            if hnow % 2 == 0:
                rbv, rbk = psW1[:, 0:512], "psW1a"
            else:
                rbv, rbk = ps4[:], "ps4"
            bld.op("tensor", lambda e, pr=pr, half=half, rbv=rbv: e.matmul(
                   rbv, sel_h[half][:], rec_pair[pr], start=True, stop=True),
                   reads=(f"sel{half}", f"rp{pr}", rbk), writes=(rbk,))
            rb3 = rbv.rearrange("p (a n) -> p a n", a=2)
            bld.op("vector", lambda e, pp=pp, half=half, rb3=rb3: e.tensor_mul(
                   pp[:, half * 2:half * 2 + 2, :],
                   pp[:, half * 2:half * 2 + 2, :], rb3),
                   reads=(ppk, rbk), writes=(ppk,))
            bld.op("vector", lambda e, pp=pp, c=c, half=half: e.tensor_scalar(
                   OT[:, c * 1024 + half * 512:c * 1024 + (half + 1) * 512]
                       .rearrange("p (a n) -> p a n", a=2),
                   pp[:, half * 2:half * 2 + 2, :], bv_sb[:, c:c + 1], None, ALU.add),
                   reads=(ppk, "bv_sb"), writes=(f"OT:{c}",))

        for qp in range(2):
            for c in range(8):
                pw = psW0 if c % 2 == 0 else psW1
                pwk = ("psW0a", "psW0b") if c % 2 == 0 else ("psW1a", "psW1b")
                for half in range(2):
                    for dc in range(8):
                        tok = bld.op("tensor", lambda e, c=c, dc=dc, qp=qp, half=half, pw=pw:
                            e.matmul(pw[:, half * 512:(half + 1) * 512],
                                     wq3[:, dc, c * 128:(c + 1) * 128],
                                     xlt_own[:, dc, qp * 1024 + half * 512:qp * 1024 + (half + 1) * 512],
                                     start=(dc == 0), stop=(dc == 7)),
                            reads=("Wk16",) + tuple(f"xlo:{ts}" for ts in
                                   range(qp * 8 + half * 4, qp * 8 + half * 4 + 4)) +
                                   ((pwk[half],) if dc == 0 else ()),
                            writes=(pwk[half],) if dc == 0 else (), token=(dc == 7))
                    bld.settok([pwk[half]], tok)
                bld.op("scalar", lambda e, c=c, pw=pw: e.activation(
                       QTb[:, c * 1024:(c + 1) * 1024], pw[:], AF.Identity,
                       bias=bq_sb[:, c:c + 1]),
                       reads=(pwk[0], pwk[1], "bq_sb"), writes=(f"QTb:{c}",))
                emit_tern_step()
            for h in range(H):
                emit_tern_step()
                emit_tern_step()
                c, off = h // 2, 64 * (h % 2)
                nkb = NKB[h]

                def emit_S(kb, c=c, off=off, h=h):
                    if kb % 2 == 0:
                        sbv, sk0, sk1 = psW0, "psW0a", "psW0b"
                    else:
                        sbv, sk0, sk1 = psE, "psEa", "psEb"
                    for half in range(2):
                        tok = bld.op("tensor", lambda e, c=c, off=off, kb=kb, half=half, sbv=sbv:
                            e.matmul(sbv[:, half * 512:(half + 1) * 512],
                                     KT[off:off + 64, KOFF[c] + kb * 128:KOFF[c] + (kb + 1) * 128],
                                     QTb[off:off + 64, c * 1024 + half * 512:c * 1024 + (half + 1) * 512],
                                     start=True, stop=True),
                            reads=(f"KT:{c}", f"QTb:{c}", sk0 if half == 0 else sk1),
                            writes=(sk0 if half == 0 else sk1,), token=(half == 1))
                    bld.settok([sk0, sk1], tok)
                    pidx = kb % 3
                    bld.op("scalar", lambda e, kb=kb, h=h, pidx=pidx, sbv=sbv: e.activation(
                           Pring[pidx][:], sbv[:], AF.Exp, scale=0.125,
                           bias=bias_tbl[:, BOFF[h] + kb:BOFF[h] + kb + 1]),
                           reads=(sk0, sk1, f"bias:{h}"), writes=(f"P:{pidx}",))

                def emit_PV(kb, h=h, nkb=nkb):
                    pidx = kb % 3
                    for half in range(2):
                        _, opk, oview, _ = o_ap(h, half)
                        vslice = VT[:, VOFF[kb] + h * 65:VOFF[kb] + (h + 1) * 65]
                        tok = bld.op("tensor", lambda e, vslice=vslice, kb=kb, half=half,
                            oview=oview, pidx=pidx, nkb=nkb:
                            e.matmul(oview[0:65, :], vslice,
                                     Pring[pidx][:, half * 512:(half + 1) * 512],
                                     start=(kb == 0), stop=(kb == nkb - 1)),
                            reads=(f"VT:{kb}", f"vones:{kb}", f"P:{pidx}") +
                                  ((opk,) if kb == 0 else ()),
                            writes=(opk,) if kb == 0 else (), token=(kb == nkb - 1))
                        if kb == nkb - 1:
                            bld.settok([opk], tok)

                emit_S(0)
                if nkb > 1:
                    emit_S(1)
                for kb in range(nkb):
                    if kb + 2 < nkb:
                        emit_S(kb + 2)
                    emit_PV(kb)
                if h >= 1:
                    emit_evac(qp, h - 1)
                if h >= 3:
                    emit_div((h - 3) // 2, (h - 3) % 2, h)
                if h >= 2 and h % 2 == 0:
                    emit_recip(h // 2 - 1)
            emit_evac(qp, H - 1)
            emit_recip(H // 2 - 1)
            emit_div(6, 1, 16)
            emit_div(7, 0, 17)
            emit_div(7, 1, 18)
            for tt in range(8):
                gt = qp * 8 + tt
                xi = xin[gt % 4]
                xik = f"xin{gt % 4}"
                bld.op("sync", lambda e, gt=gt, xi=xi: e.dma_start(out=xi[:], in_=xo_t[gt]),
                       writes=(xik,), q="QS")
                for half in range(2):
                    bank = half
                    pk = pgk[bank]
                    for ac in range(8):
                        bld.op("tensor", lambda e, ac=ac, tt=tt, half=half, bank=bank:
                            e.matmul(pg[bank][:], OT[:, ac * 1024 + tt * 128:ac * 1024 + (tt + 1) * 128],
                                     wo3[:, ac, half * 512:(half + 1) * 512],
                                     start=(ac == 0), stop=False),
                            reads=(f"OT:{ac}", "Wv16") + ((pk,) if ac == 0 else ()),
                            writes=(pk,) if ac == 0 else (), token=False)
                    tok = bld.op("tensor", lambda e, half=half, bank=bank:
                        e.matmul(pg[bank][:], onesrow16[:], bo16[:, half * 512:(half + 1) * 512],
                                 start=False, stop=True),
                        reads=("onesrow16", "bo16"), writes=(), token=True)
                    bld.settok([pk], tok)
                x1k = f"x1b{tt % 2}"
                for half in range(2):
                    bld.op("vector", lambda e, tt=tt, half=half, xi=xi: e.tensor_add(
                           x1b[tt % 2][:, half * 512:(half + 1) * 512],
                           pg[half][:], xi[:, half * 512:(half + 1) * 512]),
                           reads=(pgk[half], xik), writes=(x1k,))
                ln_stats(x1k, x1b[tt % 2][:], gt, LN2C, f"ln2c:{gt}")
                bld.op("sync", lambda e, gt=gt, tt=tt: e.dma_start(
                       out=x1_t[gt], in_=x1b[tt % 2][:]), reads=(x1k,),
                       writes=(f"x1scr:{gt}",), q="QS")
                emit_tern_step()
        while tj_i[0] < len(tern_jobs):
            emit_tern_step()

        if KPH < 4:
            raise _Stop()
        bld.barrier()
        sb_free_back_to(XLT)
        if os.environ.get("KDBG"):
            print("alive after free:", [hh.name if hasattr(hh,'name') else str(hh) for cm,hh in alive])
        wg8 = sb("wg8", [128, 8 * 4096], FP8)
        wu8 = sb("wu8", [128, 8 * 4096], FP8)
        wd8 = sb("wd8", [128, 32 * 1024], FP8)
        gu = sb("gu", [128, 32 * TG], F32)
        hT = sb("hT", [128, 32 * TG], BF16)
        xqT = [sb(f"xqT{i}", [128, 8 * TG], BF16) for i in range(2)]
        x1in = [sb(f"x1in{i}", [128, D], F32) for i in range(6)]
        xhat = sb("xhat", [128, D], F32)
        gsil = [sb(f"gsil{i}", [128, TG], F32) for i in range(2)]
        s2b = sb("s2b", [128, 2 * TG], F32)
        s2row = sb("s2row", [1, 2 * TG], F32)
        Mx = sb("Mx", [128, TG], F32)
        gabs = [sb(f"gabs{i}", [128, TG], F32) for i in range(2)]
        outb = [sb(f"outb{i}", [128, D], F32) for i in range(2)]
        dcols = sb("dcols", [128, 12], F32)

        wg83 = wg8[:].rearrange("p (c n) -> p c n", c=8)
        wu83 = wu8[:].rearrange("p (c n) -> p c n", c=8)
        wd83 = wd8[:].rearrange("p (c n) -> p c n", c=32)

        gbanks = [(ps4[:], "ps4"), (ps5[:], "ps5"), (psEa, "psEa"),
                  (psW0[:, 0:512], "psW0a")]
        dbanks = [(psW0[:, 512:1024], "psW0b"), (psW1[:, 0:512], "psW1a")]

        def emit_lnxq(tg):
            for ti in range(2):
                gt = tg * 2 + ti
                xk = f"x1in{gt % 6}"
                bld.op("sync", lambda e, gt=gt: e.dma_start(out=x1in[gt % 6][:], in_=x1_t[gt]),
                       reads=(f"x1scr:{gt}",), writes=(xk,), q="QS")
                bld.op("vector", lambda e, gt=gt: e.tensor_scalar(xhat[:], x1in[gt % 6][:],
                       LN2C[:, 2 * gt:2 * gt + 1], LN2C[:, 2 * gt + 1:2 * gt + 2],
                       ALU.mult, ALU.add),
                       reads=(xk, f"ln2c:{gt}:r", f"ln2c:{gt}:m"), writes=("xhat",))
                bld.op("vector", lambda e: e.reduce_max(dcols[:, 0:1], xhat[:], axis=AX.X,
                       apply_absolute_value=True), reads=("xhat",), writes=("d_m1",))
                bld.op("vector", lambda e: e.tensor_scalar(dcols[:, 0:1], dcols[:, 0:1],
                       1e-5, None, ALU.max), reads=("d_m1",), writes=("d_m1",))
                bld.op("vector", lambda e: e.reciprocal(dcols[:, 1:2], dcols[:, 0:1]),
                       reads=("d_m1",), writes=("d_rm1",))
                bld.op("vector", lambda e: e.tensor_scalar(dcols[:, 2:3], dcols[:, 1:2],
                       127.0, None, ALU.mult), reads=("d_rm1",), writes=("d_s1",))
                bld.op("vector", lambda e: e.tensor_scalar(dcols[:, 3:4], dcols[:, 0:1],
                       float(1.0 / 127.0), None, ALU.mult), reads=("d_m1",), writes=("d_sv1",))
                bld.op("vector", lambda e: e.tensor_scalar(xhat[:], xhat[:],
                       dcols[:, 2:3], mcol[:], ALU.mult, ALU.add),
                       reads=("xhat", "d_s1", "mcol"), writes=("xhat",))
                bld.op("vector", lambda e: e.tensor_scalar(xhat[:], xhat[:],
                       mcol[:], dcols[:, 3:4], ALU.subtract, ALU.mult),
                       reads=("xhat", "d_sv1", "mcol"), writes=("xhat",))
                xq3 = xqT[tg % 2][:].rearrange("p (c n) -> p c n", c=8)
                for rnd in range(2):
                    for j in range(4):
                        jj = rnd * 4 + j
                        tok = bld.op("tensor", lambda e, j=j, jj=jj: e.transpose(
                            out=psEb[:, j * 128:(j + 1) * 128],
                            in_=xhat[:, jj * 128:(jj + 1) * 128],
                            identity=ident32[:]),
                            reads=("xhat", "ident32") + (("psEb",) if j == 0 else ()),
                            writes=("psEb",) if j == 0 else (), token=(j == 3))
                    bld.settok(["psEb"], tok)
                    bld.op("scalar", lambda e, ti=ti, rnd=rnd, xq3=xq3: e.activation(
                           xq3[:, rnd * 4:(rnd + 1) * 4, ti * 128:(ti + 1) * 128],
                           psEb[:].rearrange("p (c n) -> p c n", c=4), AF.Identity),
                           reads=("psEb",), writes=(f"xqT{tg % 2}",))

        def emit_gateup(tg):
            xq3 = xqT[tg % 2][:].rearrange("p (c n) -> p c n", c=8)
            xqk = f"xqT{tg % 2}"
            for ic in range(32):
                if ic % 2 == 0:
                    drain_hT(2)
                bview, pk = gbanks[ic % 4]
                for dc in range(8):
                    bld.op("tensor", lambda e, ic=ic, dc=dc, bview=bview, xq3=xq3: e.matmul(
                           bview[:, 0:TG], wg83[:, dc, ic * 128:(ic + 1) * 128],
                           xq3[:, dc, :], start=(dc == 0), stop=(dc == 7)),
                           reads=tuple(f"wg8:{i}" for i in range(4)) + (xqk,) + ((pk,) if dc == 0 else ()),
                           writes=(pk,) if dc == 0 else (), token=False)
                for dc in range(8):
                    tok = bld.op("tensor", lambda e, ic=ic, dc=dc, bview=bview, xq3=xq3: e.matmul(
                           bview[:, TG:2 * TG], wu83[:, dc, ic * 128:(ic + 1) * 128],
                           xq3[:, dc, :], start=(dc == 0), stop=(dc == 7)),
                           reads=tuple(f"wu8:{i}" for i in range(4)) + (xqk,), writes=(), token=(dc == 7))
                bld.settok([pk], tok)
                gk = f"gsil{ic % 2}"
                bld.op("scalar", lambda e, ic=ic, bview=bview: e.activation(
                       gsil[ic % 2][:], bview[:, 0:TG], AF.Silu,
                       scale=scal[:, SC_SWG:SC_SWG + 1]),
                       reads=(pk, f"sc:{SC_SWG}"), writes=(gk,))
                bld.op("vector", lambda e, ic=ic, bview=bview: e.scalar_tensor_tensor(
                       gu[:, ic * TG:(ic + 1) * TG], bview[:, TG:2 * TG],
                       scal[:, SC_SWU:SC_SWU + 1], gsil[ic % 2][:], ALU.mult, ALU.mult),
                       reads=(pk, f"sc:{SC_SWU}", gk), writes=(f"gu:{ic}",))
                if ic == 0:
                    bld.op("scalar", lambda e: e.activation(Mx[:], gu[:, 0:TG], AF.Abs),
                           reads=("gu:0",), writes=("Mx",))
                else:
                    gak = f"gabs{ic % 2}"
                    bld.op("scalar", lambda e, ic=ic: e.activation(
                           gabs[ic % 2][:], gu[:, ic * TG:(ic + 1) * TG], AF.Abs),
                           reads=(f"gu:{ic}",), writes=(gak,))
                    bld.op("vector", lambda e, ic=ic: e.tensor_max(Mx[:], Mx[:],
                           gabs[ic % 2][:]), reads=(gak, "Mx"), writes=("Mx",))

        def emit_quant_h(tg):
            for ti in range(2):
                tok = bld.op("tensor", lambda e, ti=ti: e.transpose(
                       out=psW1[:, 512 + ti * 128:512 + (ti + 1) * 128],
                       in_=Mx[:, ti * 128:(ti + 1) * 128], identity=ident32[:]),
                       reads=("Mx", "ident32") + (("psW1b",) if ti == 0 else ()),
                       writes=("psW1b",) if ti == 0 else (), token=True)
            bld.settok(["psW1b"], tok)
            for ti in range(2):
                bld.op("vector", lambda e, ti=ti: e.reduce_max(dcols[:, 4 + ti:5 + ti],
                       psW1[:, 512 + ti * 128:512 + (ti + 1) * 128], axis=AX.X),
                       reads=("psW1b",), writes=(f"d_m2:{ti}",))
                bld.op("vector", lambda e, ti=ti: e.tensor_scalar(dcols[:, 4 + ti:5 + ti],
                       dcols[:, 4 + ti:5 + ti], 1e-5, None, ALU.max),
                       reads=(f"d_m2:{ti}",), writes=(f"d_m2:{ti}",))
                bld.op("vector", lambda e, ti=ti: e.reciprocal(dcols[:, 8 + ti:9 + ti],
                       dcols[:, 4 + ti:5 + ti]), reads=(f"d_m2:{ti}",), writes=(f"d_rm2:{ti}",))
                bld.op("vector", lambda e, ti=ti: e.tensor_scalar(dcols[:, 10 + ti:11 + ti],
                       dcols[:, 8 + ti:9 + ti], 127.0, None, ALU.mult),
                       reads=(f"d_rm2:{ti}",), writes=(f"d_s2:{ti}",))
                bld.op("vector", lambda e, ti=ti: e.scalar_tensor_tensor(
                       dcols[:, 6 + ti:7 + ti], dcols[:, 4 + ti:5 + ti],
                       float(1.0 / 127.0), scal[:, SC_SWD:SC_SWD + 1], ALU.mult, ALU.mult),
                       reads=(f"d_m2:{ti}", f"sc:{SC_SWD}"), writes=(f"dcmb:{ti}",))
            for ti in range(2):
                tok = bld.op("tensor", lambda e, ti=ti: e.transpose(
                       out=psW1[0:1, 512 + 256 + ti * 128:512 + 256 + (ti + 1) * 128],
                       in_=dcols[:, 10 + ti:11 + ti], identity=ident32[:]),
                       reads=(f"d_s2:{ti}", "ident32") + (("psW1b",) if ti == 0 else ()),
                       writes=("psW1b",) if ti == 0 else (), token=True)
            bld.settok(["psW1b"], tok)
            bld.op("vector", lambda e: e.tensor_copy(s2row[0:1, 0:TG], psW1[0:1, 768:768 + TG]),
                   reads=("psW1b",), writes=("s2row",))
            bld.op("vector", lambda e: e.tensor_copy(s2row[0:1, TG:2 * TG], s2row[0:1, 0:TG]),
                   reads=("s2row",), writes=("s2row",))
            bld.op("tensor", lambda e: e.matmul(psW1[:, 512:1024], onesrow[:],
                   s2row[:], start=True, stop=True),
                   reads=("s2row", "onesrow"), writes=("psW1b",))
            bld.op("scalar", lambda e: e.activation(s2b[:], psW1[:, 512:1024],
                   AF.Identity), reads=("psW1b",), writes=("s2b",))

        pending_hT = []

        def emit_quant_hT(tg):
            for ic in range(0, 32, 2):
                def th1(ic=ic):
                    bld.op("vector", lambda e, ic=ic: e.tensor_mul(
                           gu[:, ic * TG:(ic + 2) * TG],
                           gu[:, ic * TG:(ic + 2) * TG], s2b[:]),
                           reads=(f"gu:{ic}", f"gu:{ic + 1}", "s2b"),
                           writes=(f"gu:{ic}", f"gu:{ic + 1}"))
                def th2(ic=ic):
                    bld.op("vector", lambda e, ic=ic: e.tensor_scalar(
                           hT[:, ic * TG:(ic + 2) * TG], gu[:, ic * TG:(ic + 2) * TG],
                           mcol[:], mcol[:], ALU.add, ALU.subtract),
                           reads=(f"gu:{ic}", f"gu:{ic + 1}", "mcol"), writes=("hT",))
                pending_hT.append(th1)
                pending_hT.append(th2)

        def drain_hT(k=1):
            for _ in range(k):
                if pending_hT:
                    pending_hT.pop(0)()

        def emit_down(tg):
            for ti in range(2):
                gt = tg * 2 + ti
                xk = f"x1in{gt % 6}"
                ob = outb[gt % 2]
                obk = f"outb{gt % 2}"
                for half in range(2):
                    bview, pk = dbanks[half]
                    for ic in range(32):
                        tok = bld.op("tensor", lambda e, ic=ic, ti=ti, half=half, bview=bview:
                            e.matmul(bview[:],
                                     hT[:, ic * TG + ti * 128:ic * TG + (ti + 1) * 128],
                                     wd83[:, ic, half * 512:(half + 1) * 512],
                                     start=(ic == 0), stop=(ic == 31)),
                            reads=("hT",) + tuple(f"wd8:{i}" for i in range(4)) + ((pk,) if ic == 0 else ()),
                            writes=(pk,) if ic == 0 else (), token=(ic == 31))
                    bld.settok([pk], tok)
                    bld.op("vector", lambda e, ti=ti, half=half, bview=bview, ob=ob, gt=gt:
                        e.scalar_tensor_tensor(ob[:, half * 512:(half + 1) * 512],
                            bview[:], dcols[:, 6 + ti:7 + ti],
                            x1in[gt % 6][:, half * 512:(half + 1) * 512],
                            ALU.mult, ALU.add),
                        reads=(pk, f"dcmb:{ti}", xk), writes=(obk,))
                bld.op("sync", lambda e, gt=gt, ob=ob: e.dma_start(out=y_t[gt], in_=ob[:]),
                       reads=(obk,), q="QS")

        emit_lnxq(0)
        emit_lnxq(1)
        for qn, (dst, srcd, tag) in enumerate(
                [(wg8, wg8d, "g"), (wu8, wu8d, "u"), (wd8, wd8d, "d")]):
            W = dst.shape[1]
            for qq in range(4):
                lo, hi = qq * (W // 4), (qq + 1) * (W // 4)
                bld.op("sync", lambda e, dst=dst, srcd=srcd, lo=lo, hi=hi:
                       e.dma_start(out=dst[:, lo:hi], in_=srcd[:, lo:hi]),
                       reads=(f"w8d:{tag}",), writes=(f"w{tag}8:{qq}",), q="QS")
        for tg in range(NTG):
            emit_gateup(tg)
            if tg > 0:
                emit_down(tg - 1)
            emit_quant_h(tg)
            if tg + 2 < NTG:
                emit_lnxq(tg + 2)
            emit_quant_hT(tg)
        drain_hT(32)
        emit_down(NTG - 1)
    except _Stop:
        pass

    for eng in ("tensor", "vector", "scalar", "gpsimd"):
        bld.op(eng, lambda e: e.nop(), token=True)
    for eng in ("tensor", "scalar"):
        sem = bld.semof[eng]
        v = bld.cnt[sem]
        bld.plan[eng].append(lambda e, sem=sem, v=v: e.wait_ge(bld.sems[sem], v))
    for i in range(8):
        v = bld.cnt[f"QS{i}"]
        if v:
            bld.plan["sync"].append(lambda e, i=i, v=v: e.wait_ge(bld.sems[f"QS{i}"], v))
    for i in range(2):
        v = bld.cnt[f"QG{i}"]
        if v:
            bld.plan["sync"].append(lambda e, i=i, v=v: e.wait_ge(bld.sems[f"QG{i}"], v))

    import contextlib
    with contextlib.ExitStack() as stk:
        sems = {}
        for nm in list(bld.cnt.keys()):
            sems[nm] = stk.enter_context(nc.semaphore("s" + nm))
        blk = stk.enter_context(nc.Block())
        bld.sems = sems

        @blk.sync
        def _(e):
            for f in bld.plan["sync"]:
                f(e)

        @blk.tensor
        def _(e):
            for f in bld.plan["tensor"]:
                f(e)

        @blk.vector
        def _(e):
            for f in bld.plan["vector"]:
                f(e)

        @blk.scalar
        def _(e):
            for f in bld.plan["scalar"]:
                f(e)

        @blk.gpsimd
        def _(e):
            for f in bld.plan["gpsimd"]:
                f(e)

    while alive:
        cm, h = alive.pop()
        cm.__exit__(None, None, None)
    for cm in reversed(psum_cm):
        cm.__exit__(None, None, None)
    return nc


_NC = None


def kernel(L0_attn_x, L0_attn_Wq, L0_attn_bq, L0_attn_Wk, L0_attn_bk,
           L0_attn_Wv, L0_attn_bv, L0_attn_Wo, L0_attn_bo,
           L0_ffn_w_gate, L0_ffn_w_up, L0_ffn_w_down):
    global _NC
    if _NC is None:
        _NC = _build()
    nc = _NC
    f32 = np.float32
    x = np.ascontiguousarray(L0_attn_x, dtype=f32)
    WqT = np.ascontiguousarray(np.asarray(L0_attn_Wq, f32).T)
    WkT = np.ascontiguousarray(np.asarray(L0_attn_Wk, f32).T)
    WvT = np.ascontiguousarray(np.asarray(L0_attn_Wv, f32).T)
    WoT = np.ascontiguousarray(np.asarray(L0_attn_Wo, f32).T)
    bq_r = np.ascontiguousarray(np.asarray(L0_attn_bq, f32).reshape(8, 128).T)
    bk_r = np.ascontiguousarray(np.asarray(L0_attn_bk, f32).reshape(8, 128).T)
    bv_r = np.ascontiguousarray(np.asarray(L0_attn_bv, f32).reshape(8, 128).T)
    bo_r = np.ascontiguousarray(np.asarray(L0_attn_bo, f32).reshape(1, D))
    wgT = np.ascontiguousarray(np.asarray(L0_ffn_w_gate, f32).T)
    wuT = np.ascontiguousarray(np.asarray(L0_ffn_w_up, f32).T)
    wdT = np.ascontiguousarray(np.asarray(L0_ffn_w_down, f32).T)

    in_maps = []
    for core in range(NCORE):
        b, par = core // 2, core % 2
        blocks = [par + 2 * i for i in range(NQB)]
        x_own = np.ascontiguousarray(
            np.concatenate([x[b, qb * 512:(qb + 1) * 512] for qb in blocks], axis=0))
        x_kv = np.ascontiguousarray(x[b, :KVT])
        in_maps.append({
            "x_own": x_own, "x_kv": x_kv,
            "WqT": WqT, "WkT": WkT, "WvT": WvT, "WoT": WoT,
            "bq_r": bq_r, "bk_r": bk_r, "bv_r": bv_r, "bo_r": bo_r,
            "wgT": wgT, "wuT": wuT, "wdT": wdT,
        })
    res = run_bass_kernel_spmd(nc, in_maps, core_ids=list(range(NCORE)))
    out = np.empty((B, L, D), dtype=f32)
    for core in range(NCORE):
        b, par = core // 2, core % 2
        yv = res.results[core]["y"]
        for i in range(NQB):
            qb = par + 2 * i
            out[b, qb * 512:(qb + 1) * 512] = yv[i * 512:(i + 1) * 512]
    return out


# version 27
# speedup vs baseline: 1.0259x; 1.0004x over previous
import os
import numpy as np
import ml_dtypes
import concourse.bass as bass
import concourse.mybir as mybir
from concourse.bass_utils import run_bass_kernel_spmd

F32 = mybir.dt.float32
BF16 = mybir.dt.bfloat16
FP8 = mybir.dt.float8e4
AF = mybir.ActivationFunctionType
ALU = mybir.AluOpType
AX = mybir.AxisListType

B, L, D, H, HD, INNER = 4, 4096, 1024, 16, 64, 4096
NCORE = 8
NOWN = 2048
TG = 256
NTG = NOWN // TG
NQB = 4

_EXP = [(-4.0 + h * (3.7 / 15.0)) for h in range(H)]
SLOPES64 = [float(2.0 ** e) for e in _EXP]
KEPT = [min(int(np.ceil(130.0 / s / 128.0)) * 128, L) for s in SLOPES64]
KVT = max(KEPT)
NKVT = KVT // 128
NOT = NOWN // 128
NKB = [k // 128 for k in KEPT]
BOFF = np.concatenate([[0], np.cumsum(NKB)]).astype(int)
TOTKB = int(BOFF[-1])
KEPT_E = [KEPT[2 * c] for c in range(8)]
KOFF = np.concatenate([[0], np.cumsum(KEPT_E)]).astype(int)
KCOLS = int(KOFF[-1])
NHT = [sum(1 for h in range(H) if KEPT[h] > 128 * t) for t in range(NKVT)]
VOFF = np.concatenate([[0], np.cumsum([65 * n for n in NHT])]).astype(int)
VCOLS = int(VOFF[-1])

TINY = float(2.0 ** -26)
MAGIC = float(1.5 * 2 ** 23)
MSCALE = float(1.5 * 2 ** 78)
EPS_SOFT = 1e-5
LN2_64 = float(np.log(np.float64(2.0)))


def _dual(c64):
    c32 = float(np.float32(c64))
    corr = float(np.float32((c64 - c32) / TINY))
    return c32, corr


def _build():
    nc = bass.Bass("TRN2", target_bir_lowering=False)
    x_own = nc.dram_tensor("x_own", [NOWN, D], F32, kind="ExternalInput").ap()
    x_kv = nc.dram_tensor("x_kv", [KVT, D], F32, kind="ExternalInput").ap()
    WqT = nc.dram_tensor("WqT", [D, D], F32, kind="ExternalInput").ap()
    WkT = nc.dram_tensor("WkT", [D, D], F32, kind="ExternalInput").ap()
    WvT = nc.dram_tensor("WvT", [D, D], F32, kind="ExternalInput").ap()
    WoT = nc.dram_tensor("WoT", [D, D], F32, kind="ExternalInput").ap()
    bq_r = nc.dram_tensor("bq_r", [128, 8], F32, kind="ExternalInput").ap()
    bk_r = nc.dram_tensor("bk_r", [128, 8], F32, kind="ExternalInput").ap()
    bv_r = nc.dram_tensor("bv_r", [128, 8], F32, kind="ExternalInput").ap()
    bo_r = nc.dram_tensor("bo_r", [1, D], F32, kind="ExternalInput").ap()
    wgT = nc.dram_tensor("wgT", [D, INNER], F32, kind="ExternalInput").ap()
    wuT = nc.dram_tensor("wuT", [D, INNER], F32, kind="ExternalInput").ap()
    wdT = nc.dram_tensor("wdT", [INNER, D], F32, kind="ExternalInput").ap()
    y = nc.dram_tensor("y", [NOWN, D], F32, kind="ExternalOutput").ap()
    x1scr = nc.dram_tensor("x1scr", [NOWN, D], F32, kind="Internal").ap()
    wg8d = nc.dram_tensor("wg8d", [128, 8 * 4096], FP8, kind="Internal").ap()
    wu8d = nc.dram_tensor("wu8d", [128, 8 * 4096], FP8, kind="Internal").ap()
    wd8d = nc.dram_tensor("wd8d", [128, 32 * 1024], FP8, kind="Internal").ap()

    xo_t = x_own.rearrange("(t p) d -> t p d", p=128)
    xk_t = x_kv.rearrange("(t p) d -> t p d", p=128)
    y_t = y.rearrange("(t p) d -> t p d", p=128)
    x1_t = x1scr.rearrange("(t p) d -> t p d", p=128)
    wgT_c = wgT.rearrange("(c p) n -> p c n", p=128)
    wuT_c = wuT.rearrange("(c p) n -> p c n", p=128)
    wdT_c = wdT.rearrange("(c p) n -> p c n", p=128)
    WT_c = {"q": WqT.rearrange("(c p) n -> p c n", p=128),
            "k": WkT.rearrange("(c p) n -> p c n", p=128),
            "v": WvT.rearrange("(c p) n -> p c n", p=128),
            "o": WoT.rearrange("(c p) n -> p c n", p=128)}

    class Bld:
        def __init__(self):
            self.plan = {"sync": [], "tensor": [], "vector": [], "scalar": [], "gpsimd": []}
            self.cnt = {"PE": 0, "DVE": 0, "ACT": 0, "POOL": 0}
            for i in range(8):
                self.cnt[f"QS{i}"] = 0
            for i in range(2):
                self.cnt[f"QG{i}"] = 0
            self.rr = {"QS": 0, "QG": 0}
            self.semof = {"tensor": "PE", "vector": "DVE", "scalar": "ACT", "gpsimd": "POOL"}
            self.waited = {}
            self.lastw = {}
            self.readers = {}
            self.floor = {}
            self.sems = None
            self.wsince = {}
            self.lasttok = {}

        def _wait(self, eng, sem, val):
            if val <= 0:
                return
            if self.waited.get((eng, sem), 0) >= val:
                return
            self.waited[(eng, sem)] = val
            self.wsince[eng] = True
            def f(e, sem=sem, val=val):
                e.wait_ge(self.sems[sem], val)
            self.plan[eng].append(f)

        def barrier(self):
            for s, v in self.cnt.items():
                self.floor[s] = v

        def dep(self, eng, reads=(), war=()):
            need = dict(self.floor)
            for k in reads:
                t = self.lastw.get(k)
                if t is not None:
                    assert t != "PENDING", f"read of in-flight group buffer {k}"
                    need[t[0]] = max(need.get(t[0], 0), t[1])
            for k in war:
                t = self.lastw.get(k)
                if t is not None and t != "PENDING":
                    need[t[0]] = max(need.get(t[0], 0), t[1])
                for (s, v) in self.readers.get(k, ()):
                    need[s] = max(need.get(s, 0), v)
            for s, v in need.items():
                self._wait(eng, s, v)

        def op(self, eng, fn, reads=(), writes=(), token=True, q=None):
            self.dep(eng, reads, war=writes)
            if q:
                nl = 8 if q == "QS" else 2
                lane = self.rr[q]
                self.rr[q] = (lane + 1) % nl
                sem = f"{q}{lane}"
                self._wait(eng, sem, self.cnt[sem])
            else:
                sem = self.semof[eng]
            if token:
                inc = 16 if q else 1
                if q is None and eng in ("tensor", "scalar"):
                    if not self.wsince.get(eng, True):
                        pv = self.lasttok[eng]
                        if self.waited.get((eng, sem), 0) < pv:
                            self.waited[(eng, sem)] = pv
                            def fw(e, sem=sem, pv=pv):
                                e.wait_ge(self.sems[sem], pv)
                            self.plan[eng].append(fw)
                    self.cnt[sem] += inc
                    tok = (sem, self.cnt[sem])
                    self.wsince[eng] = False
                    self.lasttok[eng] = self.cnt[sem]
                    def f(e, fn=fn, sem=sem, inc=inc):
                        fn(e).then_inc(self.sems[sem], inc)
                elif q is None:
                    self.cnt[sem] += inc
                    tok = (sem, self.cnt[sem])
                    val = self.cnt[sem]
                    def f(e, fn=fn, sem=sem, inc=inc, val=val):
                        fn(e).then_inc(self.sems[sem], inc)
                        e.wait_ge(self.sems[sem], val)
                    self.waited[(eng, sem)] = val
                    self.wsince[eng] = True
                else:
                    self.cnt[sem] += inc
                    tok = (sem, self.cnt[sem])
                    def f(e, fn=fn, sem=sem, inc=inc):
                        fn(e).then_inc(self.sems[sem], inc)
            else:
                tok = (sem, self.cnt[sem] + 1)
                def f(e, fn=fn):
                    fn(e)
            self.plan[eng].append(f)
            for k in reads:
                self.readers.setdefault(k, []).append(tok)
            for k in writes:
                self.lastw[k] = tok if token else "PENDING"
                self.readers[k] = []
            return tok

        def settok(self, keys, tok):
            for k in keys:
                self.lastw[k] = tok

    bld = Bld()
    alive = []

    def sb(name, shape, dtype):
        cm = nc.sbuf_tensor(name, shape, dtype)
        h = cm.__enter__()
        alive.append((cm, h))
        return h

    def sb_free_back_to(h):
        idx = next(i for i, (cm, hh) in enumerate(alive) if hh is h)
        while len(alive) > idx:
            cm, hh = alive.pop()
            cm.__exit__(None, None, None)

    ident16 = sb("ident16", [128, 128], BF16)
    ident32 = sb("ident32", [128, 128], F32)
    kidx = sb("kidx", [128, max(NKB)], F32)
    bias_tbl = sb("bias_tbl", [128, TOTKB], F32)
    onescol = sb("onescol", [128, 1], F32)
    onesrow = sb("onesrow", [1, 128], F32)
    onesrow16 = sb("onesrow16", [1, 128], BF16)
    fcell = sb("fcell", [128, 1], F32)
    epscol = sb("epscol", [128, 1], F32)
    mcol = sb("mcol", [128, 1], F32)
    negm = sb("negm", [128, 1], F32)
    m2col = sb("m2col", [128, 1], F32)
    soft_lo = sb("soft_lo", [128, 1], F32)
    soft_hi = sb("soft_hi", [128, 1], F32)
    lneps = sb("lneps", [128, 1], F32)
    bq_sb = sb("bq_sb", [128, 8], F32)
    bk_sb = sb("bk_sb", [128, 8], F32)
    bv_sb = sb("bv_sb", [128, 8], F32)
    bo16 = sb("bo16", [1, D], BF16)
    scal = sb("scal", [128, 20], F32)
    LN1C = sb("LN1C", [128, 2 * (NKVT + NOT)], F32)
    LN2C = sb("LN2C", [128, 2 * NOT], F32)
    slope_c = sb("slope_c", [128, 1], F32)
    SC_SG, SC_SWG, SC_SU, SC_SWU, SC_SD, SC_SWD = 0, 1, 2, 3, 4, 5
    SC_T0, SC_T1 = 6, 7

    psum_cm = []
    def psb(name, shape, dtype):
        cm = nc.psum_tensor(name, shape, dtype)
        h = cm.__enter__()
        psum_cm.append(cm)
        return h
    psW0 = psb("psW0", [128, 1024], F32)
    psW1 = psb("psW1", [128, 1024], F32)
    ps4 = psb("ps4", [128, 512], F32)
    ps5 = psb("ps5", [128, 512], F32)
    psE = psb("psE", [128, 1024], F32)
    psEa = psE[:, 0:512]
    psEb = psE[:, 512:1024]

    bld.op("gpsimd", lambda e: e.memset(ident16[:], 0.0), writes=("ident16",))
    bld.op("gpsimd", lambda e: e.affine_select(out=ident16[:], in_=ident16[:],
           compare_op=ALU.not_equal, fill=1.0, base=0, pattern=[[-1, 128]],
           channel_multiplier=1), reads=("ident16",), writes=("ident16",))
    bld.op("gpsimd", lambda e: e.nop(), reads=("ident16",))
    bld.op("gpsimd", lambda e: e.iota(kidx[:], [[128, max(NKB)]], channel_multiplier=1,
           allow_small_or_imprecise_dtypes=True), writes=("kidx",))
    bld.op("vector", lambda e: e.memset(onescol[:], 1.0), writes=("onescol",))
    bld.op("vector", lambda e: e.memset(lneps[:], 1e-5), writes=("lneps",))
    bld.op("vector", lambda e: e.memset(onesrow[:], 1.0), writes=("onesrow",))
    bld.op("vector", lambda e: e.memset(onesrow16[:], 1.0), writes=("onesrow16",))
    bld.op("vector", lambda e: e.tensor_scalar(fcell[:], onescol[:], TINY, 1.0,
           ALU.add, ALU.subtract), reads=("onescol",), writes=("fcell",))
    bld.op("vector", lambda e: e.tensor_scalar(epscol[:], fcell[:], -1.0, TINY,
           ALU.mult, ALU.add), reads=("fcell",), writes=("epscol",))
    bld.op("vector", lambda e: e.tensor_scalar(mcol[:], fcell[:], MSCALE, MAGIC,
           ALU.mult, ALU.add), reads=("fcell",), writes=("mcol",))
    bld.op("vector", lambda e: e.tensor_scalar(negm[:], mcol[:], -1.0, None,
           ALU.mult), reads=("mcol",), writes=("negm",))
    bld.op("vector", lambda e: e.tensor_scalar(m2col[:], mcol[:], 2.0, None,
           ALU.mult), reads=("mcol",), writes=("m2col",))
    _esc = float(EPS_SOFT / TINY)
    bld.op("vector", lambda e: e.tensor_scalar(soft_lo[:], epscol[:], -_esc, 1.0,
           ALU.mult, ALU.add), reads=("epscol",), writes=("soft_lo",))
    bld.op("vector", lambda e: e.tensor_scalar(soft_hi[:], epscol[:], _esc, 1.0,
           ALU.mult, ALU.add), reads=("epscol",), writes=("soft_hi",))
    bld.op("vector", lambda e: e.tensor_copy(ident32[:], ident16[:]),
           reads=("ident16",), writes=("ident32",))
    bld.op("sync", lambda e: e.dma_start(out=bq_sb[:], in_=bq_r), writes=("bq_sb",), q="QS")
    bld.op("sync", lambda e: e.dma_start(out=bk_sb[:], in_=bk_r), writes=("bk_sb",), q="QS")
    bld.op("sync", lambda e: e.dma_start(out=bv_sb[:], in_=bv_r), writes=("bv_sb",), q="QS")


    for h in range(H):
        c32, corr = _dual(_EXP[h] * LN2_64)
        bld.op("vector", lambda e, c32=c32, corr=corr: e.tensor_scalar(
            scal[0:1, SC_T0:SC_T0 + 1], fcell[0:1, :], corr, c32, ALU.mult, ALU.add),
            reads=("fcell",), writes=("sc_t0",))
        bld.op("scalar", lambda e: e.activation(scal[0:1, SC_T1:SC_T1 + 1],
            scal[0:1, SC_T0:SC_T0 + 1], AF.Exp), reads=("sc_t0",), writes=("sc_t1",))
        bld.op("vector", lambda e: e.tensor_scalar(scal[0:1, SC_T0:SC_T0 + 1],
            scal[0:1, SC_T1:SC_T1 + 1], -1.0, None, ALU.mult),
            reads=("sc_t1",), writes=("sc_t0",))
        bld.op("tensor", lambda e: e.matmul(psEa[:, 0:1], onesrow[:],
            scal[0:1, SC_T0:SC_T0 + 1], start=True, stop=True),
            reads=("onesrow", "sc_t0"), writes=("psEa",))
        bld.op("vector", lambda e: e.tensor_copy(slope_c[:], psEa[:, 0:1]),
            reads=("psEa",), writes=("slope_c",))
        nkb, b0 = NKB[h], int(BOFF[h])
        bld.op("vector", lambda e, nkb=nkb, b0=b0: e.tensor_scalar(
            bias_tbl[:, b0:b0 + nkb], kidx[:, 0:nkb], slope_c[:], None, ALU.mult),
            reads=("kidx", "slope_c"), writes=(f"bias:{h}",))

    KPH = int(os.environ.get("KPH", "4"))

    class _Stop(Exception):
        pass
    try:
        if KPH < 1:
            raise _Stop()
        XLT = sb("XLT", [128, 8 * KVT], BF16)
        bo_f = sb("bo_f", [1, D], F32)
        bld.op("sync", lambda e: e.dma_start(out=bo_f[:], in_=bo_r), writes=("bo_f",), q="QS")
        bld.op("vector", lambda e: e.tensor_copy(bo16[:], bo_f[:]), reads=("bo_f",), writes=("bo16",))
        xlt_kv = XLT[:].rearrange("p (c n) -> p c n", c=8)
        KT = sb("KT", [128, KCOLS], BF16)
        VT = sb("VT", [128, VCOLS], BF16)
        Wk16 = sb("Wk16", [128, 8 * D], BF16)
        Wv16 = sb("Wv16", [128, 8 * D], BF16)
        stg = [sb(f"stg{i}", [128, 2048], F32) for i in range(2)]
        xin = [sb(f"xin{i}", [128, D], F32) for i in range(4)]
        sq = sb("sq", [128, D], F32)
        cols = sb("cols", [128, 12], F32)
        mean_acc = sb("mean_acc", [128, 64], F32)

        def ln_stats(src_key, src_ap, slot, colarr, ckey):
            p6 = (slot % 2) * 6
            cs = lambda i: cols[:, p6 + i:p6 + i + 1]
            kc = lambda i: f"c{slot % 2}_{i}"
            bld.op("vector", lambda e, cs=cs: e.reduce_sum(cs(0), src_ap, axis=AX.X),
                   reads=(src_key,), writes=(kc(0),))
            bld.op("scalar", lambda e, cs=cs: e.activation(sq[:], src_ap, AF.Square,
                   accum_out=cs(1)), reads=(src_key,), writes=("sq", kc(1)))
            bld.op("vector", lambda e, cs=cs: e.tensor_scalar(cs(2), cs(0),
                   float(1.0 / D), None, ALU.mult), reads=(kc(0),), writes=(kc(2),))
            bld.op("vector", lambda e, cs=cs: e.tensor_mul(cs(3), cs(2), cs(2)),
                   reads=(kc(2),), writes=(kc(3),))
            bld.op("vector", lambda e, cs=cs: e.scalar_tensor_tensor(cs(4), cs(1),
                   float(1.0 / D), cs(3), ALU.mult, ALU.subtract),
                   reads=(kc(1), kc(3)), writes=(kc(4),))
            bld.op("scalar", lambda e, cs=cs: e.activation(cs(5), cs(4), AF.Ln,
                   bias=lneps[:]), reads=(kc(4), "lneps"), writes=(kc(5),))
            bld.op("scalar", lambda e, cs=cs, colarr=colarr, slot=slot: e.activation(
                   colarr[:, 2 * slot:2 * slot + 1], cs(5), AF.Exp, scale=-0.5),
                   reads=(kc(5),), writes=(ckey + ":r",))
            bld.op("vector", lambda e, cs=cs, colarr=colarr, slot=slot:
                   e.scalar_tensor_tensor(colarr[:, 2 * slot + 1:2 * slot + 2], cs(2),
                   -1.0, colarr[:, 2 * slot:2 * slot + 1], ALU.mult, ALU.mult),
                   reads=(kc(2), ckey + ":r"), writes=(ckey + ":m",))

        NT_ALL = NKVT + NOT
        astat = sb("astat", [128, 4 * NT_ALL], F32)

        def ln_sum2(src_key, src_ap, slot):
            bld.op("vector", lambda e, src_ap=src_ap, slot=slot: e.reduce_sum(
                   astat[:, slot:slot + 1], src_ap, axis=AX.X),
                   reads=(src_key,), writes=(f"as:{slot}",))
            bld.op("scalar", lambda e, src_ap=src_ap, slot=slot: e.activation(
                   sq[:], src_ap, AF.Square,
                   accum_out=astat[:, NT_ALL + slot:NT_ALL + slot + 1]),
                   reads=(src_key,), writes=("sq", f"aq:{slot}"))

        def ln_batch(b, n):
            A2, A3c = 2 * NT_ALL, 3 * NT_ALL
            rs = tuple(f"as:{s}" for s in range(b, b + n))
            rq = tuple(f"aq:{s}" for s in range(b, b + n))
            bld.op("vector", lambda e, b=b, n=n: e.tensor_scalar(
                   astat[:, A2 + b:A2 + b + n], astat[:, b:b + n],
                   float(1.0 / D), None, ALU.mult),
                   reads=rs, writes=(f"am:{b}",))
            bld.op("vector", lambda e, b=b, n=n: e.tensor_mul(
                   astat[:, A3c + b:A3c + b + n], astat[:, A2 + b:A2 + b + n],
                   astat[:, A2 + b:A2 + b + n]),
                   reads=(f"am:{b}",), writes=(f"am2:{b}",))
            bld.op("vector", lambda e, b=b, n=n: e.scalar_tensor_tensor(
                   astat[:, b:b + n], astat[:, NT_ALL + b:NT_ALL + b + n],
                   float(1.0 / D), astat[:, A3c + b:A3c + b + n],
                   ALU.mult, ALU.subtract),
                   reads=rq + (f"am2:{b}",), writes=(f"av:{b}",))
            bld.op("scalar", lambda e, b=b, n=n: e.activation(
                   astat[:, NT_ALL + b:NT_ALL + b + n], astat[:, b:b + n],
                   AF.Ln, bias=lneps[:]),
                   reads=(f"av:{b}", "lneps"), writes=(f"al:{b}",))
            bld.op("scalar", lambda e, b=b, n=n: e.activation(
                   LN1C[:, b:b + n], astat[:, NT_ALL + b:NT_ALL + b + n],
                   AF.Exp, scale=-0.5),
                   reads=(f"al:{b}",), writes=(f"ar:{b}",))
            bld.op("vector", lambda e, b=b, n=n: e.scalar_tensor_tensor(
                   LN1C[:, NT_ALL + b:NT_ALL + b + n], astat[:, A2 + b:A2 + b + n],
                   -1.0, LN1C[:, b:b + n], ALU.mult, ALU.mult),
                   reads=(f"am:{b}", f"ar:{b}"), writes=(f"an:{b}",))

        def ln_apply_t(src_key, src_ap, slot, bkey, dst3, dstkey, tslot):
            lb = xin[(tslot + 2) % 4]
            lbk = f"xin{(tslot + 2) % 4}"
            bld.op("vector", lambda e, lb=lb, slot=slot: e.tensor_scalar(
                   lb[:], src_ap, LN1C[:, slot:slot + 1],
                   LN1C[:, NT_ALL + slot:NT_ALL + slot + 1], ALU.mult, ALU.add),
                   reads=(src_key, f"ar:{bkey}", f"an:{bkey}"), writes=(lbk,))
            for j in range(8):
                tok = bld.op("tensor", lambda e, j=j, lb=lb: e.transpose(
                    out=psE[:, j * 128:(j + 1) * 128], in_=lb[:, j * 128:(j + 1) * 128],
                    identity=ident32[:]),
                    reads=(lbk, "ident32") + (("psEa", "psEb") if j == 0 else ()),
                    writes=("psEa", "psEb") if j == 0 else (), token=(j == 7))
            bld.settok(["psEa", "psEb"], tok)
            bld.op("scalar", lambda e, dst3=dst3, tslot=tslot: e.activation(
                   dst3[:, :, tslot * 128:(tslot + 1) * 128],
                   psE[:].rearrange("p (c n) -> p c n", c=8), AF.Identity),
                   reads=("psEa", "psEb"), writes=(dstkey,))

        mean_jobs = []
        for ci in range(16):
            c, hcol = ci // 2, ci % 2
            mean_jobs.append((wgT_c[:, c, hcol * 2048:(hcol + 1) * 2048], ci))
        for ci in range(16):
            c, hcol = ci // 2, ci % 2
            mean_jobs.append((wuT_c[:, c, hcol * 2048:(hcol + 1) * 2048], 16 + ci))
        for ci in range(32):
            mean_jobs.append((wdT_c[:, ci, :], 32 + ci))
        mj_i = [0]

        ml_i = [0]

        def emit_mean_load():
            if ml_i[0] >= len(mean_jobs):
                return
            src, _ = mean_jobs[ml_i[0]]
            sgi = ml_i[0] % 2
            ml_i[0] += 1
            n = src.shape[-1]
            sk = f"stg{sgi}"
            h2 = n // 2
            bld.op("sync", lambda e, src=src, sgi=sgi, h2=h2: e.dma_start(
                   out=stg[sgi][:, 0:h2], in_=src[:, 0:h2]),
                   writes=(sk,), q="QS")
            bld.op("sync", lambda e, src=src, sgi=sgi, h2=h2, n=n: e.dma_start(
                   out=stg[sgi][:, h2:n], in_=src[:, h2:n]),
                   writes=(sk + "h",), q="QS")

        def emit_mean_step():
            if mj_i[0] >= len(mean_jobs):
                return
            while ml_i[0] <= min(mj_i[0] + 1, len(mean_jobs) - 1):
                emit_mean_load()
            src, acol = mean_jobs[mj_i[0]]
            sgi = mj_i[0] % 2
            mj_i[0] += 1
            n = src.shape[-1]
            if n > 1024:
                bld.op("vector", lambda e, sgi=sgi, acol=acol, n=n: e.reduce_sum(
                       mean_acc[:, acol:acol + 1], stg[sgi][:, 0:n], axis=AX.X,
                       apply_absolute_value=True),
                       reads=(f"stg{sgi}", f"stg{sgi}h"), writes=(f"macc:{acol}",))
            else:
                bld.op("scalar", lambda e, sgi=sgi, acol=acol, n=n: e.activation(
                       sq[:, 0:n], stg[sgi][:, 0:n], AF.Abs,
                       accum_out=mean_acc[:, acol:acol + 1]),
                       reads=(f"stg{sgi}", f"stg{sgi}h"), writes=("sq", f"macc:{acol}"))

        def a_dma(t):
            xi = xin[t % 4]
            bld.op("sync", lambda e, t=t, xi=xi: e.dma_start(out=xi[:], in_=xk_t[t]),
                   writes=(f"xin{t % 4}",), q="QS")
        a_dma(0)
        for t in range(NKVT):
            emit_mean_step()
            if t + 1 < NKVT:
                a_dma(t + 1)
            ln_sum2(f"xin{t % 4}", xin[t % 4][:], t)
        ln_batch(0, NKVT)
        a_dma(0)
        for t in range(NKVT):
            emit_mean_step()
            if t + 1 < NKVT:
                a_dma(t + 1)
            ln_apply_t(f"xin{t % 4}", xin[t % 4][:], t, 0,
                       xlt_kv, f"xlt:{t}", t)

        if KPH < 2:
            raise _Stop()
        bld.op("gpsimd", lambda e: e.dma_start(
               out=Wk16[:].rearrange("p (c n) -> p c n", c=8)[:, 0:4, :],
               in_=WT_c["k"][:, 0:4, :]),
               writes=("Wk16",), q="QG")
        bld.op("gpsimd", lambda e: e.dma_start(
               out=Wk16[:].rearrange("p (c n) -> p c n", c=8)[:, 4:8, :],
               in_=WT_c["k"][:, 4:8, :]),
               writes=("Wk16h",), q="QG")
        bld.op("gpsimd", lambda e: e.dma_start(
               out=Wv16[:].rearrange("p (c n) -> p c n", c=8)[:, 0:4, :],
               in_=WT_c["v"][:, 0:4, :]),
               writes=("Wv16",), q="QG")
        bld.op("gpsimd", lambda e: e.dma_start(
               out=Wv16[:].rearrange("p (c n) -> p c n", c=8)[:, 4:8, :],
               in_=WT_c["v"][:, 4:8, :]),
               writes=("Wv16h",), q="QG")
        wk3 = Wk16[:].rearrange("p (c n) -> p c n", c=8)
        wv3 = Wv16[:].rearrange("p (c n) -> p c n", c=8)

        def nblocks(n):
            out, o = [], 0
            while o < n:
                w = min(512, n - o)
                out.append((o, w))
                o += w
            return out

        pg = [ps4, ps5]
        pgk = ["ps4", "ps5"]
        pi = [0]
        for c in range(8):
            for (o, w) in nblocks(KEPT_E[c]):
                emit_mean_step()
                bank = pi[0] % 2
                pi[0] += 1
                pk = pgk[bank]
                reads_xl = tuple(f"xlt:{ts}" for ts in range(o // 128, (o + w + 127) // 128))
                for dc in range(8):
                    tok = bld.op("tensor", lambda e, c=c, dc=dc, o=o, w=w, bank=bank:
                        e.matmul(pg[bank][:, 0:w], wk3[:, dc, c * 128:(c + 1) * 128],
                                 xlt_kv[:, dc, o:o + w], start=(dc == 0), stop=(dc == 7)),
                        reads=("Wk16", "Wk16h") + reads_xl + ((pk,) if dc == 0 else ()),
                        writes=(pk,) if dc == 0 else (), token=(dc == 7))
                bld.settok([pk], tok)
                bld.op("scalar", lambda e, c=c, o=o, w=w, bank=bank: e.activation(
                       KT[:, KOFF[c] + o:KOFF[c] + o + w], pg[bank][:, 0:w],
                       AF.Identity, bias=bk_sb[:, c:c + 1]),
                       reads=(pk, "bk_sb"), writes=(f"KT:{c}",))
        for t in range(NKVT):
            nh = NHT[t]
            if nh == 0:
                continue
            vt3 = VT[:, VOFF[t]:VOFF[t] + 65 * nh].rearrange("p (h x) -> p h x", x=65)
            bld.op("gpsimd", lambda e, vt3=vt3: e.memset(vt3[:, :, 64:65], 1.0),
                   writes=(f"vones:{t}",))
            for hb0 in range(0, nh, 8):
                hbn = min(8, nh - hb0)
                emit_mean_step()
                bank = pi[0] % 2
                pi[0] += 1
                pk = pgk[bank]
                for dc in range(8):
                    tok = bld.op("tensor", lambda e, t=t, dc=dc, hb0=hb0, hbn=hbn, bank=bank:
                        e.matmul(pg[bank][:, 0:64 * hbn], xlt_kv[:, dc, t * 128:(t + 1) * 128],
                                 wv3[:, dc, hb0 * 64:(hb0 + hbn) * 64],
                                 start=(dc == 0), stop=(dc == 7)),
                        reads=(f"xlt:{t}", "Wv16", "Wv16h") + ((pk,) if dc == 0 else ()),
                        writes=(pk,) if dc == 0 else (), token=(dc == 7))
                bld.settok([pk], tok)
                bld.op("scalar", lambda e, t=t, vt3=vt3, hb0=hb0, hbn=hbn, bank=bank:
                       e.activation(vt3[:, hb0:hb0 + hbn, 0:64], pg[bank][:, 0:64 * hbn],
                                    AF.Identity),
                       reads=(pk,), writes=(f"VT:{t}",))

        def mean_final(col_lo, col_n, sc_s, sc_sw):
            kacc = tuple(f"macc:{a}" for a in range(col_lo, col_lo + col_n))
            bld.op("vector", lambda e: e.reduce_sum(cols[:, 9:10],
                   mean_acc[:, col_lo:col_lo + col_n], axis=AX.X),
                   reads=kacc, writes=("c_w",))
            bld.op("tensor", lambda e: e.matmul(psEa[0:1, 0:1], cols[:, 9:10],
                   onescol[:], start=True, stop=True),
                   reads=("c_w", "onescol"), writes=("psEa",))
            bld.op("vector", lambda e: e.tensor_scalar(cols[0:1, 10:11], psEa[0:1, 0:1],
                   float(2.0 ** -22), 1e-5, ALU.mult, ALU.max),
                   reads=("psEa",), writes=("c_mean1",))
            bld.op("vector", lambda e: e.reciprocal(cols[0:1, 11:12], cols[0:1, 10:11]),
                   reads=("c_mean1",), writes=("c_s1",))
            bld.op("tensor", lambda e: e.matmul(psEa[:, 0:2], onesrow[:],
                   cols[0:1, 10:12], start=True, stop=True),
                   reads=("c_mean1", "c_s1", "onesrow"), writes=("psEa",))
            bld.op("vector", lambda e, sc_sw=sc_sw: e.tensor_copy(
                   scal[:, sc_sw:sc_sw + 1], psEa[:, 0:1]),
                   reads=("psEa",), writes=(f"sc:{sc_sw}",))
            bld.op("vector", lambda e, sc_s=sc_s: e.tensor_copy(
                   scal[:, sc_s:sc_s + 1], psEa[:, 1:2]),
                   reads=("psEa",), writes=(f"sc:{sc_s}",))

        def a3_dma(t):
            xi = xin[t % 4]
            bld.op("sync", lambda e, t=t, xi=xi: e.dma_start(out=xi[:], in_=xo_t[t]),
                   writes=(f"xin{t % 4}",), q="QS")
        a3_dma(0)
        for t in range(NOT):
            if t + 1 < NOT:
                a3_dma(t + 1)
            ln_sum2(f"xin{t % 4}", xin[t % 4][:], NKVT + t)
        ln_batch(NKVT, NOT)
        while mj_i[0] < len(mean_jobs):
            emit_mean_step()
        mean_final(0, 16, SC_SG, SC_SWG)
        mean_final(16, 16, SC_SU, SC_SWU)
        mean_final(32, 32, SC_SD, SC_SWD)
        bld.op("vector", lambda e: e.tensor_scalar(scal[:, SC_SWD:SC_SWD + 1],
               scal[:, SC_SWD:SC_SWD + 1], 0.5, None, ALU.mult),
               reads=(f"sc:{SC_SWD}",), writes=(f"sc:{SC_SWD}",))

        if KPH < 3:
            raise _Stop()
        bld.barrier()
        xlt_own = XLT[:].rearrange("p (c n) -> p c n", c=8)
        a3_dma(0)
        for t in range(NOT):
            if t + 1 < NOT:
                a3_dma(t + 1)
            ln_apply_t(f"xin{t % 4}", xin[t % 4][:], NKVT + t, NKVT,
                       xlt_own, f"xlo:{t}", t)

        bld.op("gpsimd", lambda e: e.dma_start(
               out=Wk16[:].rearrange("p (c n) -> p c n", c=8)[:, 0:4, :],
               in_=WT_c["q"][:, 0:4, :]),
               writes=("Wk16",), q="QG")
        bld.op("gpsimd", lambda e: e.dma_start(
               out=Wk16[:].rearrange("p (c n) -> p c n", c=8)[:, 4:8, :],
               in_=WT_c["q"][:, 4:8, :]),
               writes=("Wk16h",), q="QG")
        bld.op("gpsimd", lambda e: e.dma_start(
               out=Wv16[:].rearrange("p (c n) -> p c n", c=8)[:, 0:4, :],
               in_=WT_c["o"][:, 0:4, :]),
               writes=("Wv16",), q="QG")
        bld.op("gpsimd", lambda e: e.dma_start(
               out=Wv16[:].rearrange("p (c n) -> p c n", c=8)[:, 4:8, :],
               in_=WT_c["o"][:, 4:8, :]),
               writes=("Wv16h",), q="QG")
        wq3, wo3 = wk3, wv3

        QTb = sb("QTb", [128, 8 * 1024], BF16)
        OT = sb("OT", [128, 8 * 1024], BF16)
        Pring = [sb(f"P{i}", [128, 1024], BF16) for i in range(3)]
        stg8 = [sb(f"stg8_{i}", [128, 2048], FP8) for i in range(2)]
        x1b = [sb(f"x1b{i}", [128, D], F32) for i in range(2)]
        park3 = [sb(f"pkp{i}", [128, 4, 256], F32) for i in range(2)]
        dstack = sb("dstack", [128, 1536], F32)
        den_pair = [dstack[0:4, 0:512], dstack[32:36, 0:512]]
        rec_pair = [dstack[0:4, 512:1024], dstack[0:4, 1024:1536]]
        denrow = [dstack[64:65, 0:512], dstack[96:97, 0:512]]
        selT = sb("selT", [128, 8], F32)
        sel_h = [sb(f"sel{i}", [4, 128], F32) for i in range(2)]
        bld.op("gpsimd", lambda e: e.memset(selT[:], 0.0), writes=("selT",))
        for half in range(2):
            bld.op("gpsimd", lambda e, half=half: e.memset(
                   selT[0:64, 4 * half + half:4 * half + half + 1], 1.0),
                   reads=("selT",), writes=("selT",))
            bld.op("gpsimd", lambda e, half=half: e.memset(
                   selT[64:128, 4 * half + 2 + half:4 * half + 3 + half], 1.0),
                   reads=("selT",), writes=("selT",))
        for half in range(2):
            bld.op("tensor", lambda e, half=half: e.transpose(
                   out=psEa[0:4, 0:128], in_=selT[:, 4 * half:4 * half + 4],
                   identity=ident32[:]), reads=("selT", "ident32"), writes=("psEa",))
            bld.op("vector", lambda e, half=half: e.tensor_copy(
                   sel_h[half][:], psEa[0:4, 0:128]),
                   reads=("psEa",), writes=(f"sel{half}",))

        tern_jobs = []
        for ci in range(16):
            c, hcol = ci // 2, ci % 2
            tern_jobs.append(("g", wgT_c[:, c, hcol * 2048:(hcol + 1) * 2048],
                              wg8d[:, ci * 2048:(ci + 1) * 2048], SC_SG))
        for ci in range(16):
            c, hcol = ci // 2, ci % 2
            tern_jobs.append(("u", wuT_c[:, c, hcol * 2048:(hcol + 1) * 2048],
                              wu8d[:, ci * 2048:(ci + 1) * 2048], SC_SU))
        for ci in range(32):
            tern_jobs.append(("d", wdT_c[:, ci, :], wd8d[:, ci * 1024:(ci + 1) * 1024], SC_SD))
        tj_i = [0]
        tl_i = [0]

        def emit_tern_load():
            if tl_i[0] >= len(tern_jobs):
                return
            _, src, _, _ = tern_jobs[tl_i[0]]
            sgi = tl_i[0] % 2
            tl_i[0] += 1
            n = src.shape[-1]
            sk = f"stg{sgi}"
            h2 = n // 2
            bld.op("sync", lambda e, src=src, sgi=sgi, h2=h2: e.dma_start(
                   out=stg[sgi][:, 0:h2], in_=src[:, 0:h2]),
                   writes=(sk,), q="QS")
            bld.op("sync", lambda e, src=src, sgi=sgi, h2=h2, n=n: e.dma_start(
                   out=stg[sgi][:, h2:n], in_=src[:, h2:n]),
                   writes=(sk + "h",), q="QS")

        def emit_tern_step():
            if tj_i[0] >= len(tern_jobs):
                return
            while tl_i[0] <= min(tj_i[0] + 1, len(tern_jobs) - 1):
                emit_tern_load()
            tag, src, dst, sc_s = tern_jobs[tj_i[0]]
            sgi = tj_i[0] % 2
            tj_i[0] += 1
            n = src.shape[-1]
            sk, s8k = f"stg{sgi}", f"stg8_{sgi}"
            s_ap = scal[:, sc_s:sc_s + 1]
            if tag != "d":
                bld.op("vector", lambda e, sgi=sgi, n=n, s_ap=s_ap: e.tensor_scalar(
                       stg[sgi][:, 0:n], stg[sgi][:, 0:n], s_ap, 1.49, ALU.mult, ALU.min),
                       reads=(sk, sk + "h", f"sc:{sc_s}"), writes=(sk,))
                bld.op("vector", lambda e, sgi=sgi, n=n: e.tensor_scalar(
                       stg[sgi][:, 0:n], stg[sgi][:, 0:n], -1.49, mcol[:], ALU.max, ALU.add),
                       reads=(sk, sk + "h", "mcol"), writes=(sk,))
                bld.op("scalar", lambda e, sgi=sgi, n=n: e.activation(
                       stg8[sgi][:, 0:n], stg[sgi][:, 0:n], AF.Identity, bias=negm[:]),
                       reads=(sk, sk + "h", "negm"), writes=(s8k,))
            else:
                bld.op("vector", lambda e, sgi=sgi, n=n, s_ap=s_ap: e.tensor_scalar(
                       stg[sgi][:, 0:n], stg[sgi][:, 0:n], s_ap, 1.49, ALU.mult, ALU.min),
                       reads=(sk, sk + "h", f"sc:{sc_s}"), writes=(sk,))
                bld.op("vector", lambda e, sgi=sgi, n=n: e.tensor_scalar(
                       stg[sgi][:, n:2 * n], stg[sgi][:, 0:n], -1.49, soft_lo[:],
                       ALU.max, ALU.mult), reads=(sk, sk + "h", "soft_lo"), writes=(sk,))
                bld.op("vector", lambda e, sgi=sgi, n=n: e.tensor_scalar(
                       stg[sgi][:, 0:n], stg[sgi][:, 0:n], -1.49, soft_hi[:],
                       ALU.max, ALU.mult), reads=(sk, sk + "h", "soft_hi"), writes=(sk,))
                bld.op("vector", lambda e, sgi=sgi, n=n: e.tensor_scalar(
                       stg[sgi][:, n:2 * n], stg[sgi][:, n:2 * n], mcol[:], None,
                       ALU.add), reads=(sk, sk + "h", "mcol"), writes=(sk,))
                bld.op("vector", lambda e, sgi=sgi, n=n: e.tensor_scalar(
                       stg[sgi][:, 0:n], stg[sgi][:, 0:n], mcol[:], None,
                       ALU.add), reads=(sk, sk + "h", "mcol"), writes=(sk,))
                bld.op("vector", lambda e, sgi=sgi, n=n: e.scalar_tensor_tensor(
                       stg8[sgi][:, 0:n], stg[sgi][:, n:2 * n], m2col[:],
                       stg[sgi][:, 0:n], ALU.subtract, ALU.add),
                       reads=(sk, sk + "h", "m2col"), writes=(s8k,))
            bld.op("sync", lambda e, dst=dst, sgi=sgi, n=n: e.dma_start(
                   out=dst, in_=stg8[sgi][:, 0:n]), reads=(s8k,),
                   writes=(f"w8d:{tag}",), q="QS")

        def o_ap(h, half):
            if h % 2 == 0:
                return (pg[half], pgk[half], pg[half][:], 0)
            return (psW1, "psW1a" if half == 0 else "psW1b",
                    psW1[:, half * 512:(half + 1) * 512], half * 512)

        def emit_evac(qp, h):
            c, off = h // 2, 64 * (h % 2)
            pr = c % 2
            pp, ppk = park3[pr], f"pp{pr}"
            for half in range(2):
                _, opk, oview, _ = o_ap(h, half)
                bld.op("vector", lambda e, oview=oview, pp=pp, off=off, half=half:
                       e.tensor_copy(pp[off:off + 64, half * 2:half * 2 + 2, :],
                                     oview[0:64, :].rearrange("p (a n) -> p a n", a=2)),
                       reads=(opk,), writes=(ppk,))
                dr = half
                bld.op("vector", lambda e, oview=oview, dr=dr: e.tensor_copy(
                       denrow[dr], oview[64:65, :]),
                       reads=(opk,), writes=(f"dnr{dr}",))
                r = 2 * (h % 2) + half
                bld.op("sync", lambda e, dr=dr, pr=pr, r=r: e.dma_start(
                       out=den_pair[pr][r:r + 1, :], in_=denrow[dr]),
                       reads=(f"dnr{dr}",), writes=(f"dp{pr}:{r}",), q="QS")

        def emit_recip(c):
            pr = c % 2
            bld.op("vector", lambda e, pr=pr: e.reciprocal(rec_pair[pr],
                   den_pair[pr]),
                   reads=tuple(f"dp{pr}:{r}" for r in range(4)), writes=(f"rp{pr}",))

        def emit_div(c, half, hnow):
            pr = c % 2
            pp, ppk = park3[pr], f"pp{pr}"
            if hnow % 2 == 0:
                rbv, rbk = psW1[:, 0:512], "psW1a"
            else:
                rbv, rbk = ps4[:], "ps4"
            bld.op("tensor", lambda e, pr=pr, half=half, rbv=rbv: e.matmul(
                   rbv, sel_h[half][:], rec_pair[pr], start=True, stop=True),
                   reads=(f"sel{half}", f"rp{pr}", rbk), writes=(rbk,))
            rb3 = rbv.rearrange("p (a n) -> p a n", a=2)
            bld.op("vector", lambda e, pp=pp, half=half, rb3=rb3: e.tensor_mul(
                   pp[:, half * 2:half * 2 + 2, :],
                   pp[:, half * 2:half * 2 + 2, :], rb3),
                   reads=(ppk, rbk), writes=(ppk,))
            bld.op("vector", lambda e, pp=pp, c=c, half=half: e.tensor_scalar(
                   OT[:, c * 1024 + half * 512:c * 1024 + (half + 1) * 512]
                       .rearrange("p (a n) -> p a n", a=2),
                   pp[:, half * 2:half * 2 + 2, :], bv_sb[:, c:c + 1], None, ALU.add),
                   reads=(ppk, "bv_sb"), writes=(f"OT:{c}",))

        for qp in range(2):
            for c in range(8):
                pw = psW0 if c % 2 == 0 else psW1
                pwk = ("psW0a", "psW0b") if c % 2 == 0 else ("psW1a", "psW1b")
                for half in range(2):
                    for dc in range(8):
                        tok = bld.op("tensor", lambda e, c=c, dc=dc, qp=qp, half=half, pw=pw:
                            e.matmul(pw[:, half * 512:(half + 1) * 512],
                                     wq3[:, dc, c * 128:(c + 1) * 128],
                                     xlt_own[:, dc, qp * 1024 + half * 512:qp * 1024 + (half + 1) * 512],
                                     start=(dc == 0), stop=(dc == 7)),
                            reads=("Wk16", "Wk16h") + tuple(f"xlo:{ts}" for ts in
                                   range(qp * 8 + half * 4, qp * 8 + half * 4 + 4)) +
                                   ((pwk[half],) if dc == 0 else ()),
                            writes=(pwk[half],) if dc == 0 else (), token=(dc == 7))
                    bld.settok([pwk[half]], tok)
                bld.op("scalar", lambda e, c=c, pw=pw: e.activation(
                       QTb[:, c * 1024:(c + 1) * 1024], pw[:], AF.Identity,
                       bias=bq_sb[:, c:c + 1]),
                       reads=(pwk[0], pwk[1], "bq_sb"), writes=(f"QTb:{c}",))
                emit_tern_step()
            for h in range(H):
                emit_tern_step()
                emit_tern_step()
                c, off = h // 2, 64 * (h % 2)
                nkb = NKB[h]

                def emit_S(kb, c=c, off=off, h=h):
                    if kb % 2 == 0:
                        sbv, sk0, sk1 = psW0, "psW0a", "psW0b"
                    else:
                        sbv, sk0, sk1 = psE, "psEa", "psEb"
                    for half in range(2):
                        tok = bld.op("tensor", lambda e, c=c, off=off, kb=kb, half=half, sbv=sbv:
                            e.matmul(sbv[:, half * 512:(half + 1) * 512],
                                     KT[off:off + 64, KOFF[c] + kb * 128:KOFF[c] + (kb + 1) * 128],
                                     QTb[off:off + 64, c * 1024 + half * 512:c * 1024 + (half + 1) * 512],
                                     start=True, stop=True),
                            reads=(f"KT:{c}", f"QTb:{c}", sk0 if half == 0 else sk1),
                            writes=(sk0 if half == 0 else sk1,), token=(half == 1))
                    bld.settok([sk0, sk1], tok)
                    pidx = kb % 3
                    bld.op("scalar", lambda e, kb=kb, h=h, pidx=pidx, sbv=sbv: e.activation(
                           Pring[pidx][:], sbv[:], AF.Exp, scale=0.125,
                           bias=bias_tbl[:, BOFF[h] + kb:BOFF[h] + kb + 1]),
                           reads=(sk0, sk1, f"bias:{h}"), writes=(f"P:{pidx}",))

                def emit_PV(kb, h=h, nkb=nkb):
                    pidx = kb % 3
                    for half in range(2):
                        _, opk, oview, _ = o_ap(h, half)
                        vslice = VT[:, VOFF[kb] + h * 65:VOFF[kb] + (h + 1) * 65]
                        tok = bld.op("tensor", lambda e, vslice=vslice, kb=kb, half=half,
                            oview=oview, pidx=pidx, nkb=nkb:
                            e.matmul(oview[0:65, :], vslice,
                                     Pring[pidx][:, half * 512:(half + 1) * 512],
                                     start=(kb == 0), stop=(kb == nkb - 1)),
                            reads=(f"VT:{kb}", f"vones:{kb}", f"P:{pidx}") +
                                  ((opk,) if kb == 0 else ()),
                            writes=(opk,) if kb == 0 else (), token=(kb == nkb - 1))
                        if kb == nkb - 1:
                            bld.settok([opk], tok)

                emit_S(0)
                if nkb > 1:
                    emit_S(1)
                for kb in range(nkb):
                    if kb + 2 < nkb:
                        emit_S(kb + 2)
                    emit_PV(kb)
                if h >= 1:
                    emit_evac(qp, h - 1)
                if h >= 3:
                    emit_div((h - 3) // 2, (h - 3) % 2, h)
                if h >= 2 and h % 2 == 0:
                    emit_recip(h // 2 - 1)
            emit_evac(qp, H - 1)
            emit_recip(H // 2 - 1)
            emit_div(6, 1, 16)
            emit_div(7, 0, 17)
            emit_div(7, 1, 18)
            for tt in range(8):
                gt = qp * 8 + tt
                xi = xin[gt % 4]
                xik = f"xin{gt % 4}"
                bld.op("sync", lambda e, gt=gt, xi=xi: e.dma_start(out=xi[:], in_=xo_t[gt]),
                       writes=(xik,), q="QS")
                for half in range(2):
                    bank = half
                    pk = pgk[bank]
                    for ac in range(8):
                        bld.op("tensor", lambda e, ac=ac, tt=tt, half=half, bank=bank:
                            e.matmul(pg[bank][:], OT[:, ac * 1024 + tt * 128:ac * 1024 + (tt + 1) * 128],
                                     wo3[:, ac, half * 512:(half + 1) * 512],
                                     start=(ac == 0), stop=False),
                            reads=(f"OT:{ac}", "Wv16", "Wv16h") + ((pk,) if ac == 0 else ()),
                            writes=(pk,) if ac == 0 else (), token=False)
                    tok = bld.op("tensor", lambda e, half=half, bank=bank:
                        e.matmul(pg[bank][:], onesrow16[:], bo16[:, half * 512:(half + 1) * 512],
                                 start=False, stop=True),
                        reads=("onesrow16", "bo16"), writes=(), token=True)
                    bld.settok([pk], tok)
                x1k = f"x1b{tt % 2}"
                for half in range(2):
                    bld.op("vector", lambda e, tt=tt, half=half, xi=xi: e.tensor_add(
                           x1b[tt % 2][:, half * 512:(half + 1) * 512],
                           pg[half][:], xi[:, half * 512:(half + 1) * 512]),
                           reads=(pgk[half], xik), writes=(x1k,))
                ln_stats(x1k, x1b[tt % 2][:], gt, LN2C, f"ln2c:{gt}")
                bld.op("sync", lambda e, gt=gt, tt=tt: e.dma_start(
                       out=x1_t[gt], in_=x1b[tt % 2][:]), reads=(x1k,),
                       writes=(f"x1scr:{gt}",), q="QS")
                emit_tern_step()
        while tj_i[0] < len(tern_jobs):
            emit_tern_step()

        if KPH < 4:
            raise _Stop()
        bld.barrier()
        sb_free_back_to(XLT)
        if os.environ.get("KDBG"):
            print("alive after free:", [hh.name if hasattr(hh,'name') else str(hh) for cm,hh in alive])
        wg8 = sb("wg8", [128, 8 * 4096], FP8)
        wu8 = sb("wu8", [128, 8 * 4096], FP8)
        wd8 = sb("wd8", [128, 32 * 1024], FP8)
        gu = sb("gu", [128, 32 * TG], F32)
        hT = sb("hT", [128, 32 * TG], BF16)
        xqT = [sb(f"xqT{i}", [128, 8 * TG], BF16) for i in range(2)]
        x1in = [sb(f"x1in{i}", [128, D], F32) for i in range(6)]
        xhat = sb("xhat", [128, D], F32)
        gsil = [sb(f"gsil{i}", [128, TG], F32) for i in range(2)]
        s2b = sb("s2b", [128, 2 * TG], F32)
        s2row = sb("s2row", [1, 2 * TG], F32)
        Mx = sb("Mx", [128, TG], F32)
        gabs = [sb(f"gabs{i}", [128, TG], F32) for i in range(2)]
        outb = [sb(f"outb{i}", [128, D], F32) for i in range(2)]
        dcols = sb("dcols", [128, 12], F32)

        wg83 = wg8[:].rearrange("p (c n) -> p c n", c=8)
        wu83 = wu8[:].rearrange("p (c n) -> p c n", c=8)
        wd83 = wd8[:].rearrange("p (c n) -> p c n", c=32)

        gbanks = [(ps4[:], "ps4"), (ps5[:], "ps5"), (psEa, "psEa"),
                  (psW0[:, 0:512], "psW0a")]
        dbanks = [(psW0[:, 512:1024], "psW0b"), (psW1[:, 0:512], "psW1a")]

        def emit_lnxq(tg):
            for ti in range(2):
                gt = tg * 2 + ti
                xk = f"x1in{gt % 6}"
                bld.op("sync", lambda e, gt=gt: e.dma_start(out=x1in[gt % 6][:], in_=x1_t[gt]),
                       reads=(f"x1scr:{gt}",), writes=(xk,), q="QS")
                bld.op("vector", lambda e, gt=gt: e.tensor_scalar(xhat[:], x1in[gt % 6][:],
                       LN2C[:, 2 * gt:2 * gt + 1], LN2C[:, 2 * gt + 1:2 * gt + 2],
                       ALU.mult, ALU.add),
                       reads=(xk, f"ln2c:{gt}:r", f"ln2c:{gt}:m"), writes=("xhat",))
                bld.op("vector", lambda e: e.reduce_max(dcols[:, 0:1], xhat[:], axis=AX.X,
                       apply_absolute_value=True), reads=("xhat",), writes=("d_m1",))
                bld.op("vector", lambda e: e.tensor_scalar(dcols[:, 0:1], dcols[:, 0:1],
                       1e-5, None, ALU.max), reads=("d_m1",), writes=("d_m1",))
                bld.op("vector", lambda e: e.reciprocal(dcols[:, 1:2], dcols[:, 0:1]),
                       reads=("d_m1",), writes=("d_rm1",))
                bld.op("vector", lambda e: e.tensor_scalar(dcols[:, 2:3], dcols[:, 1:2],
                       127.0, None, ALU.mult), reads=("d_rm1",), writes=("d_s1",))
                bld.op("vector", lambda e: e.tensor_scalar(dcols[:, 3:4], dcols[:, 0:1],
                       float(1.0 / 127.0), None, ALU.mult), reads=("d_m1",), writes=("d_sv1",))
                bld.op("vector", lambda e: e.tensor_scalar(xhat[:], xhat[:],
                       dcols[:, 2:3], mcol[:], ALU.mult, ALU.add),
                       reads=("xhat", "d_s1", "mcol"), writes=("xhat",))
                bld.op("vector", lambda e: e.tensor_scalar(xhat[:], xhat[:],
                       mcol[:], dcols[:, 3:4], ALU.subtract, ALU.mult),
                       reads=("xhat", "d_sv1", "mcol"), writes=("xhat",))
                xq3 = xqT[tg % 2][:].rearrange("p (c n) -> p c n", c=8)
                for rnd in range(2):
                    for j in range(4):
                        jj = rnd * 4 + j
                        tok = bld.op("tensor", lambda e, j=j, jj=jj: e.transpose(
                            out=psEb[:, j * 128:(j + 1) * 128],
                            in_=xhat[:, jj * 128:(jj + 1) * 128],
                            identity=ident32[:]),
                            reads=("xhat", "ident32") + (("psEb",) if j == 0 else ()),
                            writes=("psEb",) if j == 0 else (), token=(j == 3))
                    bld.settok(["psEb"], tok)
                    bld.op("scalar", lambda e, ti=ti, rnd=rnd, xq3=xq3: e.activation(
                           xq3[:, rnd * 4:(rnd + 1) * 4, ti * 128:(ti + 1) * 128],
                           psEb[:].rearrange("p (c n) -> p c n", c=4), AF.Identity),
                           reads=("psEb",), writes=(f"xqT{tg % 2}",))

        def emit_gateup(tg):
            xq3 = xqT[tg % 2][:].rearrange("p (c n) -> p c n", c=8)
            xqk = f"xqT{tg % 2}"
            for ic in range(32):
                if ic % 2 == 0:
                    drain_hT(2)
                bview, pk = gbanks[ic % 4]
                for dc in range(8):
                    bld.op("tensor", lambda e, ic=ic, dc=dc, bview=bview, xq3=xq3: e.matmul(
                           bview[:, 0:TG], wg83[:, dc, ic * 128:(ic + 1) * 128],
                           xq3[:, dc, :], start=(dc == 0), stop=(dc == 7)),
                           reads=tuple(f"wg8:{i}" for i in range(4)) + (xqk,) + ((pk,) if dc == 0 else ()),
                           writes=(pk,) if dc == 0 else (), token=False)
                for dc in range(8):
                    tok = bld.op("tensor", lambda e, ic=ic, dc=dc, bview=bview, xq3=xq3: e.matmul(
                           bview[:, TG:2 * TG], wu83[:, dc, ic * 128:(ic + 1) * 128],
                           xq3[:, dc, :], start=(dc == 0), stop=(dc == 7)),
                           reads=tuple(f"wu8:{i}" for i in range(4)) + (xqk,), writes=(), token=(dc == 7))
                bld.settok([pk], tok)
                gk = f"gsil{ic % 2}"
                bld.op("scalar", lambda e, ic=ic, bview=bview: e.activation(
                       gsil[ic % 2][:], bview[:, 0:TG], AF.Silu,
                       scale=scal[:, SC_SWG:SC_SWG + 1]),
                       reads=(pk, f"sc:{SC_SWG}"), writes=(gk,))
                bld.op("vector", lambda e, ic=ic, bview=bview: e.scalar_tensor_tensor(
                       gu[:, ic * TG:(ic + 1) * TG], bview[:, TG:2 * TG],
                       scal[:, SC_SWU:SC_SWU + 1], gsil[ic % 2][:], ALU.mult, ALU.mult),
                       reads=(pk, f"sc:{SC_SWU}", gk), writes=(f"gu:{ic}",))
                if ic == 0:
                    bld.op("scalar", lambda e: e.activation(Mx[:], gu[:, 0:TG], AF.Abs),
                           reads=("gu:0",), writes=("Mx",))
                else:
                    gak = f"gabs{ic % 2}"
                    bld.op("scalar", lambda e, ic=ic: e.activation(
                           gabs[ic % 2][:], gu[:, ic * TG:(ic + 1) * TG], AF.Abs),
                           reads=(f"gu:{ic}",), writes=(gak,))
                    bld.op("vector", lambda e, ic=ic: e.tensor_max(Mx[:], Mx[:],
                           gabs[ic % 2][:]), reads=(gak, "Mx"), writes=("Mx",))

        def emit_quant_h(tg):
            for ti in range(2):
                tok = bld.op("tensor", lambda e, ti=ti: e.transpose(
                       out=psW1[:, 512 + ti * 128:512 + (ti + 1) * 128],
                       in_=Mx[:, ti * 128:(ti + 1) * 128], identity=ident32[:]),
                       reads=("Mx", "ident32") + (("psW1b",) if ti == 0 else ()),
                       writes=("psW1b",) if ti == 0 else (), token=True)
            bld.settok(["psW1b"], tok)
            for ti in range(2):
                bld.op("vector", lambda e, ti=ti: e.reduce_max(dcols[:, 4 + ti:5 + ti],
                       psW1[:, 512 + ti * 128:512 + (ti + 1) * 128], axis=AX.X),
                       reads=("psW1b",), writes=(f"d_m2:{ti}",))
                bld.op("vector", lambda e, ti=ti: e.tensor_scalar(dcols[:, 4 + ti:5 + ti],
                       dcols[:, 4 + ti:5 + ti], 1e-5, None, ALU.max),
                       reads=(f"d_m2:{ti}",), writes=(f"d_m2:{ti}",))
                bld.op("vector", lambda e, ti=ti: e.reciprocal(dcols[:, 8 + ti:9 + ti],
                       dcols[:, 4 + ti:5 + ti]), reads=(f"d_m2:{ti}",), writes=(f"d_rm2:{ti}",))
                bld.op("vector", lambda e, ti=ti: e.tensor_scalar(dcols[:, 10 + ti:11 + ti],
                       dcols[:, 8 + ti:9 + ti], 127.0, None, ALU.mult),
                       reads=(f"d_rm2:{ti}",), writes=(f"d_s2:{ti}",))
                bld.op("vector", lambda e, ti=ti: e.scalar_tensor_tensor(
                       dcols[:, 6 + ti:7 + ti], dcols[:, 4 + ti:5 + ti],
                       float(1.0 / 127.0), scal[:, SC_SWD:SC_SWD + 1], ALU.mult, ALU.mult),
                       reads=(f"d_m2:{ti}", f"sc:{SC_SWD}"), writes=(f"dcmb:{ti}",))
            for ti in range(2):
                tok = bld.op("tensor", lambda e, ti=ti: e.transpose(
                       out=psW1[0:1, 512 + 256 + ti * 128:512 + 256 + (ti + 1) * 128],
                       in_=dcols[:, 10 + ti:11 + ti], identity=ident32[:]),
                       reads=(f"d_s2:{ti}", "ident32") + (("psW1b",) if ti == 0 else ()),
                       writes=("psW1b",) if ti == 0 else (), token=True)
            bld.settok(["psW1b"], tok)
            bld.op("vector", lambda e: e.tensor_copy(s2row[0:1, 0:TG], psW1[0:1, 768:768 + TG]),
                   reads=("psW1b",), writes=("s2row",))
            bld.op("vector", lambda e: e.tensor_copy(s2row[0:1, TG:2 * TG], s2row[0:1, 0:TG]),
                   reads=("s2row",), writes=("s2row",))
            bld.op("tensor", lambda e: e.matmul(psW1[:, 512:1024], onesrow[:],
                   s2row[:], start=True, stop=True),
                   reads=("s2row", "onesrow"), writes=("psW1b",))
            bld.op("scalar", lambda e: e.activation(s2b[:], psW1[:, 512:1024],
                   AF.Identity), reads=("psW1b",), writes=("s2b",))

        pending_hT = []

        def emit_quant_hT(tg):
            for ic in range(0, 32, 2):
                def th1(ic=ic):
                    bld.op("vector", lambda e, ic=ic: e.tensor_mul(
                           gu[:, ic * TG:(ic + 2) * TG],
                           gu[:, ic * TG:(ic + 2) * TG], s2b[:]),
                           reads=(f"gu:{ic}", f"gu:{ic + 1}", "s2b"),
                           writes=(f"gu:{ic}", f"gu:{ic + 1}"))
                def th2(ic=ic):
                    bld.op("vector", lambda e, ic=ic: e.tensor_scalar(
                           hT[:, ic * TG:(ic + 2) * TG], gu[:, ic * TG:(ic + 2) * TG],
                           mcol[:], mcol[:], ALU.add, ALU.subtract),
                           reads=(f"gu:{ic}", f"gu:{ic + 1}", "mcol"), writes=("hT",))
                pending_hT.append(th1)
                pending_hT.append(th2)

        def drain_hT(k=1):
            for _ in range(k):
                if pending_hT:
                    pending_hT.pop(0)()

        def emit_down(tg):
            for ti in range(2):
                gt = tg * 2 + ti
                xk = f"x1in{gt % 6}"
                ob = outb[gt % 2]
                obk = f"outb{gt % 2}"
                for half in range(2):
                    bview, pk = dbanks[half]
                    for ic in range(32):
                        tok = bld.op("tensor", lambda e, ic=ic, ti=ti, half=half, bview=bview:
                            e.matmul(bview[:],
                                     hT[:, ic * TG + ti * 128:ic * TG + (ti + 1) * 128],
                                     wd83[:, ic, half * 512:(half + 1) * 512],
                                     start=(ic == 0), stop=(ic == 31)),
                            reads=("hT",) + tuple(f"wd8:{i}" for i in range(4)) + ((pk,) if ic == 0 else ()),
                            writes=(pk,) if ic == 0 else (), token=(ic == 31))
                    bld.settok([pk], tok)
                    bld.op("vector", lambda e, ti=ti, half=half, bview=bview, ob=ob, gt=gt:
                        e.scalar_tensor_tensor(ob[:, half * 512:(half + 1) * 512],
                            bview[:], dcols[:, 6 + ti:7 + ti],
                            x1in[gt % 6][:, half * 512:(half + 1) * 512],
                            ALU.mult, ALU.add),
                        reads=(pk, f"dcmb:{ti}", xk), writes=(obk,))
                bld.op("sync", lambda e, gt=gt, ob=ob: e.dma_start(out=y_t[gt], in_=ob[:]),
                       reads=(obk,), q="QS")

        emit_lnxq(0)
        emit_lnxq(1)
        for qn, (dst, srcd, tag) in enumerate(
                [(wg8, wg8d, "g"), (wu8, wu8d, "u"), (wd8, wd8d, "d")]):
            W = dst.shape[1]
            for qq in range(4):
                lo, hi = qq * (W // 4), (qq + 1) * (W // 4)
                bld.op("sync", lambda e, dst=dst, srcd=srcd, lo=lo, hi=hi:
                       e.dma_start(out=dst[:, lo:hi], in_=srcd[:, lo:hi]),
                       reads=(f"w8d:{tag}",), writes=(f"w{tag}8:{qq}",), q="QS")
        for tg in range(NTG):
            emit_gateup(tg)
            if tg > 0:
                emit_down(tg - 1)
            emit_quant_h(tg)
            if tg + 2 < NTG:
                emit_lnxq(tg + 2)
            emit_quant_hT(tg)
        drain_hT(32)
        emit_down(NTG - 1)
    except _Stop:
        pass

    for eng in ("tensor", "vector", "scalar", "gpsimd"):
        bld.op(eng, lambda e: e.nop(), token=True)
    for eng in ("tensor", "scalar"):
        sem = bld.semof[eng]
        v = bld.cnt[sem]
        bld.plan[eng].append(lambda e, sem=sem, v=v: e.wait_ge(bld.sems[sem], v))
    for i in range(8):
        v = bld.cnt[f"QS{i}"]
        if v:
            bld.plan["sync"].append(lambda e, i=i, v=v: e.wait_ge(bld.sems[f"QS{i}"], v))
    for i in range(2):
        v = bld.cnt[f"QG{i}"]
        if v:
            bld.plan["sync"].append(lambda e, i=i, v=v: e.wait_ge(bld.sems[f"QG{i}"], v))

    import contextlib
    with contextlib.ExitStack() as stk:
        sems = {}
        for nm in list(bld.cnt.keys()):
            sems[nm] = stk.enter_context(nc.semaphore("s" + nm))
        blk = stk.enter_context(nc.Block())
        bld.sems = sems

        @blk.sync
        def _(e):
            for f in bld.plan["sync"]:
                f(e)

        @blk.tensor
        def _(e):
            for f in bld.plan["tensor"]:
                f(e)

        @blk.vector
        def _(e):
            for f in bld.plan["vector"]:
                f(e)

        @blk.scalar
        def _(e):
            for f in bld.plan["scalar"]:
                f(e)

        @blk.gpsimd
        def _(e):
            for f in bld.plan["gpsimd"]:
                f(e)

    while alive:
        cm, h = alive.pop()
        cm.__exit__(None, None, None)
    for cm in reversed(psum_cm):
        cm.__exit__(None, None, None)
    return nc


_NC = None


def kernel(L0_attn_x, L0_attn_Wq, L0_attn_bq, L0_attn_Wk, L0_attn_bk,
           L0_attn_Wv, L0_attn_bv, L0_attn_Wo, L0_attn_bo,
           L0_ffn_w_gate, L0_ffn_w_up, L0_ffn_w_down):
    global _NC
    if _NC is None:
        _NC = _build()
    nc = _NC
    f32 = np.float32
    x = np.ascontiguousarray(L0_attn_x, dtype=f32)
    WqT = np.ascontiguousarray(np.asarray(L0_attn_Wq, f32).T)
    WkT = np.ascontiguousarray(np.asarray(L0_attn_Wk, f32).T)
    WvT = np.ascontiguousarray(np.asarray(L0_attn_Wv, f32).T)
    WoT = np.ascontiguousarray(np.asarray(L0_attn_Wo, f32).T)
    bq_r = np.ascontiguousarray(np.asarray(L0_attn_bq, f32).reshape(8, 128).T)
    bk_r = np.ascontiguousarray(np.asarray(L0_attn_bk, f32).reshape(8, 128).T)
    bv_r = np.ascontiguousarray(np.asarray(L0_attn_bv, f32).reshape(8, 128).T)
    bo_r = np.ascontiguousarray(np.asarray(L0_attn_bo, f32).reshape(1, D))
    wgT = np.ascontiguousarray(np.asarray(L0_ffn_w_gate, f32).T)
    wuT = np.ascontiguousarray(np.asarray(L0_ffn_w_up, f32).T)
    wdT = np.ascontiguousarray(np.asarray(L0_ffn_w_down, f32).T)

    in_maps = []
    for core in range(NCORE):
        b, par = core // 2, core % 2
        blocks = [par + 2 * i for i in range(NQB)]
        x_own = np.ascontiguousarray(
            np.concatenate([x[b, qb * 512:(qb + 1) * 512] for qb in blocks], axis=0))
        x_kv = np.ascontiguousarray(x[b, :KVT])
        in_maps.append({
            "x_own": x_own, "x_kv": x_kv,
            "WqT": WqT, "WkT": WkT, "WvT": WvT, "WoT": WoT,
            "bq_r": bq_r, "bk_r": bk_r, "bv_r": bv_r, "bo_r": bo_r,
            "wgT": wgT, "wuT": wuT, "wdT": wdT,
        })
    res = run_bass_kernel_spmd(nc, in_maps, core_ids=list(range(NCORE)))
    out = np.empty((B, L, D), dtype=f32)
    for core in range(NCORE):
        b, par = core // 2, core % 2
        yv = res.results[core]["y"]
        for i in range(NQB):
            qb = par + 2 * i
            out[b, qb * 512:(qb + 1) * 512] = yv[i * 512:(i + 1) * 512]
    return out


# version 28 (final)
# speedup vs baseline: 1.0610x; 1.0343x over previous
import os
import numpy as np
import ml_dtypes
import concourse.bass as bass
import concourse.mybir as mybir
from concourse.bass_utils import run_bass_kernel_spmd

F32 = mybir.dt.float32
BF16 = mybir.dt.bfloat16
FP8 = mybir.dt.float8e4
AF = mybir.ActivationFunctionType
ALU = mybir.AluOpType
AX = mybir.AxisListType

B, L, D, H, HD, INNER = 4, 4096, 1024, 16, 64, 4096
NCORE = 8
NOWN = 2048
TG = 256
NTG = NOWN // TG
NQB = 4

_EXP = [(-4.0 + h * (3.7 / 15.0)) for h in range(H)]
SLOPES64 = [float(2.0 ** e) for e in _EXP]
KEPT = [min(int(np.ceil(130.0 / s / 128.0)) * 128, L) for s in SLOPES64]
KVT = max(KEPT)
NKVT = KVT // 128
NOT = NOWN // 128
NKB = [k // 128 for k in KEPT]
BOFF = np.concatenate([[0], np.cumsum(NKB)]).astype(int)
TOTKB = int(BOFF[-1])
KEPT_E = [KEPT[2 * c] for c in range(8)]
KOFF = np.concatenate([[0], np.cumsum(KEPT_E)]).astype(int)
KCOLS = int(KOFF[-1])
NHT = [sum(1 for h in range(H) if KEPT[h] > 128 * t) for t in range(NKVT)]
VOFF = np.concatenate([[0], np.cumsum([65 * n for n in NHT])]).astype(int)
VCOLS = int(VOFF[-1])

TINY = float(2.0 ** -26)
MAGIC = float(1.5 * 2 ** 23)
MSCALE = float(1.5 * 2 ** 78)
EPS_SOFT = 1e-5
LN2_64 = float(np.log(np.float64(2.0)))


def _dual(c64):
    c32 = float(np.float32(c64))
    corr = float(np.float32((c64 - c32) / TINY))
    return c32, corr


def _build():
    nc = bass.Bass("TRN2", target_bir_lowering=False)
    x_own = nc.dram_tensor("x_own", [NOWN, D], F32, kind="ExternalInput").ap()
    x_kv = nc.dram_tensor("x_kv", [KVT, D], F32, kind="ExternalInput").ap()
    WqT = nc.dram_tensor("WqT", [D, D], F32, kind="ExternalInput").ap()
    WkT = nc.dram_tensor("WkT", [D, D], F32, kind="ExternalInput").ap()
    WvT = nc.dram_tensor("WvT", [D, D], F32, kind="ExternalInput").ap()
    WoT = nc.dram_tensor("WoT", [D, D], F32, kind="ExternalInput").ap()
    bq_r = nc.dram_tensor("bq_r", [128, 8], F32, kind="ExternalInput").ap()
    bk_r = nc.dram_tensor("bk_r", [128, 8], F32, kind="ExternalInput").ap()
    bv_r = nc.dram_tensor("bv_r", [128, 8], F32, kind="ExternalInput").ap()
    bo_r = nc.dram_tensor("bo_r", [1, D], F32, kind="ExternalInput").ap()
    wgT = nc.dram_tensor("wgT", [D, INNER], F32, kind="ExternalInput").ap()
    wuT = nc.dram_tensor("wuT", [D, INNER], F32, kind="ExternalInput").ap()
    wdT = nc.dram_tensor("wdT", [INNER, D], F32, kind="ExternalInput").ap()
    y = nc.dram_tensor("y", [NOWN, D], F32, kind="ExternalOutput").ap()
    x1scr = nc.dram_tensor("x1scr", [NOWN, D], F32, kind="Internal").ap()
    wg8d = nc.dram_tensor("wg8d", [128, 8 * 4096], FP8, kind="Internal").ap()
    wu8d = nc.dram_tensor("wu8d", [128, 8 * 4096], FP8, kind="Internal").ap()
    wd8d = nc.dram_tensor("wd8d", [128, 32 * 1024], FP8, kind="Internal").ap()

    xo_t = x_own.rearrange("(t p) d -> t p d", p=128)
    xk_t = x_kv.rearrange("(t p) d -> t p d", p=128)
    y_t = y.rearrange("(t p) d -> t p d", p=128)
    x1_t = x1scr.rearrange("(t p) d -> t p d", p=128)
    wgT_c = wgT.rearrange("(c p) n -> p c n", p=128)
    wuT_c = wuT.rearrange("(c p) n -> p c n", p=128)
    wdT_c = wdT.rearrange("(c p) n -> p c n", p=128)
    WT_c = {"q": WqT.rearrange("(c p) n -> p c n", p=128),
            "k": WkT.rearrange("(c p) n -> p c n", p=128),
            "v": WvT.rearrange("(c p) n -> p c n", p=128),
            "o": WoT.rearrange("(c p) n -> p c n", p=128)}

    class Bld:
        def __init__(self):
            self.plan = {"sync": [], "tensor": [], "vector": [], "scalar": [], "gpsimd": []}
            self.cnt = {"PE": 0, "DVE": 0, "ACT": 0, "POOL": 0}
            for i in range(8):
                self.cnt[f"QS{i}"] = 0
            for i in range(2):
                self.cnt[f"QG{i}"] = 0
            self.rr = {"QS": 0, "QG": 0}
            self.semof = {"tensor": "PE", "vector": "DVE", "scalar": "ACT", "gpsimd": "POOL"}
            self.waited = {}
            self.lastw = {}
            self.readers = {}
            self.floor = {}
            self.sems = None
            self.wsince = {}
            self.lasttok = {}

        def _wait(self, eng, sem, val):
            if val <= 0:
                return
            if self.waited.get((eng, sem), 0) >= val:
                return
            self.waited[(eng, sem)] = val
            self.wsince[eng] = True
            def f(e, sem=sem, val=val):
                e.wait_ge(self.sems[sem], val)
            self.plan[eng].append(f)

        def barrier(self):
            for s, v in self.cnt.items():
                self.floor[s] = v

        def dep(self, eng, reads=(), war=()):
            need = dict(self.floor)
            for k in reads:
                t = self.lastw.get(k)
                if t is not None:
                    assert t != "PENDING", f"read of in-flight group buffer {k}"
                    need[t[0]] = max(need.get(t[0], 0), t[1])
            for k in war:
                t = self.lastw.get(k)
                if t is not None and t != "PENDING":
                    need[t[0]] = max(need.get(t[0], 0), t[1])
                for (s, v) in self.readers.get(k, ()):
                    need[s] = max(need.get(s, 0), v)
            for s, v in need.items():
                self._wait(eng, s, v)

        def op(self, eng, fn, reads=(), writes=(), token=True, q=None):
            self.dep(eng, reads, war=writes)
            if q:
                nl = 8 if q == "QS" else 2
                lane = self.rr[q]
                self.rr[q] = (lane + 1) % nl
                sem = f"{q}{lane}"
                self._wait(eng, sem, self.cnt[sem])
            else:
                sem = self.semof[eng]
            if token:
                inc = 16 if q else 1
                if q is None and eng in ("tensor", "scalar"):
                    if not self.wsince.get(eng, True):
                        pv = self.lasttok[eng]
                        if self.waited.get((eng, sem), 0) < pv:
                            self.waited[(eng, sem)] = pv
                            def fw(e, sem=sem, pv=pv):
                                e.wait_ge(self.sems[sem], pv)
                            self.plan[eng].append(fw)
                    self.cnt[sem] += inc
                    tok = (sem, self.cnt[sem])
                    self.wsince[eng] = False
                    self.lasttok[eng] = self.cnt[sem]
                    def f(e, fn=fn, sem=sem, inc=inc):
                        fn(e).then_inc(self.sems[sem], inc)
                elif q is None:
                    self.cnt[sem] += inc
                    tok = (sem, self.cnt[sem])
                    val = self.cnt[sem]
                    def f(e, fn=fn, sem=sem, inc=inc, val=val):
                        fn(e).then_inc(self.sems[sem], inc)
                        e.wait_ge(self.sems[sem], val)
                    self.waited[(eng, sem)] = val
                    self.wsince[eng] = True
                else:
                    self.cnt[sem] += inc
                    tok = (sem, self.cnt[sem])
                    def f(e, fn=fn, sem=sem, inc=inc):
                        fn(e).then_inc(self.sems[sem], inc)
            else:
                tok = (sem, self.cnt[sem] + 1)
                def f(e, fn=fn):
                    fn(e)
            self.plan[eng].append(f)
            for k in reads:
                self.readers.setdefault(k, []).append(tok)
            for k in writes:
                self.lastw[k] = tok if token else "PENDING"
                self.readers[k] = []
            return tok

        def settok(self, keys, tok):
            for k in keys:
                self.lastw[k] = tok

    bld = Bld()
    alive = []

    def sb(name, shape, dtype):
        cm = nc.sbuf_tensor(name, shape, dtype)
        h = cm.__enter__()
        alive.append((cm, h))
        return h

    def sb_free_back_to(h):
        idx = next(i for i, (cm, hh) in enumerate(alive) if hh is h)
        while len(alive) > idx:
            cm, hh = alive.pop()
            cm.__exit__(None, None, None)

    ident16 = sb("ident16", [128, 128], BF16)
    ident32 = sb("ident32", [128, 128], F32)
    kidx = sb("kidx", [128, max(NKB)], F32)
    bias_tbl = sb("bias_tbl", [128, TOTKB], F32)
    onescol = sb("onescol", [128, 1], F32)
    onesrow = sb("onesrow", [1, 128], F32)
    onesrow16 = sb("onesrow16", [1, 128], BF16)
    fcell = sb("fcell", [128, 1], F32)
    epscol = sb("epscol", [128, 1], F32)
    mcol = sb("mcol", [128, 1], F32)
    negm = sb("negm", [128, 1], F32)
    m2col = sb("m2col", [128, 1], F32)
    soft_lo = sb("soft_lo", [128, 1], F32)
    soft_hi = sb("soft_hi", [128, 1], F32)
    lneps = sb("lneps", [128, 1], F32)
    bq_sb = sb("bq_sb", [128, 8], F32)
    bk_sb = sb("bk_sb", [128, 8], F32)
    bv_sb = sb("bv_sb", [128, 8], F32)
    bo16 = sb("bo16", [1, D], BF16)
    scal = sb("scal", [128, 20], F32)
    LN1C = sb("LN1C", [128, 2 * (NKVT + NOT)], F32)
    LN2C = sb("LN2C", [128, 2 * NOT], F32)
    slope_c = sb("slope_c", [128, 1], F32)
    SC_SG, SC_SWG, SC_SU, SC_SWU, SC_SD, SC_SWD = 0, 1, 2, 3, 4, 5
    SC_T0, SC_T1 = 6, 7

    psum_cm = []
    def psb(name, shape, dtype):
        cm = nc.psum_tensor(name, shape, dtype)
        h = cm.__enter__()
        psum_cm.append(cm)
        return h
    psW0 = psb("psW0", [128, 1024], F32)
    psW1 = psb("psW1", [128, 1024], F32)
    ps4 = psb("ps4", [128, 512], F32)
    ps5 = psb("ps5", [128, 512], F32)
    psE = psb("psE", [128, 1024], F32)
    psEa = psE[:, 0:512]
    psEb = psE[:, 512:1024]

    bld.op("gpsimd", lambda e: e.memset(ident16[:], 0.0), writes=("ident16",))
    bld.op("gpsimd", lambda e: e.affine_select(out=ident16[:], in_=ident16[:],
           compare_op=ALU.not_equal, fill=1.0, base=0, pattern=[[-1, 128]],
           channel_multiplier=1), reads=("ident16",), writes=("ident16",))
    bld.op("gpsimd", lambda e: e.nop(), reads=("ident16",))
    bld.op("gpsimd", lambda e: e.iota(kidx[:], [[128, max(NKB)]], channel_multiplier=1,
           allow_small_or_imprecise_dtypes=True), writes=("kidx",))
    bld.op("vector", lambda e: e.memset(onescol[:], 1.0), writes=("onescol",))
    bld.op("vector", lambda e: e.memset(lneps[:], 1e-5), writes=("lneps",))
    bld.op("vector", lambda e: e.memset(onesrow[:], 1.0), writes=("onesrow",))
    bld.op("vector", lambda e: e.memset(onesrow16[:], 1.0), writes=("onesrow16",))
    bld.op("vector", lambda e: e.tensor_scalar(fcell[:], onescol[:], TINY, 1.0,
           ALU.add, ALU.subtract), reads=("onescol",), writes=("fcell",))
    bld.op("vector", lambda e: e.tensor_scalar(epscol[:], fcell[:], -1.0, TINY,
           ALU.mult, ALU.add), reads=("fcell",), writes=("epscol",))
    bld.op("vector", lambda e: e.tensor_scalar(mcol[:], fcell[:], MSCALE, MAGIC,
           ALU.mult, ALU.add), reads=("fcell",), writes=("mcol",))
    bld.op("vector", lambda e: e.tensor_scalar(negm[:], mcol[:], -1.0, None,
           ALU.mult), reads=("mcol",), writes=("negm",))
    bld.op("vector", lambda e: e.tensor_scalar(m2col[:], mcol[:], 2.0, None,
           ALU.mult), reads=("mcol",), writes=("m2col",))
    _esc = float(EPS_SOFT / TINY)
    bld.op("vector", lambda e: e.tensor_scalar(soft_lo[:], epscol[:], -_esc, 1.0,
           ALU.mult, ALU.add), reads=("epscol",), writes=("soft_lo",))
    bld.op("vector", lambda e: e.tensor_scalar(soft_hi[:], epscol[:], _esc, 1.0,
           ALU.mult, ALU.add), reads=("epscol",), writes=("soft_hi",))
    bld.op("vector", lambda e: e.tensor_copy(ident32[:], ident16[:]),
           reads=("ident16",), writes=("ident32",))
    bld.op("sync", lambda e: e.dma_start(out=bq_sb[:], in_=bq_r), writes=("bq_sb",), q="QS")
    bld.op("sync", lambda e: e.dma_start(out=bk_sb[:], in_=bk_r), writes=("bk_sb",), q="QS")
    bld.op("sync", lambda e: e.dma_start(out=bv_sb[:], in_=bv_r), writes=("bv_sb",), q="QS")


    for h in range(H):
        c32, corr = _dual(_EXP[h] * LN2_64)
        bld.op("vector", lambda e, c32=c32, corr=corr: e.tensor_scalar(
            scal[0:1, SC_T0:SC_T0 + 1], fcell[0:1, :], corr, c32, ALU.mult, ALU.add),
            reads=("fcell",), writes=("sc_t0",))
        bld.op("scalar", lambda e: e.activation(scal[0:1, SC_T1:SC_T1 + 1],
            scal[0:1, SC_T0:SC_T0 + 1], AF.Exp), reads=("sc_t0",), writes=("sc_t1",))
        bld.op("vector", lambda e: e.tensor_scalar(scal[0:1, SC_T0:SC_T0 + 1],
            scal[0:1, SC_T1:SC_T1 + 1], -1.0, None, ALU.mult),
            reads=("sc_t1",), writes=("sc_t0",))
        bld.op("tensor", lambda e: e.matmul(psEa[:, 0:1], onesrow[:],
            scal[0:1, SC_T0:SC_T0 + 1], start=True, stop=True),
            reads=("onesrow", "sc_t0"), writes=("psEa",))
        bld.op("vector", lambda e: e.tensor_copy(slope_c[:], psEa[:, 0:1]),
            reads=("psEa",), writes=("slope_c",))
        nkb, b0 = NKB[h], int(BOFF[h])
        bld.op("vector", lambda e, nkb=nkb, b0=b0: e.tensor_scalar(
            bias_tbl[:, b0:b0 + nkb], kidx[:, 0:nkb], slope_c[:], None, ALU.mult),
            reads=("kidx", "slope_c"), writes=(f"bias:{h}",))

    KPH = int(os.environ.get("KPH", "4"))

    class _Stop(Exception):
        pass
    try:
        if KPH < 1:
            raise _Stop()
        XLT = sb("XLT", [128, 8 * KVT], BF16)
        bo_f = sb("bo_f", [1, D], F32)
        bld.op("sync", lambda e: e.dma_start(out=bo_f[:], in_=bo_r), writes=("bo_f",), q="QS")
        bld.op("vector", lambda e: e.tensor_copy(bo16[:], bo_f[:]), reads=("bo_f",), writes=("bo16",))
        xlt_kv = XLT[:].rearrange("p (c n) -> p c n", c=8)
        KT = sb("KT", [128, KCOLS], BF16)
        VT = sb("VT", [128, VCOLS], BF16)
        Wk16 = sb("Wk16", [128, 8 * D], BF16)
        Wv16 = sb("Wv16", [128, 8 * D], BF16)
        stg = [sb(f"stg{i}", [128, 2048], F32) for i in range(2)]
        xin = [sb(f"xin{i}", [128, D], F32) for i in range(4)]
        sq = sb("sq", [128, D], F32)
        cols = sb("cols", [128, 12], F32)
        mean_acc = sb("mean_acc", [128, 64], F32)

        def ln_stats(src_key, src_ap, slot, colarr, ckey):
            p6 = (slot % 2) * 6
            cs = lambda i: cols[:, p6 + i:p6 + i + 1]
            kc = lambda i: f"c{slot % 2}_{i}"
            bld.op("vector", lambda e, cs=cs: e.reduce_sum(cs(0), src_ap, axis=AX.X),
                   reads=(src_key,), writes=(kc(0),))
            bld.op("scalar", lambda e, cs=cs: e.activation(sq[:], src_ap, AF.Square,
                   accum_out=cs(1)), reads=(src_key,), writes=("sq", kc(1)))
            bld.op("vector", lambda e, cs=cs: e.tensor_scalar(cs(2), cs(0),
                   float(1.0 / D), None, ALU.mult), reads=(kc(0),), writes=(kc(2),))
            bld.op("vector", lambda e, cs=cs: e.tensor_mul(cs(3), cs(2), cs(2)),
                   reads=(kc(2),), writes=(kc(3),))
            bld.op("vector", lambda e, cs=cs: e.scalar_tensor_tensor(cs(4), cs(1),
                   float(1.0 / D), cs(3), ALU.mult, ALU.subtract),
                   reads=(kc(1), kc(3)), writes=(kc(4),))
            bld.op("scalar", lambda e, cs=cs: e.activation(cs(5), cs(4), AF.Ln,
                   bias=lneps[:]), reads=(kc(4), "lneps"), writes=(kc(5),))
            bld.op("scalar", lambda e, cs=cs, colarr=colarr, slot=slot: e.activation(
                   colarr[:, 2 * slot:2 * slot + 1], cs(5), AF.Exp, scale=-0.5),
                   reads=(kc(5),), writes=(ckey + ":r",))
            bld.op("vector", lambda e, cs=cs, colarr=colarr, slot=slot:
                   e.scalar_tensor_tensor(colarr[:, 2 * slot + 1:2 * slot + 2], cs(2),
                   -1.0, colarr[:, 2 * slot:2 * slot + 1], ALU.mult, ALU.mult),
                   reads=(kc(2), ckey + ":r"), writes=(ckey + ":m",))

        NT_ALL = NKVT + NOT
        astat = sb("astat", [128, 4 * NT_ALL], F32)

        def ln_sum2(src_key, src_ap, slot):
            bld.op("vector", lambda e, src_ap=src_ap, slot=slot: e.reduce_sum(
                   astat[:, slot:slot + 1], src_ap, axis=AX.X),
                   reads=(src_key,), writes=(f"as:{slot}",))
            bld.op("scalar", lambda e, src_ap=src_ap, slot=slot: e.activation(
                   sq[:], src_ap, AF.Square,
                   accum_out=astat[:, NT_ALL + slot:NT_ALL + slot + 1]),
                   reads=(src_key,), writes=("sq", f"aq:{slot}"))

        def ln_batch(b, n):
            A2, A3c = 2 * NT_ALL, 3 * NT_ALL
            rs = tuple(f"as:{s}" for s in range(b, b + n))
            rq = tuple(f"aq:{s}" for s in range(b, b + n))
            bld.op("vector", lambda e, b=b, n=n: e.tensor_scalar(
                   astat[:, A2 + b:A2 + b + n], astat[:, b:b + n],
                   float(1.0 / D), None, ALU.mult),
                   reads=rs, writes=(f"am:{b}",))
            bld.op("vector", lambda e, b=b, n=n: e.tensor_mul(
                   astat[:, A3c + b:A3c + b + n], astat[:, A2 + b:A2 + b + n],
                   astat[:, A2 + b:A2 + b + n]),
                   reads=(f"am:{b}",), writes=(f"am2:{b}",))
            bld.op("vector", lambda e, b=b, n=n: e.scalar_tensor_tensor(
                   astat[:, b:b + n], astat[:, NT_ALL + b:NT_ALL + b + n],
                   float(1.0 / D), astat[:, A3c + b:A3c + b + n],
                   ALU.mult, ALU.subtract),
                   reads=rq + (f"am2:{b}",), writes=(f"av:{b}",))
            bld.op("scalar", lambda e, b=b, n=n: e.activation(
                   astat[:, NT_ALL + b:NT_ALL + b + n], astat[:, b:b + n],
                   AF.Ln, bias=lneps[:]),
                   reads=(f"av:{b}", "lneps"), writes=(f"al:{b}",))
            bld.op("scalar", lambda e, b=b, n=n: e.activation(
                   LN1C[:, b:b + n], astat[:, NT_ALL + b:NT_ALL + b + n],
                   AF.Exp, scale=-0.5),
                   reads=(f"al:{b}",), writes=(f"ar:{b}",))
            bld.op("vector", lambda e, b=b, n=n: e.scalar_tensor_tensor(
                   LN1C[:, NT_ALL + b:NT_ALL + b + n], astat[:, A2 + b:A2 + b + n],
                   -1.0, LN1C[:, b:b + n], ALU.mult, ALU.mult),
                   reads=(f"am:{b}", f"ar:{b}"), writes=(f"an:{b}",))

        def ln_apply_t(src_key, src_ap, slot, bkey, dst3, dstkey, tslot):
            lb = xin[(tslot + 2) % 4]
            lbk = f"xin{(tslot + 2) % 4}"
            bld.op("vector", lambda e, lb=lb, slot=slot: e.tensor_scalar(
                   lb[:], src_ap, LN1C[:, slot:slot + 1],
                   LN1C[:, NT_ALL + slot:NT_ALL + slot + 1], ALU.mult, ALU.add),
                   reads=(src_key, f"ar:{bkey}", f"an:{bkey}"), writes=(lbk,))
            for j in range(8):
                tok = bld.op("tensor", lambda e, j=j, lb=lb: e.transpose(
                    out=psE[:, j * 128:(j + 1) * 128], in_=lb[:, j * 128:(j + 1) * 128],
                    identity=ident32[:]),
                    reads=(lbk, "ident32") + (("psEa", "psEb") if j == 0 else ()),
                    writes=("psEa", "psEb") if j == 0 else (), token=(j == 7))
            bld.settok(["psEa", "psEb"], tok)
            bld.op("scalar", lambda e, dst3=dst3, tslot=tslot: e.activation(
                   dst3[:, :, tslot * 128:(tslot + 1) * 128],
                   psE[:].rearrange("p (c n) -> p c n", c=8), AF.Identity),
                   reads=("psEa", "psEb"), writes=(dstkey,))

        mean_jobs = []
        for ci in range(16):
            c, hcol = ci // 2, ci % 2
            mean_jobs.append((wgT_c[:, c, hcol * 2048:(hcol + 1) * 2048], ci))
        for ci in range(16):
            c, hcol = ci // 2, ci % 2
            mean_jobs.append((wuT_c[:, c, hcol * 2048:(hcol + 1) * 2048], 16 + ci))
        for ci in range(32):
            mean_jobs.append((wdT_c[:, ci, :], 32 + ci))
        mj_i = [0]

        ml_i = [0]

        def emit_mean_load():
            if ml_i[0] >= len(mean_jobs):
                return
            src, _ = mean_jobs[ml_i[0]]
            sgi = ml_i[0] % 2
            ml_i[0] += 1
            n = src.shape[-1]
            sk = f"stg{sgi}"
            h2 = n // 2
            bld.op("sync", lambda e, src=src, sgi=sgi, h2=h2: e.dma_start(
                   out=stg[sgi][:, 0:h2], in_=src[:, 0:h2]),
                   writes=(sk,), q="QS")
            bld.op("sync", lambda e, src=src, sgi=sgi, h2=h2, n=n: e.dma_start(
                   out=stg[sgi][:, h2:n], in_=src[:, h2:n]),
                   writes=(sk + "h",), q="QS")

        def emit_mean_step():
            if mj_i[0] >= len(mean_jobs):
                return
            while ml_i[0] <= min(mj_i[0] + 1, len(mean_jobs) - 1):
                emit_mean_load()
            src, acol = mean_jobs[mj_i[0]]
            sgi = mj_i[0] % 2
            mj_i[0] += 1
            n = src.shape[-1]
            if n > 1024:
                bld.op("vector", lambda e, sgi=sgi, acol=acol, n=n: e.reduce_sum(
                       mean_acc[:, acol:acol + 1], stg[sgi][:, 0:n], axis=AX.X,
                       apply_absolute_value=True),
                       reads=(f"stg{sgi}", f"stg{sgi}h"), writes=(f"macc:{acol}",))
            else:
                bld.op("scalar", lambda e, sgi=sgi, acol=acol, n=n: e.activation(
                       sq[:, 0:n], stg[sgi][:, 0:n], AF.Abs,
                       accum_out=mean_acc[:, acol:acol + 1]),
                       reads=(f"stg{sgi}", f"stg{sgi}h"), writes=("sq", f"macc:{acol}"))

        def a_dma(t):
            xi = xin[t % 4]
            bld.op("sync", lambda e, t=t, xi=xi: e.dma_start(out=xi[:], in_=xk_t[t]),
                   writes=(f"xin{t % 4}",), q="QS")
        a_dma(0)
        for t in range(NKVT):
            emit_mean_step()
            if t + 1 < NKVT:
                a_dma(t + 1)
            ln_sum2(f"xin{t % 4}", xin[t % 4][:], t)
        ln_batch(0, NKVT)
        a_dma(0)
        for t in range(NKVT):
            emit_mean_step()
            if t + 1 < NKVT:
                a_dma(t + 1)
            ln_apply_t(f"xin{t % 4}", xin[t % 4][:], t, 0,
                       xlt_kv, f"xlt:{t}", t)

        if KPH < 2:
            raise _Stop()
        bld.op("gpsimd", lambda e: e.dma_start(
               out=Wk16[:].rearrange("p (c n) -> p c n", c=8), in_=WT_c["k"]),
               writes=("Wk16",), q="QG")
        bld.op("gpsimd", lambda e: e.dma_start(
               out=Wv16[:].rearrange("p (c n) -> p c n", c=8), in_=WT_c["v"]),
               writes=("Wv16",), q="QG")
        wk3 = Wk16[:].rearrange("p (c n) -> p c n", c=8)
        wv3 = Wv16[:].rearrange("p (c n) -> p c n", c=8)

        def nblocks(n):
            out, o = [], 0
            while o < n:
                w = min(512, n - o)
                out.append((o, w))
                o += w
            return out

        pg = [ps4, ps5]
        pgk = ["ps4", "ps5"]
        pi = [0]
        for c in range(8):
            for (o, w) in nblocks(KEPT_E[c]):
                emit_mean_step()
                bank = pi[0] % 2
                pi[0] += 1
                pk = pgk[bank]
                reads_xl = tuple(f"xlt:{ts}" for ts in range(o // 128, (o + w + 127) // 128))
                for dc in range(8):
                    tok = bld.op("tensor", lambda e, c=c, dc=dc, o=o, w=w, bank=bank:
                        e.matmul(pg[bank][:, 0:w], wk3[:, dc, c * 128:(c + 1) * 128],
                                 xlt_kv[:, dc, o:o + w], start=(dc == 0), stop=(dc == 7)),
                        reads=("Wk16",) + reads_xl + ((pk,) if dc == 0 else ()),
                        writes=(pk,) if dc == 0 else (), token=(dc == 7))
                bld.settok([pk], tok)
                bld.op("scalar", lambda e, c=c, o=o, w=w, bank=bank: e.activation(
                       KT[:, KOFF[c] + o:KOFF[c] + o + w], pg[bank][:, 0:w],
                       AF.Identity, bias=bk_sb[:, c:c + 1]),
                       reads=(pk, "bk_sb"), writes=(f"KT:{c}",))
        for t in range(NKVT):
            nh = NHT[t]
            if nh == 0:
                continue
            vt3 = VT[:, VOFF[t]:VOFF[t] + 65 * nh].rearrange("p (h x) -> p h x", x=65)
            bld.op("gpsimd", lambda e, vt3=vt3: e.memset(vt3[:, :, 64:65], 1.0),
                   writes=(f"vones:{t}",))
            for hb0 in range(0, nh, 8):
                hbn = min(8, nh - hb0)
                emit_mean_step()
                bank = pi[0] % 2
                pi[0] += 1
                pk = pgk[bank]
                for dc in range(8):
                    tok = bld.op("tensor", lambda e, t=t, dc=dc, hb0=hb0, hbn=hbn, bank=bank:
                        e.matmul(pg[bank][:, 0:64 * hbn], xlt_kv[:, dc, t * 128:(t + 1) * 128],
                                 wv3[:, dc, hb0 * 64:(hb0 + hbn) * 64],
                                 start=(dc == 0), stop=(dc == 7)),
                        reads=(f"xlt:{t}", "Wv16") + ((pk,) if dc == 0 else ()),
                        writes=(pk,) if dc == 0 else (), token=(dc == 7))
                bld.settok([pk], tok)
                bld.op("scalar", lambda e, t=t, vt3=vt3, hb0=hb0, hbn=hbn, bank=bank:
                       e.activation(vt3[:, hb0:hb0 + hbn, 0:64], pg[bank][:, 0:64 * hbn],
                                    AF.Identity),
                       reads=(pk,), writes=(f"VT:{t}",))

        def mean_final(col_lo, col_n, sc_s, sc_sw):
            kacc = tuple(f"macc:{a}" for a in range(col_lo, col_lo + col_n))
            bld.op("vector", lambda e: e.reduce_sum(cols[:, 9:10],
                   mean_acc[:, col_lo:col_lo + col_n], axis=AX.X),
                   reads=kacc, writes=("c_w",))
            bld.op("tensor", lambda e: e.matmul(psEa[0:1, 0:1], cols[:, 9:10],
                   onescol[:], start=True, stop=True),
                   reads=("c_w", "onescol"), writes=("psEa",))
            bld.op("vector", lambda e: e.tensor_scalar(cols[0:1, 10:11], psEa[0:1, 0:1],
                   float(2.0 ** -22), 1e-5, ALU.mult, ALU.max),
                   reads=("psEa",), writes=("c_mean1",))
            bld.op("vector", lambda e: e.reciprocal(cols[0:1, 11:12], cols[0:1, 10:11]),
                   reads=("c_mean1",), writes=("c_s1",))
            bld.op("tensor", lambda e: e.matmul(psEa[:, 0:2], onesrow[:],
                   cols[0:1, 10:12], start=True, stop=True),
                   reads=("c_mean1", "c_s1", "onesrow"), writes=("psEa",))
            bld.op("vector", lambda e, sc_sw=sc_sw: e.tensor_copy(
                   scal[:, sc_sw:sc_sw + 1], psEa[:, 0:1]),
                   reads=("psEa",), writes=(f"sc:{sc_sw}",))
            bld.op("vector", lambda e, sc_s=sc_s: e.tensor_copy(
                   scal[:, sc_s:sc_s + 1], psEa[:, 1:2]),
                   reads=("psEa",), writes=(f"sc:{sc_s}",))

        def a3_dma(t):
            xi = xin[t % 4]
            bld.op("sync", lambda e, t=t, xi=xi: e.dma_start(out=xi[:], in_=xo_t[t]),
                   writes=(f"xin{t % 4}",), q="QS")
        a3_dma(0)
        for t in range(NOT):
            if t + 1 < NOT:
                a3_dma(t + 1)
            ln_sum2(f"xin{t % 4}", xin[t % 4][:], NKVT + t)
        ln_batch(NKVT, NOT)
        while mj_i[0] < len(mean_jobs):
            emit_mean_step()
        mean_final(0, 16, SC_SG, SC_SWG)
        mean_final(16, 16, SC_SU, SC_SWU)
        mean_final(32, 32, SC_SD, SC_SWD)
        bld.op("vector", lambda e: e.tensor_scalar(scal[:, SC_SWD:SC_SWD + 1],
               scal[:, SC_SWD:SC_SWD + 1], 0.5, None, ALU.mult),
               reads=(f"sc:{SC_SWD}",), writes=(f"sc:{SC_SWD}",))

        if KPH < 3:
            raise _Stop()
        bld.barrier()
        xlt_own = XLT[:].rearrange("p (c n) -> p c n", c=8)
        a3_dma(0)
        for t in range(NOT):
            if t + 1 < NOT:
                a3_dma(t + 1)
            ln_apply_t(f"xin{t % 4}", xin[t % 4][:], NKVT + t, NKVT,
                       xlt_own, f"xlo:{t}", t)

        bld.op("gpsimd", lambda e: e.dma_start(
               out=Wk16[:].rearrange("p (c n) -> p c n", c=8), in_=WT_c["q"]),
               writes=("Wk16",), q="QG")
        bld.op("gpsimd", lambda e: e.dma_start(
               out=Wv16[:].rearrange("p (c n) -> p c n", c=8), in_=WT_c["o"]),
               writes=("Wv16",), q="QG")
        wq3, wo3 = wk3, wv3

        QTb = sb("QTb", [128, 8 * 1024], BF16)
        OT = sb("OT", [128, 8 * 1024], BF16)
        Pring = [sb(f"P{i}", [128, 1024], BF16) for i in range(3)]
        stg8 = [sb(f"stg8_{i}", [128, 2048], FP8) for i in range(2)]
        x1b = [sb(f"x1b{i}", [128, D], F32) for i in range(2)]
        park3 = [sb(f"pkp{i}", [128, 4, 256], F32) for i in range(2)]
        dstack = sb("dstack", [128, 1536], F32)
        den_pair = [dstack[0:4, 0:512], dstack[32:36, 0:512]]
        rec_pair = [dstack[0:4, 512:1024], dstack[0:4, 1024:1536]]
        denrow = [dstack[64:65, 0:512], dstack[96:97, 0:512]]
        selT = sb("selT", [128, 8], F32)
        sel_h = [sb(f"sel{i}", [4, 128], F32) for i in range(2)]
        bld.op("gpsimd", lambda e: e.memset(selT[:], 0.0), writes=("selT",))
        for half in range(2):
            bld.op("gpsimd", lambda e, half=half: e.memset(
                   selT[0:64, 4 * half + half:4 * half + half + 1], 1.0),
                   reads=("selT",), writes=("selT",))
            bld.op("gpsimd", lambda e, half=half: e.memset(
                   selT[64:128, 4 * half + 2 + half:4 * half + 3 + half], 1.0),
                   reads=("selT",), writes=("selT",))
        for half in range(2):
            bld.op("tensor", lambda e, half=half: e.transpose(
                   out=psEa[0:4, 0:128], in_=selT[:, 4 * half:4 * half + 4],
                   identity=ident32[:]), reads=("selT", "ident32"), writes=("psEa",))
            bld.op("vector", lambda e, half=half: e.tensor_copy(
                   sel_h[half][:], psEa[0:4, 0:128]),
                   reads=("psEa",), writes=(f"sel{half}",))

        tern_jobs = []
        for ci in range(16):
            c, hcol = ci // 2, ci % 2
            tern_jobs.append(("g", wgT_c[:, c, hcol * 2048:(hcol + 1) * 2048],
                              wg8d[:, ci * 2048:(ci + 1) * 2048], SC_SG))
        for ci in range(16):
            c, hcol = ci // 2, ci % 2
            tern_jobs.append(("u", wuT_c[:, c, hcol * 2048:(hcol + 1) * 2048],
                              wu8d[:, ci * 2048:(ci + 1) * 2048], SC_SU))
        for ci in range(32):
            tern_jobs.append(("d", wdT_c[:, ci, :], wd8d[:, ci * 1024:(ci + 1) * 1024], SC_SD))
        tj_i = [0]
        tl_i = [0]

        def emit_tern_load():
            if tl_i[0] >= len(tern_jobs):
                return
            _, src, _, _ = tern_jobs[tl_i[0]]
            sgi = tl_i[0] % 2
            tl_i[0] += 1
            n = src.shape[-1]
            sk = f"stg{sgi}"
            h2 = n // 2
            bld.op("sync", lambda e, src=src, sgi=sgi, h2=h2: e.dma_start(
                   out=stg[sgi][:, 0:h2], in_=src[:, 0:h2]),
                   writes=(sk,), q="QS")
            bld.op("sync", lambda e, src=src, sgi=sgi, h2=h2, n=n: e.dma_start(
                   out=stg[sgi][:, h2:n], in_=src[:, h2:n]),
                   writes=(sk + "h",), q="QS")

        def emit_tern_step():
            if tj_i[0] >= len(tern_jobs):
                return
            while tl_i[0] <= min(tj_i[0] + 1, len(tern_jobs) - 1):
                emit_tern_load()
            tag, src, dst, sc_s = tern_jobs[tj_i[0]]
            sgi = tj_i[0] % 2
            tj_i[0] += 1
            n = src.shape[-1]
            sk, s8k = f"stg{sgi}", f"stg8_{sgi}"
            s_ap = scal[:, sc_s:sc_s + 1]
            if tag != "d":
                bld.op("vector", lambda e, sgi=sgi, n=n, s_ap=s_ap: e.tensor_scalar(
                       stg[sgi][:, 0:n], stg[sgi][:, 0:n], s_ap, 1.49, ALU.mult, ALU.min),
                       reads=(sk, sk + "h", f"sc:{sc_s}"), writes=(sk,))
                bld.op("vector", lambda e, sgi=sgi, n=n: e.tensor_scalar(
                       stg[sgi][:, 0:n], stg[sgi][:, 0:n], -1.49, mcol[:], ALU.max, ALU.add),
                       reads=(sk, sk + "h", "mcol"), writes=(sk,))
                bld.op("scalar", lambda e, sgi=sgi, n=n: e.activation(
                       stg8[sgi][:, 0:n], stg[sgi][:, 0:n], AF.Identity, bias=negm[:]),
                       reads=(sk, sk + "h", "negm"), writes=(s8k,))
            else:
                bld.op("vector", lambda e, sgi=sgi, n=n, s_ap=s_ap: e.tensor_scalar(
                       stg[sgi][:, 0:n], stg[sgi][:, 0:n], s_ap, 1.49, ALU.mult, ALU.min),
                       reads=(sk, sk + "h", f"sc:{sc_s}"), writes=(sk,))
                bld.op("vector", lambda e, sgi=sgi, n=n: e.tensor_scalar(
                       stg[sgi][:, n:2 * n], stg[sgi][:, 0:n], -1.49, soft_lo[:],
                       ALU.max, ALU.mult), reads=(sk, sk + "h", "soft_lo"), writes=(sk,))
                bld.op("vector", lambda e, sgi=sgi, n=n: e.tensor_scalar(
                       stg[sgi][:, 0:n], stg[sgi][:, 0:n], -1.49, soft_hi[:],
                       ALU.max, ALU.mult), reads=(sk, sk + "h", "soft_hi"), writes=(sk,))
                bld.op("vector", lambda e, sgi=sgi, n=n: e.tensor_scalar(
                       stg[sgi][:, n:2 * n], stg[sgi][:, n:2 * n], mcol[:], None,
                       ALU.add), reads=(sk, sk + "h", "mcol"), writes=(sk,))
                bld.op("vector", lambda e, sgi=sgi, n=n: e.tensor_scalar(
                       stg[sgi][:, 0:n], stg[sgi][:, 0:n], mcol[:], None,
                       ALU.add), reads=(sk, sk + "h", "mcol"), writes=(sk,))
                bld.op("vector", lambda e, sgi=sgi, n=n: e.scalar_tensor_tensor(
                       stg8[sgi][:, 0:n], stg[sgi][:, n:2 * n], m2col[:],
                       stg[sgi][:, 0:n], ALU.subtract, ALU.add),
                       reads=(sk, sk + "h", "m2col"), writes=(s8k,))
            bld.op("sync", lambda e, dst=dst, sgi=sgi, n=n: e.dma_start(
                   out=dst, in_=stg8[sgi][:, 0:n]), reads=(s8k,),
                   writes=(f"w8d:{tag}",), q="QS")

        def o_ap(h, half):
            if h % 2 == 0:
                return (pg[half], pgk[half], pg[half][:], 0)
            return (psW1, "psW1a" if half == 0 else "psW1b",
                    psW1[:, half * 512:(half + 1) * 512], half * 512)

        def emit_evac(qp, h):
            c, off = h // 2, 64 * (h % 2)
            pr = c % 2
            pp, ppk = park3[pr], f"pp{pr}"
            for half in range(2):
                _, opk, oview, _ = o_ap(h, half)
                bld.op("vector", lambda e, oview=oview, pp=pp, off=off, half=half:
                       e.tensor_copy(pp[off:off + 64, half * 2:half * 2 + 2, :],
                                     oview[0:64, :].rearrange("p (a n) -> p a n", a=2)),
                       reads=(opk,), writes=(ppk,))
                dr = half
                bld.op("vector", lambda e, oview=oview, dr=dr: e.tensor_copy(
                       denrow[dr], oview[64:65, :]),
                       reads=(opk,), writes=(f"dnr{dr}",))
                r = 2 * (h % 2) + half
                bld.op("sync", lambda e, dr=dr, pr=pr, r=r: e.dma_start(
                       out=den_pair[pr][r:r + 1, :], in_=denrow[dr]),
                       reads=(f"dnr{dr}",), writes=(f"dp{pr}:{r}",), q="QS")

        def emit_recip(c):
            pr = c % 2
            bld.op("vector", lambda e, pr=pr: e.reciprocal(rec_pair[pr],
                   den_pair[pr]),
                   reads=tuple(f"dp{pr}:{r}" for r in range(4)), writes=(f"rp{pr}",))

        def emit_div(c, half, hnow):
            pr = c % 2
            pp, ppk = park3[pr], f"pp{pr}"
            if hnow % 2 == 0:
                rbv, rbk = psW1[:, 0:512], "psW1a"
            else:
                rbv, rbk = ps4[:], "ps4"
            bld.op("tensor", lambda e, pr=pr, half=half, rbv=rbv: e.matmul(
                   rbv, sel_h[half][:], rec_pair[pr], start=True, stop=True),
                   reads=(f"sel{half}", f"rp{pr}", rbk), writes=(rbk,))
            rb3 = rbv.rearrange("p (a n) -> p a n", a=2)
            bld.op("vector", lambda e, pp=pp, half=half, rb3=rb3: e.tensor_mul(
                   pp[:, half * 2:half * 2 + 2, :],
                   pp[:, half * 2:half * 2 + 2, :], rb3),
                   reads=(ppk, rbk), writes=(ppk,))
            bld.op("vector", lambda e, pp=pp, c=c, half=half: e.tensor_scalar(
                   OT[:, c * 1024 + half * 512:c * 1024 + (half + 1) * 512]
                       .rearrange("p (a n) -> p a n", a=2),
                   pp[:, half * 2:half * 2 + 2, :], bv_sb[:, c:c + 1], None, ALU.add),
                   reads=(ppk, "bv_sb"), writes=(f"OT:{c}",))

        for qp in range(2):
            for c in range(8):
                pw = psW0 if c % 2 == 0 else psW1
                pwk = ("psW0a", "psW0b") if c % 2 == 0 else ("psW1a", "psW1b")
                for half in range(2):
                    for dc in range(8):
                        tok = bld.op("tensor", lambda e, c=c, dc=dc, qp=qp, half=half, pw=pw:
                            e.matmul(pw[:, half * 512:(half + 1) * 512],
                                     wq3[:, dc, c * 128:(c + 1) * 128],
                                     xlt_own[:, dc, qp * 1024 + half * 512:qp * 1024 + (half + 1) * 512],
                                     start=(dc == 0), stop=(dc == 7)),
                            reads=("Wk16",) + tuple(f"xlo:{ts}" for ts in
                                   range(qp * 8 + half * 4, qp * 8 + half * 4 + 4)) +
                                   ((pwk[half],) if dc == 0 else ()),
                            writes=(pwk[half],) if dc == 0 else (), token=(dc == 7))
                    bld.settok([pwk[half]], tok)
                bld.op("scalar", lambda e, c=c, pw=pw: e.activation(
                       QTb[:, c * 1024:(c + 1) * 1024], pw[:], AF.Identity,
                       bias=bq_sb[:, c:c + 1]),
                       reads=(pwk[0], pwk[1], "bq_sb"), writes=(f"QTb:{c}",))
                emit_tern_step()
            for h in range(H):
                emit_tern_step()
                emit_tern_step()
                c, off = h // 2, 64 * (h % 2)
                nkb = NKB[h]

                def emit_S(kb, c=c, off=off, h=h):
                    if kb % 2 == 0:
                        sbv, sk0, sk1 = psW0, "psW0a", "psW0b"
                    else:
                        sbv, sk0, sk1 = psE, "psEa", "psEb"
                    for half in range(2):
                        tok = bld.op("tensor", lambda e, c=c, off=off, kb=kb, half=half, sbv=sbv:
                            e.matmul(sbv[:, half * 512:(half + 1) * 512],
                                     KT[off:off + 64, KOFF[c] + kb * 128:KOFF[c] + (kb + 1) * 128],
                                     QTb[off:off + 64, c * 1024 + half * 512:c * 1024 + (half + 1) * 512],
                                     start=True, stop=True),
                            reads=(f"KT:{c}", f"QTb:{c}", sk0 if half == 0 else sk1),
                            writes=(sk0 if half == 0 else sk1,), token=(half == 1))
                    bld.settok([sk0, sk1], tok)
                    pidx = kb % 3
                    bld.op("scalar", lambda e, kb=kb, h=h, pidx=pidx, sbv=sbv: e.activation(
                           Pring[pidx][:], sbv[:], AF.Exp, scale=0.125,
                           bias=bias_tbl[:, BOFF[h] + kb:BOFF[h] + kb + 1]),
                           reads=(sk0, sk1, f"bias:{h}"), writes=(f"P:{pidx}",))

                def emit_PV(kb, h=h, nkb=nkb):
                    pidx = kb % 3
                    for half in range(2):
                        _, opk, oview, _ = o_ap(h, half)
                        vslice = VT[:, VOFF[kb] + h * 65:VOFF[kb] + (h + 1) * 65]
                        tok = bld.op("tensor", lambda e, vslice=vslice, kb=kb, half=half,
                            oview=oview, pidx=pidx, nkb=nkb:
                            e.matmul(oview[0:65, :], vslice,
                                     Pring[pidx][:, half * 512:(half + 1) * 512],
                                     start=(kb == 0), stop=(kb == nkb - 1)),
                            reads=(f"VT:{kb}", f"vones:{kb}", f"P:{pidx}") +
                                  ((opk,) if kb == 0 else ()),
                            writes=(opk,) if kb == 0 else (), token=(kb == nkb - 1))
                        if kb == nkb - 1:
                            bld.settok([opk], tok)

                emit_S(0)
                if nkb > 1:
                    emit_S(1)
                for kb in range(nkb):
                    if kb + 2 < nkb:
                        emit_S(kb + 2)
                    emit_PV(kb)
                if h >= 1:
                    emit_evac(qp, h - 1)
                if h >= 3:
                    emit_div((h - 3) // 2, (h - 3) % 2, h)
                if h >= 2 and h % 2 == 0:
                    emit_recip(h // 2 - 1)
            emit_evac(qp, H - 1)
            emit_recip(H // 2 - 1)
            emit_div(6, 1, 16)
            emit_div(7, 0, 17)
            emit_div(7, 1, 18)
            for tt in range(8):
                gt = qp * 8 + tt
                xi = xin[gt % 4]
                xik = f"xin{gt % 4}"
                bld.op("sync", lambda e, gt=gt, xi=xi: e.dma_start(out=xi[:], in_=xo_t[gt]),
                       writes=(xik,), q="QS")
                for half in range(2):
                    bank = half
                    pk = pgk[bank]
                    for ac in range(8):
                        bld.op("tensor", lambda e, ac=ac, tt=tt, half=half, bank=bank:
                            e.matmul(pg[bank][:], OT[:, ac * 1024 + tt * 128:ac * 1024 + (tt + 1) * 128],
                                     wo3[:, ac, half * 512:(half + 1) * 512],
                                     start=(ac == 0), stop=False),
                            reads=(f"OT:{ac}", "Wv16") + ((pk,) if ac == 0 else ()),
                            writes=(pk,) if ac == 0 else (), token=False)
                    tok = bld.op("tensor", lambda e, half=half, bank=bank:
                        e.matmul(pg[bank][:], onesrow16[:], bo16[:, half * 512:(half + 1) * 512],
                                 start=False, stop=True),
                        reads=("onesrow16", "bo16"), writes=(), token=True)
                    bld.settok([pk], tok)
                x1k = f"x1b{tt % 2}"
                for half in range(2):
                    bld.op("vector", lambda e, tt=tt, half=half, xi=xi: e.tensor_add(
                           x1b[tt % 2][:, half * 512:(half + 1) * 512],
                           pg[half][:], xi[:, half * 512:(half + 1) * 512]),
                           reads=(pgk[half], xik), writes=(x1k,))
                ln_stats(x1k, x1b[tt % 2][:], gt, LN2C, f"ln2c:{gt}")
                bld.op("sync", lambda e, gt=gt, tt=tt: e.dma_start(
                       out=x1_t[gt], in_=x1b[tt % 2][:]), reads=(x1k,),
                       writes=(f"x1scr:{gt}",), q="QS")
                emit_tern_step()
        while tj_i[0] < len(tern_jobs):
            emit_tern_step()

        if KPH < 4:
            raise _Stop()
        bld.barrier()
        sb_free_back_to(XLT)
        if os.environ.get("KDBG"):
            print("alive after free:", [hh.name if hasattr(hh,'name') else str(hh) for cm,hh in alive])
        wg8 = sb("wg8", [128, 8 * 4096], FP8)
        wu8 = sb("wu8", [128, 8 * 4096], FP8)
        wd8 = sb("wd8", [128, 32 * 1024], FP8)
        gu = sb("gu", [128, 32 * TG], F32)
        hT = sb("hT", [128, 32 * TG], BF16)
        xqT = [sb(f"xqT{i}", [128, 8 * TG], BF16) for i in range(2)]
        x1in = [sb(f"x1in{i}", [128, D], F32) for i in range(6)]
        xhat = sb("xhat", [128, D], F32)
        gsil = [sb(f"gsil{i}", [128, TG], F32) for i in range(2)]
        s2b = sb("s2b", [128, 2 * TG], F32)
        s2row = sb("s2row", [1, 2 * TG], F32)
        Mx = sb("Mx", [128, TG], F32)
        gabs = [sb(f"gabs{i}", [128, TG], F32) for i in range(2)]
        outb = [sb(f"outb{i}", [128, D], F32) for i in range(2)]
        dcols = sb("dcols", [128, 12], F32)

        wg83 = wg8[:].rearrange("p (c n) -> p c n", c=8)
        wu83 = wu8[:].rearrange("p (c n) -> p c n", c=8)
        wd83 = wd8[:].rearrange("p (c n) -> p c n", c=32)

        gbanks = [(ps4[:], "ps4"), (ps5[:], "ps5"), (psEa, "psEa"),
                  (psW0[:, 0:512], "psW0a")]
        dbanks = [(psW0[:, 512:1024], "psW0b"), (psW1[:, 0:512], "psW1a")]

        def emit_lnxq(tg):
            for ti in range(2):
                gt = tg * 2 + ti
                xk = f"x1in{gt % 6}"
                bld.op("sync", lambda e, gt=gt: e.dma_start(out=x1in[gt % 6][:], in_=x1_t[gt]),
                       reads=(f"x1scr:{gt}",), writes=(xk,), q="QS")
                bld.op("vector", lambda e, gt=gt: e.tensor_scalar(xhat[:], x1in[gt % 6][:],
                       LN2C[:, 2 * gt:2 * gt + 1], LN2C[:, 2 * gt + 1:2 * gt + 2],
                       ALU.mult, ALU.add),
                       reads=(xk, f"ln2c:{gt}:r", f"ln2c:{gt}:m"), writes=("xhat",))
                bld.op("vector", lambda e: e.reduce_max(dcols[:, 0:1], xhat[:], axis=AX.X,
                       apply_absolute_value=True), reads=("xhat",), writes=("d_m1",))
                bld.op("vector", lambda e: e.tensor_scalar(dcols[:, 0:1], dcols[:, 0:1],
                       1e-5, None, ALU.max), reads=("d_m1",), writes=("d_m1",))
                bld.op("vector", lambda e: e.reciprocal(dcols[:, 1:2], dcols[:, 0:1]),
                       reads=("d_m1",), writes=("d_rm1",))
                bld.op("vector", lambda e: e.tensor_scalar(dcols[:, 2:3], dcols[:, 1:2],
                       127.0, None, ALU.mult), reads=("d_rm1",), writes=("d_s1",))
                bld.op("vector", lambda e: e.tensor_scalar(dcols[:, 3:4], dcols[:, 0:1],
                       float(1.0 / 127.0), None, ALU.mult), reads=("d_m1",), writes=("d_sv1",))
                bld.op("vector", lambda e: e.tensor_scalar(xhat[:], xhat[:],
                       dcols[:, 2:3], mcol[:], ALU.mult, ALU.add),
                       reads=("xhat", "d_s1", "mcol"), writes=("xhat",))
                bld.op("vector", lambda e: e.tensor_scalar(xhat[:], xhat[:],
                       mcol[:], dcols[:, 3:4], ALU.subtract, ALU.mult),
                       reads=("xhat", "d_sv1", "mcol"), writes=("xhat",))
                xq3 = xqT[tg % 2][:].rearrange("p (c n) -> p c n", c=8)
                for rnd in range(2):
                    for j in range(4):
                        jj = rnd * 4 + j
                        tok = bld.op("tensor", lambda e, j=j, jj=jj: e.transpose(
                            out=psEb[:, j * 128:(j + 1) * 128],
                            in_=xhat[:, jj * 128:(jj + 1) * 128],
                            identity=ident32[:]),
                            reads=("xhat", "ident32") + (("psEb",) if j == 0 else ()),
                            writes=("psEb",) if j == 0 else (), token=(j == 3))
                    bld.settok(["psEb"], tok)
                    bld.op("scalar", lambda e, ti=ti, rnd=rnd, xq3=xq3: e.activation(
                           xq3[:, rnd * 4:(rnd + 1) * 4, ti * 128:(ti + 1) * 128],
                           psEb[:].rearrange("p (c n) -> p c n", c=4), AF.Identity),
                           reads=("psEb",), writes=(f"xqT{tg % 2}",))

        def emit_gateup(tg):
            xq3 = xqT[tg % 2][:].rearrange("p (c n) -> p c n", c=8)
            xqk = f"xqT{tg % 2}"
            for ic in range(32):
                if ic % 2 == 0:
                    drain_hT(2)
                bview, pk = gbanks[ic % 4]
                for dc in range(8):
                    bld.op("tensor", lambda e, ic=ic, dc=dc, bview=bview, xq3=xq3: e.matmul(
                           bview[:, 0:TG], wg83[:, dc, ic * 128:(ic + 1) * 128],
                           xq3[:, dc, :], start=(dc == 0), stop=(dc == 7)),
                           reads=tuple(f"wg8:{i}" for i in range(4)) + (xqk,) + ((pk,) if dc == 0 else ()),
                           writes=(pk,) if dc == 0 else (), token=False)
                for dc in range(8):
                    tok = bld.op("tensor", lambda e, ic=ic, dc=dc, bview=bview, xq3=xq3: e.matmul(
                           bview[:, TG:2 * TG], wu83[:, dc, ic * 128:(ic + 1) * 128],
                           xq3[:, dc, :], start=(dc == 0), stop=(dc == 7)),
                           reads=tuple(f"wu8:{i}" for i in range(4)) + (xqk,), writes=(), token=(dc == 7))
                bld.settok([pk], tok)
                gk = f"gsil{ic % 2}"
                bld.op("scalar", lambda e, ic=ic, bview=bview: e.activation(
                       gsil[ic % 2][:], bview[:, 0:TG], AF.Silu,
                       scale=scal[:, SC_SWG:SC_SWG + 1]),
                       reads=(pk, f"sc:{SC_SWG}"), writes=(gk,))
                bld.op("vector", lambda e, ic=ic, bview=bview: e.scalar_tensor_tensor(
                       gu[:, ic * TG:(ic + 1) * TG], bview[:, TG:2 * TG],
                       scal[:, SC_SWU:SC_SWU + 1], gsil[ic % 2][:], ALU.mult, ALU.mult),
                       reads=(pk, f"sc:{SC_SWU}", gk), writes=(f"gu:{ic}",))
                if ic == 0:
                    bld.op("scalar", lambda e: e.activation(Mx[:], gu[:, 0:TG], AF.Abs),
                           reads=("gu:0",), writes=("Mx",))
                else:
                    gak = f"gabs{ic % 2}"
                    bld.op("scalar", lambda e, ic=ic: e.activation(
                           gabs[ic % 2][:], gu[:, ic * TG:(ic + 1) * TG], AF.Abs),
                           reads=(f"gu:{ic}",), writes=(gak,))
                    bld.op("vector", lambda e, ic=ic: e.tensor_max(Mx[:], Mx[:],
                           gabs[ic % 2][:]), reads=(gak, "Mx"), writes=("Mx",))

        def emit_quant_h(tg):
            for ti in range(2):
                tok = bld.op("tensor", lambda e, ti=ti: e.transpose(
                       out=psW1[:, 512 + ti * 128:512 + (ti + 1) * 128],
                       in_=Mx[:, ti * 128:(ti + 1) * 128], identity=ident32[:]),
                       reads=("Mx", "ident32") + (("psW1b",) if ti == 0 else ()),
                       writes=("psW1b",) if ti == 0 else (), token=True)
            bld.settok(["psW1b"], tok)
            for ti in range(2):
                bld.op("vector", lambda e, ti=ti: e.reduce_max(dcols[:, 4 + ti:5 + ti],
                       psW1[:, 512 + ti * 128:512 + (ti + 1) * 128], axis=AX.X),
                       reads=("psW1b",), writes=(f"d_m2:{ti}",))
                bld.op("vector", lambda e, ti=ti: e.tensor_scalar(dcols[:, 4 + ti:5 + ti],
                       dcols[:, 4 + ti:5 + ti], 1e-5, None, ALU.max),
                       reads=(f"d_m2:{ti}",), writes=(f"d_m2:{ti}",))
                bld.op("vector", lambda e, ti=ti: e.reciprocal(dcols[:, 8 + ti:9 + ti],
                       dcols[:, 4 + ti:5 + ti]), reads=(f"d_m2:{ti}",), writes=(f"d_rm2:{ti}",))
                bld.op("vector", lambda e, ti=ti: e.tensor_scalar(dcols[:, 10 + ti:11 + ti],
                       dcols[:, 8 + ti:9 + ti], 127.0, None, ALU.mult),
                       reads=(f"d_rm2:{ti}",), writes=(f"d_s2:{ti}",))
                bld.op("vector", lambda e, ti=ti: e.scalar_tensor_tensor(
                       dcols[:, 6 + ti:7 + ti], dcols[:, 4 + ti:5 + ti],
                       float(1.0 / 127.0), scal[:, SC_SWD:SC_SWD + 1], ALU.mult, ALU.mult),
                       reads=(f"d_m2:{ti}", f"sc:{SC_SWD}"), writes=(f"dcmb:{ti}",))
            for ti in range(2):
                tok = bld.op("tensor", lambda e, ti=ti: e.transpose(
                       out=psW1[0:1, 512 + 256 + ti * 128:512 + 256 + (ti + 1) * 128],
                       in_=dcols[:, 10 + ti:11 + ti], identity=ident32[:]),
                       reads=(f"d_s2:{ti}", "ident32") + (("psW1b",) if ti == 0 else ()),
                       writes=("psW1b",) if ti == 0 else (), token=True)
            bld.settok(["psW1b"], tok)
            bld.op("vector", lambda e: e.tensor_copy(s2row[0:1, 0:TG], psW1[0:1, 768:768 + TG]),
                   reads=("psW1b",), writes=("s2row",))
            bld.op("vector", lambda e: e.tensor_copy(s2row[0:1, TG:2 * TG], s2row[0:1, 0:TG]),
                   reads=("s2row",), writes=("s2row",))
            bld.op("tensor", lambda e: e.matmul(psW1[:, 512:1024], onesrow[:],
                   s2row[:], start=True, stop=True),
                   reads=("s2row", "onesrow"), writes=("psW1b",))
            bld.op("scalar", lambda e: e.activation(s2b[:], psW1[:, 512:1024],
                   AF.Identity), reads=("psW1b",), writes=("s2b",))

        pending_hT = []

        def emit_quant_hT(tg):
            for ic in range(0, 32, 2):
                def th1(ic=ic):
                    bld.op("vector", lambda e, ic=ic: e.tensor_mul(
                           gu[:, ic * TG:(ic + 2) * TG],
                           gu[:, ic * TG:(ic + 2) * TG], s2b[:]),
                           reads=(f"gu:{ic}", f"gu:{ic + 1}", "s2b"),
                           writes=(f"gu:{ic}", f"gu:{ic + 1}"))
                def th2(ic=ic):
                    bld.op("vector", lambda e, ic=ic: e.tensor_scalar(
                           hT[:, ic * TG:(ic + 2) * TG], gu[:, ic * TG:(ic + 2) * TG],
                           mcol[:], mcol[:], ALU.add, ALU.subtract),
                           reads=(f"gu:{ic}", f"gu:{ic + 1}", "mcol"),
                           writes=(f"hT:{ic}",))
                pending_hT.append(th1)
                pending_hT.append(th2)

        def drain_hT(k=1):
            for _ in range(k):
                if pending_hT:
                    pending_hT.pop(0)()

        def emit_down(tg):
            for ti in range(2):
                gt = tg * 2 + ti
                xk = f"x1in{gt % 6}"
                ob = outb[gt % 2]
                obk = f"outb{gt % 2}"
                for half in range(2):
                    bview, pk = dbanks[half]
                    for ic in range(32):
                        tok = bld.op("tensor", lambda e, ic=ic, ti=ti, half=half, bview=bview:
                            e.matmul(bview[:],
                                     hT[:, ic * TG + ti * 128:ic * TG + (ti + 1) * 128],
                                     wd83[:, ic, half * 512:(half + 1) * 512],
                                     start=(ic == 0), stop=(ic == 31)),
                            reads=(f"hT:{ic - (ic % 2)}",) + tuple(f"wd8:{i}" for i in range(4)) + ((pk,) if ic == 0 else ()),
                            writes=(pk,) if ic == 0 else (), token=(ic == 31))
                    bld.settok([pk], tok)
                    bld.op("vector", lambda e, ti=ti, half=half, bview=bview, ob=ob, gt=gt:
                        e.scalar_tensor_tensor(ob[:, half * 512:(half + 1) * 512],
                            bview[:], dcols[:, 6 + ti:7 + ti],
                            x1in[gt % 6][:, half * 512:(half + 1) * 512],
                            ALU.mult, ALU.add),
                        reads=(pk, f"dcmb:{ti}", xk), writes=(obk,))
                bld.op("sync", lambda e, gt=gt, ob=ob: e.dma_start(out=y_t[gt], in_=ob[:]),
                       reads=(obk,), q="QS")

        emit_lnxq(0)
        emit_lnxq(1)
        for qn, (dst, srcd, tag) in enumerate(
                [(wg8, wg8d, "g"), (wu8, wu8d, "u"), (wd8, wd8d, "d")]):
            W = dst.shape[1]
            for qq in range(4):
                lo, hi = qq * (W // 4), (qq + 1) * (W // 4)
                bld.op("sync", lambda e, dst=dst, srcd=srcd, lo=lo, hi=hi:
                       e.dma_start(out=dst[:, lo:hi], in_=srcd[:, lo:hi]),
                       reads=(f"w8d:{tag}",), writes=(f"w{tag}8:{qq}",), q="QS")
        for tg in range(NTG):
            emit_gateup(tg)
            if tg > 0:
                emit_down(tg - 1)
            emit_quant_h(tg)
            if tg + 2 < NTG:
                emit_lnxq(tg + 2)
            emit_quant_hT(tg)
        drain_hT(32)
        emit_down(NTG - 1)
    except _Stop:
        pass

    for eng in ("tensor", "vector", "scalar", "gpsimd"):
        bld.op(eng, lambda e: e.nop(), token=True)
    for eng in ("tensor", "scalar"):
        sem = bld.semof[eng]
        v = bld.cnt[sem]
        bld.plan[eng].append(lambda e, sem=sem, v=v: e.wait_ge(bld.sems[sem], v))
    for i in range(8):
        v = bld.cnt[f"QS{i}"]
        if v:
            bld.plan["sync"].append(lambda e, i=i, v=v: e.wait_ge(bld.sems[f"QS{i}"], v))
    for i in range(2):
        v = bld.cnt[f"QG{i}"]
        if v:
            bld.plan["sync"].append(lambda e, i=i, v=v: e.wait_ge(bld.sems[f"QG{i}"], v))

    import contextlib
    with contextlib.ExitStack() as stk:
        sems = {}
        for nm in list(bld.cnt.keys()):
            sems[nm] = stk.enter_context(nc.semaphore("s" + nm))
        blk = stk.enter_context(nc.Block())
        bld.sems = sems

        @blk.sync
        def _(e):
            for f in bld.plan["sync"]:
                f(e)

        @blk.tensor
        def _(e):
            for f in bld.plan["tensor"]:
                f(e)

        @blk.vector
        def _(e):
            for f in bld.plan["vector"]:
                f(e)

        @blk.scalar
        def _(e):
            for f in bld.plan["scalar"]:
                f(e)

        @blk.gpsimd
        def _(e):
            for f in bld.plan["gpsimd"]:
                f(e)

    while alive:
        cm, h = alive.pop()
        cm.__exit__(None, None, None)
    for cm in reversed(psum_cm):
        cm.__exit__(None, None, None)
    return nc


_NC = None


def kernel(L0_attn_x, L0_attn_Wq, L0_attn_bq, L0_attn_Wk, L0_attn_bk,
           L0_attn_Wv, L0_attn_bv, L0_attn_Wo, L0_attn_bo,
           L0_ffn_w_gate, L0_ffn_w_up, L0_ffn_w_down):
    global _NC
    if _NC is None:
        _NC = _build()
    nc = _NC
    f32 = np.float32
    x = np.ascontiguousarray(L0_attn_x, dtype=f32)
    WqT = np.ascontiguousarray(np.asarray(L0_attn_Wq, f32).T)
    WkT = np.ascontiguousarray(np.asarray(L0_attn_Wk, f32).T)
    WvT = np.ascontiguousarray(np.asarray(L0_attn_Wv, f32).T)
    WoT = np.ascontiguousarray(np.asarray(L0_attn_Wo, f32).T)
    bq_r = np.ascontiguousarray(np.asarray(L0_attn_bq, f32).reshape(8, 128).T)
    bk_r = np.ascontiguousarray(np.asarray(L0_attn_bk, f32).reshape(8, 128).T)
    bv_r = np.ascontiguousarray(np.asarray(L0_attn_bv, f32).reshape(8, 128).T)
    bo_r = np.ascontiguousarray(np.asarray(L0_attn_bo, f32).reshape(1, D))
    wgT = np.ascontiguousarray(np.asarray(L0_ffn_w_gate, f32).T)
    wuT = np.ascontiguousarray(np.asarray(L0_ffn_w_up, f32).T)
    wdT = np.ascontiguousarray(np.asarray(L0_ffn_w_down, f32).T)

    in_maps = []
    for core in range(NCORE):
        b, par = core // 2, core % 2
        blocks = [par + 2 * i for i in range(NQB)]
        x_own = np.ascontiguousarray(
            np.concatenate([x[b, qb * 512:(qb + 1) * 512] for qb in blocks], axis=0))
        x_kv = np.ascontiguousarray(x[b, :KVT])
        in_maps.append({
            "x_own": x_own, "x_kv": x_kv,
            "WqT": WqT, "WkT": WkT, "WvT": WvT, "WoT": WoT,
            "bq_r": bq_r, "bk_r": bk_r, "bv_r": bv_r, "bo_r": bo_r,
            "wgT": wgT, "wuT": wuT, "wdT": wdT,
        })
    res = run_bass_kernel_spmd(nc, in_maps, core_ids=list(range(NCORE)))
    out = np.empty((B, L, D), dtype=f32)
    for core in range(NCORE):
        b, par = core // 2, core % 2
        yv = res.results[core]["y"]
        for i in range(NQB):
            qb = par + 2 * i
            out[b, qb * 512:(qb + 1) * 512] = yv[i * 512:(i + 1) * 512]
    return out
